# version 05
# speedup vs baseline: 1.4220x; 1.0509x over previous
import numpy as np
import ml_dtypes
import concourse.bass as bass
import concourse.mybir as mybir
from concourse.bass_utils import run_bass_kernel_spmd

F32 = mybir.dt.float32
BF16 = mybir.dt.bfloat16
AF = mybir.ActivationFunctionType
ALU = mybir.AluOpType
AX = mybir.AxisListType.X

T = 2048
D = 1024
NCH = 8
TT = 4
NCORES = 8

ENGINES = ("tensor", "vector", "scalar", "gpsimd", "sync")
DQ = {e: e + "_q" for e in ENGINES}


class Ev:
    __slots__ = ("sem", "val")

    def __init__(self, sem, val):
        self.sem = sem
        self.val = val


class Prog:
    def __init__(self):
        self.ops = {e: [] for e in ENGINES}
        self.cnt = {}
        self.waited = {e: {} for e in ENGINES}

    def op(self, eng, emit, deps=(), sig=True, dma=False, label=""):
        byname = {}

        def add(ev):
            if ev is None:
                return
            if isinstance(ev, (list, tuple)):
                for e2 in ev:
                    add(e2)
            elif ev.val > byname.get(ev.sem, 0):
                byname[ev.sem] = ev.val
        for ev in deps:
            add(ev)
        waits = []
        for s, v in sorted(byname.items()):
            if v > self.waited[eng].get(s, 0):
                waits.append((s, v))
                self.waited[eng][s] = v
        ev = None
        if sig:
            sem = dma if dma else eng
            self.cnt[sem] = self.cnt.get(sem, 0) + (16 if dma else 1)
            ev = Ev(sem, self.cnt[sem])
        self.ops[eng].append((waits, emit, ev, dma, label))
        return ev

    def mark(self, eng):
        return Ev(eng, self.cnt.get(eng, 0))

    def finalize(self, nc, final_deps):
        self.op("sync", None, deps=final_deps, sig=False, label="final_wait")
        from contextlib import ExitStack
        stack = ExitStack()
        sems = {}
        for name in sorted(self.cnt.keys()):
            sems[name] = stack.enter_context(nc.semaphore(name))
        block = stack.enter_context(nc.Block())

        def mk(eng_name):
            def body(eng):
                for waits, emit, ev, dma, label in self.ops[eng_name]:
                    for s, v in waits:
                        eng.wait_ge(sems[s], v)
                    if emit is None:
                        continue
                    inst = emit(eng)
                    if ev is not None:
                        inst.then_inc(sems[ev.sem], 16 if dma else 1)
            return body

        for e in ENGINES:
            getattr(block, e)(mk(e))
        stack.close()
        return nc


SB_BASE = 16512
SB_LIMIT = 229344
PERSIST_SZ = 14336


class K:
    def __init__(self, fused=False):
        self.nc = bass.Bass("TRN2", target_bir_lowering=False)
        self.p = Prog()
        self.sb = {}
        self.ps = {}
        self._free = {}
        self.fused = fused
        self.pfx = ""
        self._persist = set()
        self._sb_ptr = SB_BASE
        self._persist_ptr = SB_LIMIT - PERSIST_SZ
        self._loaded_consts = set()
        self._sem_pool = 0
        self.banks = {}
        self._psum_cms = []
        self._bank_gen = 0
        self._bank_spec = None
        self.set_banks("std")

    BANK_SPECS = {
        "std": (("A0", [128, 512], F32), ("A1", [128, 512], F32),
                ("A2", [128, 512], F32), ("A3", [128, 512], F32),
                ("W0", [65, 1024], F32), ("P3", [128, 1024], BF16),
                ("PD", [1, 512], F32)),
        "moe": tuple((f"A{i}", [128, 512], F32) for i in range(8)),
    }

    def set_banks(self, spec):
        if spec == self._bank_spec:
            return
        for cm in reversed(self._psum_cms):
            cm.__exit__(None, None, None)
        self._psum_cms = []
        self.banks = {}
        self._bank_gen += 1
        for nm, shape, dt in self.BANK_SPECS[spec]:
            cm = self.nc.psum_tensor(f"{nm}_g{self._bank_gen}",
                                     list(shape), dt)
            self.banks[nm] = Bank(nm, cm.__enter__())
            self._psum_cms.append(cm)
        self._bank_spec = spec

    def bk(self, nm):
        return self.banks[nm]

    def use(self, key):
        return self._free.get(key)

    def rel(self, key, ev):
        self._free[key] = ev

    def stream_sem(self):
        self._sem_pool += 1
        return f"ws{self._sem_pool % 10}"

    def dram_in(self, name, shape, dtype=F32):
        return self.nc.dram_tensor(name, list(shape), dtype,
                                   kind="ExternalInput").ap()

    def dram_out(self, name, shape, dtype=F32):
        return self.nc.dram_tensor(name, list(shape), dtype,
                                   kind="ExternalOutput").ap()

    def dram_tmp(self, name, shape, dtype=F32):
        return self.nc.dram_tensor(name, list(shape), dtype,
                                   kind="Internal").ap()

    def sbuf(self, name, shape, dtype, persist=False):
        per_part = int(np.prod(shape[1:])) * (2 if dtype == BF16 else 4)
        per_part = (per_part + 31) & ~31
        if persist:
            off = self._persist_ptr
            self._persist_ptr += per_part
            assert self._persist_ptr <= SB_LIMIT, "persist region overflow"
            self._persist.add(name)
        else:
            off = self._sb_ptr
            self._sb_ptr += per_part
            assert self._sb_ptr <= SB_LIMIT - PERSIST_SZ,                 f"sbuf scratch overflow at {self.pfx}{name}: {self._sb_ptr}"
        h = self.nc.alloc_sbuf_tensor_at("sb_" + self.pfx + name, list(shape),
                                         dtype, offset=off)
        self.sb[name] = h
        return h

    def new_block(self, pfx):
        self.barrier()
        self.set_banks("std")
        self.pfx = pfx
        self._sb_ptr = SB_BASE
        for key in [kk for kk in self.sb if kk not in self._persist]:
            del self.sb[key]
        keep = {kk: v for kk, v in self._free.items()
                if kk in ("consts", "ln_eps_ev")}
        self._free = keep

    def barrier(self):
        p = self.p
        evs = [Ev(sname, v) for sname, v in p.cnt.items()]
        for eng in ENGINES:
            p.op(eng, None, deps=evs, sig=False, label="barrier")

    def psum(self, name, shape, dtype=F32):
        h = self.nc.alloc_psum_tensor(name, list(shape), dtype)
        self.ps[name] = h
        return h

    BIGC = {"c_sdpmask", "c_sel16"}

    def load_consts(self, keys):
        shp = {"c_ones1": (1, 128), "c_onesb": (128, 1), "c_identb": (128, 128),
               "c_identf": (128, 128), "c_triu": (128, 128),
               "c_tril": (128, 128), "c_sdpmask": (128, 2048),
               "c_palace": (128, 64), "c_sel8": (8, 512), "c_sel16": (16, 2048)}
        dt = {"c_onesb": BF16, "c_identb": BF16, "c_sel16": BF16}
        keys = [kk for kk in keys
                if (kk in self.BIGC and kk not in self.sb)
                or (kk not in self.BIGC and kk not in self._loaded_consts)]
        if not keys:
            return self.use("consts")
        casts = []
        for key in keys:
            if key in self.BIGC:
                if not hasattr(self, "_const_dram"):
                    self._const_dram = {}
                if key not in self._const_dram:
                    self._const_dram[key] = self.dram_in(
                        key, shp[key], BF16 if key == "c_sel16" else F32)
                d = self._const_dram[key]
                sb = self.sbuf(key, shp[key],
                               BF16 if key == "c_sel16" else F32)
                self.p.op("sync", lambda e, sb=sb, d=d: e.dma_start(
                    out=sb[:], in_=d), dma="cst", label=f"ld_{key}")
                continue
            self._loaded_consts.add(key)
            d = self.dram_in(key, shp[key], F32)
            sb = self.sbuf(key, shp[key], dt.get(key, F32), persist=True)
            if dt.get(key) is None:
                self.p.op("sync", lambda e, sb=sb, d=d: e.dma_start(
                    out=sb[:], in_=d), dma="cst", label=f"ld_{key}")
            else:
                stg = self.sbuf(key + "_stg", shp[key], F32)
                self.p.op("sync", lambda e, stg=stg, d=d: e.dma_start(
                    out=stg[:], in_=d), dma="cst", label=f"ld_{key}")
                casts.append((sb, stg, key))
        ev_all = Ev("cst", self.p.cnt["cst"])
        evs = [ev_all]
        for sb, stg, key in casts:
            evs.append(self.p.op("vector", lambda e, sb=sb, stg=stg:
                                 e.tensor_copy(sb[:], stg[:]), deps=[ev_all],
                                 label=f"cast_{key}"))
        prev = self.use("consts")
        self.rel("consts", evs + ([prev] if prev else []))
        return evs



def colmajor(W):
    Din, Dout = W.shape
    KC, OC = Din // 128, (Dout + 127) // 128
    if Dout % 128:
        Wp = np.zeros((Din, OC * 128), dtype=W.dtype)
        Wp[:, :Dout] = W
        W = Wp
    return np.ascontiguousarray(
        W.reshape(KC, 128, OC, 128).transpose(2, 1, 0, 3).reshape(OC, 128, KC * 128))


def rowmajor(W):
    Din, Dout = W.shape
    return np.ascontiguousarray(W.reshape(Din // 128, 128, Dout))


def pp_bias(b, OC):
    return np.ascontiguousarray(np.asarray(b).reshape(OC, 128).T)



class WCols:
    def __init__(self, k, name, hbm_ap, plan, KCW, queue="sync",
                 cast_eng="vector", stg_bufs=2, col_bufs=2):
        self.k, self.name, self.ap = k, name, hbm_ap
        self.plan, self.KCW, self.queue = plan, KCW, queue
        self.cast_eng = cast_eng
        self.nstg = stg_bufs
        self.ncol = col_bufs
        self.stg = [k.sbuf(f"{name}_stg{i}", [128, KCW], F32)
                    for i in range(stg_bufs)]
        self.col = [k.sbuf(f"{name}_col{i}", [128, KCW], BF16)
                    for i in range(col_bufs)]
        self.stg_free = [None] * stg_bufs
        self.col_free = [None] * col_bufs
        self.ready = {}
        self.emitted = 0

    def _emit(self, i):
        if i >= len(self.plan):
            return
        p = self.k.p
        s, c, oc = self.stg[i % self.nstg], self.col[i % self.ncol], self.plan[i]
        ev_d = p.op(self.queue, lambda e, s=s, oc=oc: e.dma_start(
            out=s[:], in_=self.ap[oc]), deps=[self.stg_free[i % self.nstg]],
            dma=f"wq_{self.name}{i % self.nstg}", label=f"{self.name}_dma{i}")
        ev_c = p.op(self.cast_eng, lambda e, c=c, s=s: e.tensor_copy(c[:], s[:]),
                    deps=[ev_d, self.col_free[i % self.ncol]],
                    label=f"{self.name}_cast{i}")
        self.stg_free[i % self.nstg] = ev_c
        self.ready[i] = ev_c
        self.emitted = i + 1

    def get(self, i):
        tgt = min(i + self.ncol, len(self.plan))
        while self.emitted < tgt:
            self._emit(self.emitted)
        return self.col[i % self.ncol], self.ready[i]

    def release(self, i, ev):
        self.col_free[i % self.ncol] = ev
        if self.emitted == i + self.ncol:
            self._emit(i + self.ncol)

    def tile(self, col, kc):
        return col[:, kc * 128:(kc + 1) * 128]


class Bank:
    def __init__(self, name, handle):
        self.name, self.h = name, handle



def emit_ln(k, x_tiles, x_ev, xn_tiles, b_sum, b_ss, b_bc, tag, wkey="xn"):
    p = k.p
    if "ln_xb" not in k.sb:
        k.sbuf("ln_xb", [128, 512], BF16)
        k.sbuf("ln_sq", [128, 512], BF16)
        if "g_ln_rows" not in k._persist:
            k.sbuf("g_ln_rows", [1, 4 * 512], F32, persist=True)
            k._persist.add("g_ln_rows")
        k.sb["ln_rows"] = k.sb["g_ln_rows"]
        k.sbuf("ln_mb", [128, 512], BF16)
        k.sbuf("ln_rb", [128, 512], BF16)
        k.sbuf("ln_tmp", [128, 512], F32)
        eps = k.sbuf("ln_eps", [1, 1], F32)
        ev = p.op("vector", lambda e: e.memset(eps[:], 1e-5), label="ln_eps")
        k.rel("ln_eps_ev", ev)
    xb, sq = k.sb["ln_xb"], k.sb["ln_sq"]
    rows, mb, rb, tmp = (k.sb["ln_rows"], k.sb["ln_mb"], k.sb["ln_rb"],
                         k.sb["ln_tmp"])
    onesb, ones1 = k.sb["c_onesb"], k.sb["c_ones1"]
    ps_sum, ps_ss, ps_bc = b_sum.h, b_ss.h, b_bc.h

    ev_m1 = ev_m2 = None
    prev_xb = k.use("ln_xb0")
    prev_sq = k.use("ln_sq0")
    for c in range(NCH):
        ev_cast = p.op("vector", lambda e, c=c: e.tensor_copy(
            xb[:, 0:512], x_tiles[c]),
            deps=[x_ev[c], prev_xb, k.use("consts")], label=f"{tag}_cast{c}")
        ev_sq = p.op("scalar", lambda e, c=c: e.activation(
            sq[:, 0:512], x_tiles[c], AF.Square), deps=[x_ev[c], prev_sq],
            label=f"{tag}_sq{c}")
        ev_m1 = p.op("tensor", lambda e, c=c: e.matmul(
            ps_sum[0:1, 0:512], onesb[:], xb[:, 0:512],
            start=(c == 0), stop=(c == NCH - 1)),
            deps=[ev_cast] + ([k.use(b_sum.name)] if c == 0 else []),
            label=f"{tag}_s1_{c}")
        ev_m2 = p.op("tensor", lambda e, c=c: e.matmul(
            ps_ss[0:1, 0:512], onesb[:], sq[:, 0:512],
            start=(c == 0), stop=(c == NCH - 1)),
            deps=[ev_sq] + ([k.use(b_ss.name)] if c == 0 else []),
            label=f"{tag}_s2_{c}")
        prev_xb = ev_m1
        prev_sq = ev_m2
    k.rel("ln_xb0", prev_xb)
    k.rel("ln_sq0", prev_sq)

    r = lambda i: rows[0:1, 512 * i:512 * i + 512]
    ev_rows0 = k.use("ln_rows")
    ev_m = p.op("vector", lambda e: e.tensor_scalar_mul(
        r(0), ps_sum[0:1, 0:512], 1.0 / D), deps=[ev_m1, ev_rows0],
        label=f"{tag}_m")
    k.rel(b_sum.name, ev_m)
    ev_ss = p.op("vector", lambda e: e.tensor_scalar_mul(
        r(2), ps_ss[0:1, 0:512], 1.0 / D), deps=[ev_m2, ev_rows0],
        label=f"{tag}_ss")
    k.rel(b_ss.name, ev_ss)
    ev_msq = p.op("scalar", lambda e: e.activation(r(1), r(0), AF.Square),
                  deps=[ev_m], label=f"{tag}_msq")
    ev_var = p.op("vector", lambda e: e.tensor_sub(r(2), r(2), r(1)),
                  deps=[ev_msq, ev_ss], label=f"{tag}_var")
    eps_ap = k.sb["ln_eps"][0:1, 0:1]
    ev_l = p.op("scalar", lambda e, eps_ap=eps_ap: e.activation(
        r(1), r(2), AF.Ln, bias=eps_ap),
        deps=[ev_var, k.use("ln_eps_ev")], label=f"{tag}_lnv")
    ev_r = p.op("scalar", lambda e: e.activation(r(3), r(1), AF.Exp,
                scale=-0.5), deps=[ev_l], label=f"{tag}_rstd")
    k.rel("ln_rows", ev_r)

    ev_b1 = p.op("tensor", lambda e: e.matmul(ps_bc[:, 0:512], ones1[:], r(0),
                 start=True, stop=True), deps=[ev_m, k.use(b_bc.name)],
                 label=f"{tag}_bm")
    ev_e1 = p.op("vector", lambda e: e.tensor_copy(mb[:], ps_bc[:, 0:512]),
                 deps=[ev_b1, k.use("ln_mb")], label=f"{tag}_bme")
    ev_b2 = p.op("tensor", lambda e: e.matmul(ps_bc[:, 0:512], ones1[:], r(3),
                 start=True, stop=True), deps=[ev_r, ev_e1],
                 label=f"{tag}_br")
    ev_e2 = p.op("vector", lambda e: e.tensor_copy(rb[:], ps_bc[:, 0:512]),
                 deps=[ev_b2, k.use("ln_rb")], label=f"{tag}_bre")
    k.rel(b_bc.name, ev_e2)

    out_ev = []
    last_sub = None
    last_mul = None
    for c in range(NCH):
        ev_s = p.op("vector", lambda e, c=c: e.tensor_sub(
            tmp[:], x_tiles[c], mb[:]),
            deps=[x_ev[c], ev_e1, k.use("ln_tmp")], label=f"{tag}_sub{c}")
        last_sub = ev_s
        ev_mu = p.op("vector", lambda e, c=c: e.tensor_mul(
            xn_tiles[c], tmp[:], rb[:]),
            deps=[ev_s, ev_e2, k.use(f"{wkey}{c}")], label=f"{tag}_mul{c}")
        k.rel("ln_tmp", ev_mu)
        last_mul = ev_mu
        out_ev.append(ev_mu)
    k.rel("ln_mb", last_sub)
    k.rel("ln_rb", last_mul)
    return out_ev



def emit_proj(k, ws, seq0, ocs, rhs_tiles, rhs_ev, banks, epilogue, tag,
              nt=512):
    p = k.p
    KC = len(rhs_tiles)
    for i, oc in enumerate(ocs):
        col, wev = ws.get(seq0 + i)
        bank = banks[i % len(banks)]
        ev = None
        for kc in range(KC):
            deps = [wev, rhs_ev[kc] if rhs_ev else None]
            if kc == 0:
                deps.append(k.use(bank.name))
            ev = p.op("tensor", lambda e, col=col, kc=kc, bank=bank: e.matmul(
                bank.h[:, 0:nt], ws.tile(col, kc), rhs_tiles[kc],
                start=(kc == 0), stop=(kc == KC - 1)), deps=deps,
                label=f"{tag}_mm{oc}_{kc}")
        ws.release(seq0 + i, ev)
        ev_done = epilogue(i, oc, bank.h[:, 0:nt], ev)
        k.rel(bank.name, ev_done)
    return seq0 + len(ocs)



def build_l1_attn(upto=None, parts=("kv", "pt", "o", "s"), k=None,
                  x_h=None, y_h=None):
    standalone = k is None
    if standalone:
        k = K()
    p = k.p
    if x_h is None:
        x_h = k.dram_in("x", [D, T]).rearrange("(c p) t -> c p t", p=128)
        y_h = k.dram_out("y", [D, T]).rearrange("(c p) t -> c p t", p=128)
    wqkv_h = k.dram_in("l1_wqkv", [24, 128, 1024])
    bqkv_h = k.dram_in("l1_bqkv", [128, 24])
    wout_h = k.dram_in("l1_wout", [8, 128, 1024])
    bout_h = k.dram_in("l1_bout", [128, 8])
    k.load_consts(("c_ones1", "c_onesb", "c_identb", "c_triu"))

    x_tt = [k.sbuf(f"x{c}", [128, 512], F32) for c in range(NCH)]
    xn_tt = [k.sbuf(f"xn{c}", [128, 512], BF16) for c in range(NCH)]
    q_hm = k.sbuf("q_hm", [64, 16 * 512], BF16)
    k_hm = k.sbuf("k_hm", [64, 16 * 512], BF16)
    v_hm = k.sbuf("v_hm", [64, 16 * 512], BF16)
    stg_odd = [k.sbuf(f"stgodd{kk}", [128, 8 * 512], BF16) for kk in range(3)]
    o_hm = k.sbuf("o_hm", [64, 16 * 512], BF16)
    K_sb = k.sbuf("K_sb", [128, 512], BF16)
    V_sb = k.sbuf("V_sb", [128, 512], BF16)
    WT_sb = k.sbuf("WT_sb", [128, 1024], BF16)
    S_sb = k.sbuf("S_sb", [64, 1024], BF16)
    S32 = k.sbuf("S32", [64, 1024], F32)
    oTall = k.sbuf("oTall", [128, 8 * 512], BF16)
    ytmp = k.sbuf("ytmp", [128, 512], F32)
    bqkv = k.sbuf("bqkv", [128, 24], F32)
    bqkv_s = k.sbuf("bqkv_s", [128, 8], F32)
    bout = k.sbuf("bout", [128, 8], F32)

    pS = k.bk("W0")
    pt0, pt1, psp = k.bk("A0"), k.bk("A1"), k.bk("A2")
    pkv = k.bk("P3")
    poA, poB = k.bk("A2"), k.bk("A3")

    ident = k.sb["c_identb"]
    triu = k.sb["c_triu"]

    ev_b = p.op("sync", lambda e: e.dma_start(out=bqkv[:], in_=bqkv_h),
                dma="cst2", label="ld_bqkv")
    ev_bout = p.op("sync", lambda e: e.dma_start(out=bout[:], in_=bout_h),
                   dma="cst2", label="ld_bout")
    ev_b = ev_bout = Ev("cst2", p.cnt["cst2"])
    ev_bs = p.op("vector", lambda e: e.tensor_scalar_mul(
        bqkv_s[:], bqkv[:, 8:16], 0.125), deps=[ev_b], label="bscale")

    ws_qkv = WCols(k, "wqkv", wqkv_h,
                   [oc for _ in range(TT) for oc in range(24)], 1024, "sync")
    ws_out = WCols(k, "wout", wout_h,
                   [oc for _ in range(TT) for oc in range(8)], 1024, "gpsimd")

    seq_qkv = seq_out = 0
    ev_upd = None
    ev_s32i = p.op("vector", lambda e: e.memset(S32[:], 0.0), label="s32init")
    k.rel("S32_init", ev_s32i)
    for tt in range(TT):
        tsl = slice(tt * 512, (tt + 1) * 512)
        x_ev = []
        for c in range(NCH):
            x_ev.append(p.op("sync", lambda e, c=c, tsl=tsl: e.dma_start(
                out=x_tt[c][:], in_=x_h[c, :, tsl]),
                deps=[k.use(f"x{c}")], dma="xld", label=f"ldx{c}_{tt}"))
        x_ev = [x_ev[-1]] * NCH
        xn_ev = emit_ln(k, [t[:] for t in x_tt], x_ev, [t[:] for t in xn_tt],
                        pt0, pt1, psp, f"ln{tt}")

        if upto == "ln":
            for c in range(NCH):
                ev_nn = p.op("vector", lambda e, c=c: e.tensor_add(
                    x_tt[c][:], x_tt[c][:], xn_tt[c][:]), deps=[xn_ev[c]],
                    label=f"dbg_add{c}")
                p.op("gpsimd", lambda e, c=c, tsl=tsl: e.dma_start(
                    out=y_h[c, :, tsl], in_=x_tt[c][:]), deps=[ev_nn],
                    dma="st", label=f"sty{c}_{tt}")
            ev_store_round = Ev("st", p.cnt["st"])
            for c in range(NCH):
                k.rel(f"x{c}", ev_store_round)
                k.rel(f"xn{c}", ev_nn)
            continue

        qkv_wr = {}

        def qkv_epi(i, oc, ps, ev):
            kind = oc // 8
            dst = (q_hm, k_hm, v_hm)[kind]
            if kind == 1:
                b_ap, sc, dep_b = bqkv_s[:, oc - 8:oc - 7], 0.125, ev_bs
            else:
                b_ap, sc, dep_b = bqkv[:, oc:oc + 1], 1.0, ev_b
            h0 = (oc % 8) * 2
            so = stg_odd[kind]
            oc8 = oc % 8
            ev1 = p.op("scalar", lambda e, ps=ps, b_ap=b_ap, sc=sc, h0=h0,
                       dst=dst: e.activation(
                           dst[0:64, 512 * h0:512 * h0 + 512], ps[0:64, :],
                           AF.Identity, bias=b_ap[0:64, :], scale=sc),
                       deps=[ev, dep_b, k.use(f"hm{kind}_{h0}")],
                       label=f"qkvev{oc}a")
            ev2 = p.op("scalar", lambda e, ps=ps, b_ap=b_ap, sc=sc, oc8=oc8,
                       so=so: e.activation(
                           so[64:128, 512 * oc8:512 * oc8 + 512],
                           ps[64:128, :], AF.Identity, bias=b_ap[64:128, :],
                           scale=sc),
                       deps=[ev, dep_b, k.use(f"stg{kind}")],
                       label=f"qkvev{oc}b")
            qkv_wr[oc] = [ev1, ev2]
            return [ev1, ev2]

        seq_qkv = emit_proj(k, ws_qkv, seq_qkv, list(range(24)),
                            [t[:] for t in xn_tt], xn_ev, [pt0, pt1],
                            qkv_epi, f"qkv{tt}")
        for c in range(NCH):
            k.rel(f"xn{c}", p.mark("tensor"))
        hm_ev = {}
        for kind in range(3):
            dst = (q_hm, k_hm, v_hm)[kind]
            src_ap = stg_odd[kind].ap()[64:128, :].rearrange(
                "p (oc t) -> p oc t", t=512)
            dst_ap = dst.ap().rearrange("p (oc pb t) -> p oc pb t",
                                        pb=2, t=512)[:, :, 1, :]
            odd_deps = [qkv_wr[kind * 8 + i][1] for i in range(8)]
            ev_dma = p.op("gpsimd", lambda e, src_ap=src_ap, dst_ap=dst_ap:
                          e.dma_start(out=dst_ap, in_=src_ap),
                          deps=odd_deps + [k.use(f"hmodd{kind}")],
                          dma=f"hmg{kind}", label=f"hmdma{kind}_{tt}")
            k.rel(f"stg{kind}", ev_dma)
            for h in range(16):
                if h % 2 == 0:
                    hm_ev[(kind, h)] = qkv_wr[kind * 8 + h // 2][0]
                else:
                    hm_ev[(kind, h)] = ev_dma

        if upto == "qkv":
            ev_all = [p.mark("scalar"), p.mark("gpsimd"), Ev("hmg0", p.cnt.get("hmg0", 0)),
                      Ev("hmg1", p.cnt.get("hmg1", 0)), Ev("hmg2", p.cnt.get("hmg2", 0))]
            for c in range(NCH):
                ev_nn = p.op("vector", lambda e, c=c: e.tensor_add(
                    x_tt[c][0:64, :], x_tt[c][0:64, :],
                    q_hm[0:64, c * 512:(c + 1) * 512]),
                    deps=ev_all, label=f"dbg_add{c}")
                p.op("gpsimd", lambda e, c=c, tsl=tsl: e.dma_start(
                    out=y_h[c, :, tsl], in_=x_tt[c][:]), deps=[ev_nn],
                    dma="st", label=f"sty{c}_{tt}")
            ev_store_round = Ev("st", p.cnt["st"])
            for c in range(NCH):
                k.rel(f"x{c}", ev_store_round)
            for kind in range(3):
                k.rel(f"hmodd{kind}", p.mark("vector"))
            continue

        for jj in range(4):
            j = tt * 4 + jj
            jsl = lambda h: slice(512 * h + 128 * jj, 512 * h + 128 * jj + 128)
            ev_Sprev = k.use("S_ready")
            for g in range(2):
                heads = list(range(8 * g, 8 * g + 8))
                ev_tk = {}
                ev_tv = {}
                for i_h, h in enumerate(heads):
                    kin = k_hm[0:64, jsl(h)]
                    vin = v_hm[0:64, jsl(h)]
                    ev_tk[h] = p.op("tensor", lambda e, kin=kin, i_h=i_h:
                                    e.transpose(pkv.h[:, 64 * i_h:64 * i_h + 64],
                                                kin, ident[0:64, 0:64]),
                                    deps=[hm_ev[(1, h)], k.use(pkv.name)],
                                    label=f"trK{j}_{h}")
                    ev_tv[h] = p.op("tensor", lambda e, vin=vin, i_h=i_h:
                                    e.transpose(
                                        pkv.h[:, 512 + 64 * i_h:576 + 64 * i_h],
                                        vin, ident[0:64, 0:64]),
                                    deps=[hm_ev[(2, h)]],
                                    label=f"trV{j}_{h}")
                ev_tall = [ev_tk[heads[-1]], ev_tv[heads[-1]]]
                ev_Ke = p.op("vector", lambda e: e.tensor_copy(
                    K_sb[:], pkv.h[:, 0:512]),
                    deps=ev_tall + [k.use("K_sb")], label=f"Kev{j}_{g}")
                ev_Ve = p.op("vector", lambda e: e.tensor_copy(
                    V_sb[:], pkv.h[:, 512:1024]),
                    deps=ev_tall + [k.use("V_sb")], label=f"Vev{j}_{g}")
                k.rel(pkv.name, [ev_Ke, ev_Ve])

                if "pt" not in parts:
                    continue
                ev_pt = {}
                for i_h, h in enumerate(heads):
                    bank = pt0 if i_h < 4 else pt1
                    co = 128 * (i_h % 4)
                    kin = k_hm[0:64, jsl(h)]
                    qin = q_hm[0:64, jsl(h)]
                    ev_pt[h] = p.op("tensor", lambda e, kin=kin, qin=qin,
                                    bank=bank, co=co:
                                    e.matmul(bank.h[:, co:co + 128],
                                             kin, qin, start=True, stop=True),
                                    deps=[hm_ev[(1, h)], hm_ev[(0, h)],
                                          k.use(bank.name)],
                                    label=f"PT{j}_{h}")
                ev_w0 = p.op("vector", lambda e: e.tensor_tensor(
                    WT_sb.ap()[:, 0:512].rearrange("p (r c) -> p r c", c=128),
                    pt0.h.ap()[:, 0:512].rearrange("p (r c) -> p r c", c=128),
                    triu.ap().unsqueeze(1).broadcast_to([128, 4, 128]),
                    ALU.mult),
                    deps=[ev_pt[heads[3]], k.use("WT0")], label=f"WT{j}_{g}0")
                ev_w1 = p.op("vector", lambda e: e.tensor_tensor(
                    WT_sb.ap()[:, 512:1024].rearrange("p (r c) -> p r c", c=128),
                    pt1.h.ap()[:, 0:512].rearrange("p (r c) -> p r c", c=128),
                    triu.ap().unsqueeze(1).broadcast_to([128, 4, 128]),
                    ALU.mult),
                    deps=[ev_pt[heads[7]], k.use("WT1")], label=f"WT{j}_{g}1")
                k.rel(pt0.name, ev_w0)
                k.rel(pt1.name, ev_w1)

                if "o" not in parts:
                    k.rel(pt0.name, ev_w0)
                    k.rel(pt1.name, ev_w1)
                    continue
                do_o1 = "noo1" not in parts
                do_o2 = "noo2" not in parts
                ev_o = {}
                for i_h, h in enumerate(heads):
                    pog = poA if i_h < 4 else poB
                    co = 128 * (i_h % 4)
                    first = True
                    last = None
                    if j > 0 and do_o1:
                        qin = q_hm[0:64, jsl(h)]
                        last = p.op("tensor", lambda e, h=h, co=co, qin=qin,
                                    pog=pog: e.matmul(
                                        pog.h[0:64, co:co + 128],
                                        S_sb[0:64, 64 * h:64 * h + 64],
                                        qin, start=True, stop=False),
                                    deps=[ev_Sprev, hm_ev[(0, h)]] +
                                    ([k.use(pog.name)] if i_h % 4 == 0
                                     else []), label=f"o1_{j}_{h}")
                        first = False
                    wm = ev_w0 if i_h < 4 else ev_w1
                    if do_o2 or last is None:
                        last = p.op("tensor", lambda e, h=h, co=co, i_h=i_h,
                                    first=first, pog=pog: e.matmul(
                                        pog.h[0:64, co:co + 128],
                                        V_sb[:, 64 * i_h:64 * i_h + 64],
                                        WT_sb[:, 128 * i_h:128 * i_h + 128],
                                        start=first, stop=True),
                                    deps=[ev_Ve, wm] +
                                    ([k.use(pog.name)]
                                     if first and i_h % 4 == 0 else []),
                                    label=f"o2_{j}_{h}")
                    ev_o[h] = last
                ev_oe = None
                for i_h, h in enumerate(heads):
                    pog = poA if i_h < 4 else poB
                    co = 128 * (i_h % 4)
                    do = 512 * h + 128 * jj
                    ev_oe = p.op("vector", lambda e, co=co, do=do, pog=pog:
                                 e.tensor_copy(
                                     o_hm[0:64, do:do + 128],
                                     pog.h[0:64, co:co + 128]),
                                 deps=[ev_o[heads[-1]], k.use("o_hm")],
                                 label=f"oev{j}_{h}")
                k.rel(poA.name, ev_oe)
                k.rel(poB.name, ev_oe)

                if "s" not in parts:
                    continue
                for i_h, h in enumerate(heads):
                    ev_upd = p.op("tensor", lambda e, h=h, i_h=i_h: e.matmul(
                        pS.h[0:64, 64 * h:64 * h + 64],
                        K_sb[:, 64 * i_h:64 * i_h + 64],
                        V_sb[:, 64 * i_h:64 * i_h + 64],
                        start=(i_h == 0), stop=(i_h == 7)),
                        deps=[ev_Ke, ev_Ve] +
                        ([k.use(pS.name)] if i_h == 0 and g == 0 else []),
                        label=f"Supd{j}_{h}")
                k.rel("K_sb", ev_upd)
                k.rel("V_sb", ev_upd)
                if "o" in parts:
                    k.rel("WT0", ev_o[heads[3]])
                    k.rel("WT1", ev_o[heads[7]])
            if "s" not in parts:
                continue
            ev_sadd = p.op("vector", lambda e: e.tensor_add(
                S32[:], S32[:], pS.h[0:64, :]),
                deps=[ev_upd, k.use("S32_init"), ev_Sprev], label=f"Sadd{j}")
            ev_scast = p.op("vector", lambda e: e.tensor_copy(S_sb[:], S32[:]),
                            deps=[ev_sadd], label=f"Scast{j}")
            k.rel(pS.name, ev_sadd)
            k.rel("S_ready", ev_scast)

        if upto == "scan":
            ev_all = [p.mark("vector")]
            for c in range(NCH):
                ev_nn = p.op("vector", lambda e, c=c: e.tensor_add(
                    x_tt[c][0:64, :], x_tt[c][0:64, :],
                    o_hm[0:64, c * 512:(c + 1) * 512]),
                    deps=ev_all, label=f"dbg_add{c}")
                p.op("gpsimd", lambda e, c=c, tsl=tsl: e.dma_start(
                    out=y_h[c, :, tsl], in_=x_tt[c][:]), deps=[ev_nn],
                    dma="st", label=f"sty{c}_{tt}")
            ev_store_round = Ev("st", p.cnt["st"])
            for c in range(NCH):
                k.rel(f"x{c}", ev_store_round)
            k.rel("o_hm", p.mark("vector"))
            continue

        ev_scan_done = p.mark("vector")
        ohm_e = o_hm.ap().rearrange("p (hp pb t) -> p hp pb t", pb=2, t=512)
        ev_oe1 = p.op("vector", lambda e: e.tensor_copy(
            oTall.ap()[0:64, :].rearrange("p (hp t) -> p hp t", t=512),
            ohm_e[:, :, 0, :]),
            deps=[ev_scan_done, k.use("oTall")], label=f"oTe{tt}")
        ev_oe2 = p.op("gpsimd", lambda e: e.dma_start(
            out=oTall.ap()[64:128, :].rearrange("p (hp t) -> p hp t", t=512),
            in_=ohm_e[:, :, 1, :]),
            deps=[ev_scan_done, k.use("oTall")], dma="oTq",
            label=f"oTo{tt}")
        k.rel("o_hm", [ev_oe1, ev_oe2])
        ev_oT = [ev_oe1, ev_oe2]

        def out_epi(i, oc, ps, ev):
            ev_a = p.op("scalar", lambda e, oc=oc, ps=ps: e.activation(
                ytmp[:], ps, AF.Identity, bias=bout[:, oc:oc + 1]),
                deps=[ev, ev_bout, k.use("ytmp")], label=f"yact{oc}")
            ev_add = p.op("vector", lambda e, oc=oc: e.tensor_add(
                x_tt[oc][:], x_tt[oc][:], ytmp[:]), deps=[ev_a],
                label=f"yadd{oc}")
            k.rel("ytmp", ev_add)
            p.op("gpsimd", lambda e, oc=oc, tsl=tsl: e.dma_start(
                out=y_h[oc, :, tsl], in_=x_tt[oc][:]), deps=[ev_add],
                dma="st", label=f"sty{oc}_{tt}")
            return ev_a

        seq_out = emit_proj(k, ws_out, seq_out, list(range(8)),
                            [oTall[:, 512 * hp:512 * hp + 512]
                             for hp in range(8)], [ev_oT] * 8,
                            [pt0, pt1], out_epi, f"out{tt}")
        k.rel("oTall", p.mark("tensor"))
        ev_store_round = Ev("st", p.cnt["st"])
        for c in range(NCH):
            k.rel(f"x{c}", ev_store_round)
        for kind in range(3):
            k.rel(f"hmodd{kind}", p.mark("tensor"))
            for h in range(16):
                k.rel(f"hm{kind}_{h}", p.mark("tensor"))

    if standalone:
        p.finalize(k.nc, [Ev("st", p.cnt["st"])])
        return k.nc



class WStatic:

    def __init__(self, plan, tile_fn, ready_ev):
        self.plan, self.tile_fn, self.ev = plan, tile_fn, ready_ev

    def get(self, i):
        return self.plan[i], self.ev

    def release(self, i, ev):
        pass

    def tile(self, oc, kc):
        return self.tile_fn(oc, kc)


def load_full(k, name, hbm_ap, n, width, queue="sync", bufs=2):
    p = k.p
    stg = [k.sbuf(f"{name}_stg{i}", [128, width], F32) for i in range(bufs)]
    w = k.sbuf(f"{name}_w", [128, n * width], BF16)
    stg_free = [None] * bufs
    last = None
    for i in range(n):
        ev_d = p.op(queue, lambda e, i=i, s=stg[i % bufs]: e.dma_start(
            out=s[:], in_=hbm_ap[i]), deps=[stg_free[i % bufs]],
            dma=f"wf_{name}{i % bufs}", label=f"{name}_dma{i}")
        ev_c = p.op("vector", lambda e, i=i, s=stg[i % bufs]: e.tensor_copy(
            w[:, i * width:(i + 1) * width], s[:]), deps=[ev_d],
            label=f"{name}_cast{i}")
        stg_free[i % bufs] = ev_c
        last = ev_c
    return w, last



def build_l1_ffn(k=None, x_h=None, y_h=None):
    standalone = k is None
    if standalone:
        k = K()
    p = k.p
    if x_h is None:
        x_h = k.dram_in("x", [D, T]).rearrange("(c p) t -> c p t", p=128)
        y_h = k.dram_out("y", [D, T]).rearrange("(c p) t -> c p t", p=128)
    wr_h = k.dram_in("l1f_wr", [8, 128, 1024])
    wi_h = k.dram_in("l1f_wi", [8, 128, 1024])
    fr_h = k.dram_in("c_fr", [8, 128, 1024])
    fi_h = k.dram_in("c_fi", [8, 128, 1024])
    fin_h = k.dram_in("c_fin", [8, 128, 1024])
    bias_h = k.dram_in("l1f_bias", [128, 8])
    k.load_consts(("c_ones1", "c_onesb"))

    x_tt = [k.sbuf(f"x{c}", [128, 512], F32) for c in range(NCH)]
    xn_tt = [k.sbuf(f"xn{c}", [128, 512], BF16) for c in range(NCH)]
    ytmp = k.sbuf("ytmp", [128, 512], F32)
    bias = k.sbuf("bias", [128, 8], F32)
    At_r = k.sbuf("At_r", [128, 8192], BF16)
    At_i = k.sbuf("At_i", [128, 8192], BF16)
    weff = k.sbuf("weff", [128, 8192], BF16)

    pt0, pt1, psp = k.bk("A0"), k.bk("A1"), k.bk("A2")

    ev_bias = p.op("sync", lambda e: e.dma_start(out=bias[:], in_=bias_h),
                   dma="cst2", label="ld_bias")

    FRr, ev_fr = load_full(k, "fr", fr_h, 8, 1024, "sync")
    FIr, ev_fi = load_full(k, "fi", fi_h, 8, 1024, "sync")
    FInr, ev_fin = load_full(k, "fin", fin_h, 8, 1024, "sync")
    ev_F = [ev_fr, ev_fi, ev_fin]

    ws_wr = WCols(k, "wwr", wr_h, list(range(8)), 1024, "gpsimd")
    ws_wi = WCols(k, "wwi", wi_h, list(range(8)), 1024, "gpsimd")

    banks = [pt0, pt1]
    bi = 0
    for jc in range(8):
        cr, evr = ws_wr.get(jc)
        ci, evi = ws_wi.get(jc)
        last_r = last_i = None
        for comp in range(2):
            for dt_ in range(2):
                bank = banks[bi % 2]
                bi += 1
                dsl = slice(dt_ * 512, dt_ * 512 + 512)
                ev = None
                for kc in range(8):
                    for wsel in range(2):
                        if comp == 0:
                            lhs = ws_wr.tile(cr, kc) if wsel == 0 else \
                                ws_wi.tile(ci, kc)
                            rhs = FRr[:, kc * 1024:kc * 1024 + 1024][:, dsl] \
                                if wsel == 0 else \
                                FInr[:, kc * 1024:kc * 1024 + 1024][:, dsl]
                        else:
                            lhs = ws_wi.tile(ci, kc) if wsel == 0 else \
                                ws_wr.tile(cr, kc)
                            rhs = FRr[:, kc * 1024:kc * 1024 + 1024][:, dsl] \
                                if wsel == 0 else \
                                FIr[:, kc * 1024:kc * 1024 + 1024][:, dsl]
                        first = (kc == 0 and wsel == 0)
                        ev = p.op("tensor", lambda e, lhs=lhs, rhs=rhs,
                                  bank=bank, first=first, kc=kc, wsel=wsel:
                                  e.matmul(bank.h[:, 0:512], lhs, rhs,
                                           start=first,
                                           stop=(kc == 7 and wsel == 1)),
                                  deps=[evr, evi] + ev_F +
                                  ([k.use(bank.name)] if first else []),
                                  label=f"At{comp}_{jc}_{dt_}")
                dst = (At_r, At_i)[comp]
                ev_e = p.op("vector", lambda e, dst=dst, jc=jc, dsl=dsl,
                            bank=bank: e.tensor_copy(
                                dst[:, jc * 1024:jc * 1024 + 1024][:, dsl],
                                bank.h[:, 0:512]),
                            deps=[ev], label=f"Atev{comp}_{jc}_{dt_}")
                k.rel(bank.name, ev_e)
                if comp == 0:
                    last_r = ev_e
                else:
                    last_i = ev_e
        ws_wr.release(jc, last_i)
        ws_wi.release(jc, last_i)
    ev_At = p.mark("vector")

    for dc in range(8):
        for ot in range(2):
            bank = banks[bi % 2]
            bi += 1
            osl = slice(ot * 512, ot * 512 + 512)
            ev = None
            for jc in range(8):
                for wsel in range(2):
                    lhs = (At_r if wsel == 0 else At_i)[
                        :, jc * 1024 + dc * 128: jc * 1024 + dc * 128 + 128]
                    rhs = (FRr if wsel == 0 else FIr)[
                        :, jc * 1024:jc * 1024 + 1024][:, osl]
                    first = (jc == 0 and wsel == 0)
                    ev = p.op("tensor", lambda e, lhs=lhs, rhs=rhs, bank=bank,
                              first=first, jc=jc, wsel=wsel: e.matmul(
                                  bank.h[:, 0:512], lhs, rhs, start=first,
                                  stop=(jc == 7 and wsel == 1)),
                              deps=[ev_At] +
                              ([k.use(bank.name)] if first else []),
                              label=f"B_{dc}_{ot}")
            ev_e = p.op("scalar", lambda e, dc=dc, osl=osl, bank=bank:
                        e.activation(
                            weff[:, dc * 1024:dc * 1024 + 1024][:, osl],
                            bank.h[:, 0:512], AF.Identity, scale=1.0 / D),
                        deps=[ev], label=f"Bev_{dc}_{ot}")
            k.rel(bank.name, ev_e)
    ev_weff = p.mark("scalar")

    ws_eff = WStatic([oc for _ in range(TT) for oc in range(8)],
                     lambda oc, kc: weff[:, kc * 1024 + oc * 128:
                                         kc * 1024 + oc * 128 + 128],
                     ev_weff)

    seq = 0
    for tt in range(TT):
        tsl = slice(tt * 512, (tt + 1) * 512)
        x_ev = []
        for c in range(NCH):
            x_ev.append(p.op("sync", lambda e, c=c, tsl=tsl: e.dma_start(
                out=x_tt[c][:], in_=x_h[c, :, tsl]),
                deps=[k.use(f"x{c}")], dma="xld", label=f"ldx{c}_{tt}"))
        x_ev = [x_ev[-1]] * NCH
        xn_ev = emit_ln(k, [t[:] for t in x_tt], x_ev, [t[:] for t in xn_tt],
                        pt0, pt1, psp, f"ln{tt}")

        def out_epi(i, oc, ps, ev):
            ev_a = p.op("scalar", lambda e, oc=oc, ps=ps: e.activation(
                ytmp[:], ps, AF.Identity, bias=bias[:, oc:oc + 1]),
                deps=[ev, ev_bias, k.use("ytmp")], label=f"yact{oc}")
            ev_add = p.op("vector", lambda e, oc=oc: e.tensor_add(
                x_tt[oc][:], x_tt[oc][:], ytmp[:]), deps=[ev_a],
                label=f"yadd{oc}")
            k.rel("ytmp", ev_add)
            p.op("gpsimd", lambda e, oc=oc, tsl=tsl: e.dma_start(
                out=y_h[oc, :, tsl], in_=x_tt[oc][:]), deps=[ev_add],
                dma="st", label=f"sty{oc}_{tt}")
            return ev_a

        seq = emit_proj(k, ws_eff, seq, list(range(8)),
                        [t[:] for t in xn_tt], xn_ev, [pt0, pt1],
                        out_epi, f"o{tt}")
        ev_store_round = Ev("st", p.cnt["st"])
        for c in range(NCH):
            k.rel(f"x{c}", ev_store_round)
            k.rel(f"xn{c}", p.mark("tensor"))

    if standalone:
        p.finalize(k.nc, [Ev("st", p.cnt["st"])])
        return k.nc



def build_l2_attn(k=None, x_h=None, y_h=None):
    standalone = k is None
    if standalone:
        k = K()
    p = k.p
    if x_h is None:
        x_h = k.dram_in("x", [D, T]).rearrange("(c p) t -> c p t", p=128)
        y_h = k.dram_out("y", [D, T]).rearrange("(c p) t -> c p t", p=128)
    wqk_h = k.dram_in("l2_wqk", [16, 128, 1024])
    wv_h = k.dram_in("l2_wv", [8, 128, 1024])
    ipw_h = k.dram_in("l2_ipw", [1, 1])
    k.load_consts(("c_ones1", "c_onesb", "c_palace", "c_sel8"))

    x_tt = [k.sbuf(f"x{c}", [128, 512], F32) for c in range(NCH)]
    xn = [k.sbuf(f"xn{c}", [128, T], BF16) for c in range(NCH)]
    q_hm = k.sbuf("q_hm", [64, 4 * T], BF16)
    k_hm = k.sbuf("k_hm", [64, 4 * T], BF16)
    stg_odd = k.sbuf("stg_odd", [128, T], BF16)
    V_sb = [k.sbuf(f"V{tc}", [128, 16 * 65], BF16) for tc in range(16)]
    o_hm = k.sbuf("o_hm", [64, 4 * T], BF16)
    oTall = k.sbuf("oTall", [128, 2 * T], BF16)
    E_sb = [k.sbuf(f"E{i}", [128, 512], BF16) for i in range(2)]
    Epre = k.sbuf("Epre", [128, 512], F32)
    sel8 = k.sb["c_sel8"]
    sgv = k.sbuf("sgv", [1, 1], F32)
    sg128 = k.sbuf("sg128", [128, 1], F32)
    s1m = k.sbuf("s1m", [128, 1], F32)
    meff = k.sbuf("meff", [128, 64], F32)
    dtmp = k.sbuf("dtmp", [65, 512], F32)
    rd16 = k.sbuf("rd16", [16, T], F32)
    xc_buf = k.sbuf("xc_buf", [128, 512], F32)
    tmp_y = k.sbuf("tmp_y", [128, 512], F32)

    pt0, pt1, psp = k.bk("A0"), k.bk("A1"), k.bk("A2")
    pav = k.bk("W0")
    pbc = k.bk("A3")

    ev_ipw = p.op("sync", lambda e: e.dma_start(out=sgv[:], in_=ipw_h),
                  dma="cst2", label="ld_ipw")
    ev_c2 = [Ev("cst2", p.cnt["cst2"]), k.use("consts")]

    ev_sg = p.op("scalar", lambda e: e.activation(sgv[:], sgv[:], AF.Sigmoid),
                 deps=[ev_c2], label="sig")
    ev_b = p.op("tensor", lambda e: e.matmul(psp.h[:, 0:1], k.sb["c_ones1"][:],
                sgv[:], start=True, stop=True),
                deps=[ev_sg, k.use("consts"), k.use(psp.name)], label="sgb")
    ev_sg128 = p.op("vector", lambda e: e.tensor_copy(sg128[:], psp.h[:, 0:1]),
                    deps=[ev_b], label="sgev")
    k.rel(psp.name, ev_sg128)
    ev_s1 = p.op("vector", lambda e: e.tensor_scalar(
        s1m[:], sg128[:], -1.0, 1.0, ALU.mult, ALU.add), deps=[ev_sg128],
        label="s1m")
    ev_me = p.op("vector", lambda e: e.tensor_scalar(
        meff[:], k.sb["c_palace"][:], s1m[:, 0:1], sg128[:, 0:1],
        ALU.mult, ALU.add), deps=[ev_s1, k.use("consts")], label="meff")

    for tt in range(TT):
        tsl = slice(tt * 512, (tt + 1) * 512)
        x_ev = []
        for c in range(NCH):
            x_ev.append(p.op("sync", lambda e, c=c, tsl=tsl: e.dma_start(
                out=x_tt[c][:], in_=x_h[c, :, tsl]),
                deps=[k.use(f"x{c}")], dma="xld", label=f"ldx{c}_{tt}"))
        x_ev = [x_ev[-1]] * NCH
        xn_ev = emit_ln(k, [t[:] for t in x_tt], x_ev,
                        [t[:, tsl] for t in xn], pt0, pt1, psp, f"ln{tt}")
        for c in range(NCH):
            k.rel(f"x{c}", xn_ev[c])
    ev_xn = p.mark("vector")

    wv_sb, ev_wv = load_full(k, "wv", wv_h, 8, 1024, "sync", bufs=1)
    ev_vset = []
    for tc in range(16):
        ev_vset.append(p.op("vector", lambda e, tc=tc: e.memset(
            V_sb[tc].ap().rearrange("p (h s) -> p h s", s=65)[:, :, 64:65],
            1.0), label=f"vones{tc}"))
    for tc in range(16):
        csl = slice((tc % 4) * 128, (tc % 4) * 128 + 128)
        for half in range(2):
            bank = (pt0, pt1)[half]
            ev = None
            for kc in range(8):
                lhs = xn[kc][:, tc * 128:tc * 128 + 128]
                rhs = wv_sb[:, kc * 1024 + half * 512:kc * 1024 + half * 512 + 512]
                ev = p.op("tensor", lambda e, lhs=lhs, rhs=rhs, bank=bank,
                          kc=kc: e.matmul(bank.h[:, 0:512], lhs, rhs,
                                          start=(kc == 0), stop=(kc == 7)),
                          deps=[ev_wv, ev_xn] +
                          ([k.use(bank.name)] if kc == 0 else []),
                          label=f"V{tc}_{half}")
            dst = V_sb[tc].ap().rearrange("p (h s) -> p h s", s=65)[
                :, half * 8:half * 8 + 8, 0:64]
            src = bank.h.ap()[:, 0:512].rearrange("p (h s) -> p h s", s=64)
            ev_e = p.op("vector", lambda e, dst=dst, src=src: e.tensor_copy(
                dst, src), deps=[ev, ev_vset[tc]], label=f"Vev{tc}_{half}")
            k.rel(bank.name, ev_e)
    ev_V = p.mark("vector")

    plan_qk = []
    for hg_ in range(4):
        plan_qk += [2 * hg_, 2 * hg_ + 1, 8 + 2 * hg_, 8 + 2 * hg_ + 1]
    ws_qk = WCols(k, "wqk", wqk_h, plan_qk, 1024, "sync", stg_bufs=1)
    seq_qk = 0
    for hg in range(4):
        for kind in range(2):
            ws = ws_qk
            dst = (q_hm, k_hm)[kind]
            for oci in range(2):
                oc = 2 * hg + oci
                col, evw = ws.get(seq_qk)
                last = None
                for tt in range(TT):
                    tsl = slice(tt * 512, (tt + 1) * 512)
                    bank = (pt0, pt1)[tt % 2]
                    ev = None
                    for kc in range(8):
                        lhst = ws.tile(col, kc)
                        rhst = xn[kc][:, tsl]
                        ev = p.op("tensor", lambda e, lhst=lhst, rhst=rhst,
                                  bank=bank, kc=kc: e.matmul(
                                      bank.h[:, 0:512], lhst, rhst,
                                      start=(kc == 0), stop=(kc == 7)),
                                  deps=[evw, ev_xn] +
                                  ([k.use(bank.name)] if kc == 0 else []),
                                  label=f"p{kind}_{oc}_{tt}")
                    h0 = 2 * oci
                    ev1 = p.op("scalar", lambda e, bank=bank, dst=dst, h0=h0,
                               tsl=tsl: e.activation(
                                   dst[0:64, h0 * T:h0 * T + T][:, tsl],
                                   bank.h[0:64, 0:512], AF.Identity),
                               deps=[ev, k.use(f"hm{kind}_{h0}")],
                               label=f"pe{kind}_{oc}_{tt}a")
                    ev2 = p.op("scalar", lambda e, bank=bank,
                               tsl=tsl: e.activation(
                                   stg_odd[64:128, tsl],
                                   bank.h[64:128, 0:512], AF.Identity),
                               deps=[ev, k.use("stgodd")],
                               label=f"pe{kind}_{oc}_{tt}b")
                    k.rel(bank.name, [ev1, ev2])
                    last = [ev1, ev2]
                ws.release(seq_qk, Ev("tensor", p.cnt["tensor"]))
                seq_qk += 1
                dst_ap = dst.ap().rearrange(
                    "p (oc pb t) -> p oc pb t", pb=2, t=T)[:, oci, 1, :]
                ev_dma = p.op("gpsimd", lambda e, dst_ap=dst_ap: e.dma_start(
                    out=dst_ap, in_=stg_odd[64:128, :]),
                    deps=[last, k.use(f"hmodd{kind}")],
                    dma=f"hmg{kind}", label=f"hmdma{kind}_{hg}_{oci}")
                k.rel("stgodd", ev_dma)
                k.rel(f"podd{kind}_{oci}", ev_dma)
            ev_dma_all = [k.use(f"podd{kind}_{oci}") for oci in range(2)]
            k.rel(f"hmk{kind}", ev_dma_all)

        ev_hm = [k.use("hmk0"), k.use("hmk1")]

        for hi in range(4):
            h = 4 * hg + hi
            for qt in range(TT):
                qsl = slice(qt * 512, qt * 512 + 512)
                ev_av = None
                for kc in range(16):
                    bank = (pt0, pt1)[kc % 2]
                    lhsp = k_hm[0:64, hi * T + kc * 128: hi * T + kc * 128 + 128]
                    rhsp = q_hm[0:64, hi * T:hi * T + T][:, qsl]
                    ev_pt = p.op("tensor", lambda e, lhsp=lhsp, rhsp=rhsp,
                                 bank=bank: e.matmul(bank.h[:, 0:512], lhsp,
                                                     rhsp, start=True,
                                                     stop=True),
                                 deps=ev_hm + [k.use(bank.name)],
                                 label=f"PT{h}_{qt}_{kc}")
                    eb = E_sb[kc % 2]
                    ev_pre = p.op("vector", lambda e, bank=bank: (
                        e.scalar_tensor_tensor(
                            Epre.ap().rearrange("p (r c) -> p r c", c=64),
                            bank.h.ap()[:, 0:512].rearrange(
                                "p (r c) -> p r c", c=64),
                            400.0,
                            meff.ap().unsqueeze(1).broadcast_to([128, 8, 64]),
                            ALU.min, ALU.mult)),
                        deps=[ev_pt, ev_me, k.use("Epre")],
                        label=f"Epre{h}_{qt}_{kc}")
                    k.rel(bank.name, ev_pre)
                    ev_exp = p.op("scalar", lambda e, eb=eb: e.activation(
                        eb[:], Epre[:], AF.Exp, scale=0.125),
                        deps=[ev_pre, k.use(f"E{kc % 2}")],
                        label=f"exp{h}_{qt}_{kc}")
                    k.rel("Epre", ev_exp)
                    lhsv = V_sb[kc][:, 65 * h:65 * h + 65]
                    ev_av = p.op("tensor", lambda e, lhsv=lhsv, eb=eb, kc=kc:
                                 e.matmul(pav.h[:, 0:512], lhsv, eb[:],
                                          start=(kc == 0), stop=(kc == 15)),
                                 deps=[ev_exp, ev_V] +
                                 ([k.use(pav.name)] if kc == 0 else []),
                                 label=f"AV{h}_{qt}_{kc}")
                    k.rel(f"E{kc % 2}", ev_av)
                ev_oe = p.op("vector", lambda e, hi=hi, qsl=qsl: e.tensor_copy(
                    o_hm[0:64, hi * T:hi * T + T][:, qsl], pav.h[0:64, 0:512]),
                    deps=[ev_av, k.use("o_hm")], label=f"oev{h}_{qt}")
                ev_de = p.op("vector", lambda e: e.tensor_copy(
                    dtmp[64:65, 0:512], pav.h[64:65, 0:512]),
                    deps=[ev_av, k.use("dtmp")], label=f"dev{h}_{qt}")
                k.rel(pav.name, [ev_oe, ev_de])
                ev_dd = p.op("gpsimd", lambda e, hi=hi, qsl=qsl: e.dma_start(
                    out=rd16[hi:hi + 1, qsl], in_=dtmp[64:65, 0:512]),
                    deps=[ev_de, k.use("rd16")], dma="dq",
                    label=f"dmv{h}_{qt}")
                k.rel("dtmp", ev_dd)
                k.rel("rd_part", ev_dd)

        ev_rd = p.op("vector", lambda e: e.reciprocal(rd16[0:4, :],
                     rd16[0:4, :]), deps=[k.use("rd_part")], label=f"rcp{hg}")

        ev_scan = p.mark("vector")
        ohm_e = o_hm.ap().rearrange("p (hp pb t) -> p hp pb t", pb=2, t=T)
        ev_oe1 = p.op("vector", lambda e: e.tensor_copy(
            oTall.ap()[0:64, :].rearrange("p (hp t) -> p hp t", t=T),
            ohm_e[:, :, 0, :]),
            deps=[ev_scan, k.use("oTall")], label=f"oTe{hg}")
        ev_oe2 = p.op("gpsimd", lambda e: e.dma_start(
            out=oTall.ap()[64:128, :].rearrange("p (hp t) -> p hp t", t=T),
            in_=ohm_e[:, :, 1, :]),
            deps=[ev_scan, k.use("oTall")], dma="oTq", label=f"oTo{hg}")
        ev_oT = [ev_oe1, ev_oe2]
        k.rel("o_hm", ev_oT)

        for hpl in range(2):
            c = 2 * hg + hpl
            for qt in range(TT):
                qsl = slice(qt * 512, qt * 512 + 512)
                ev_x = p.op("sync", lambda e, c=c, qsl=qsl: e.dma_start(
                    out=xc_buf[:], in_=x_h[c, :, qsl]),
                    deps=[k.use("xc_buf")], dma="xr", label=f"ldxr{c}_{qt}")
                ev_bc = p.op("tensor", lambda e, hpl=hpl, qsl=qsl: e.matmul(
                    pbc.h[:, 0:512], sel8[0:4, 128 * hpl:128 * hpl + 128],
                    rd16[0:4, qsl], start=True, stop=True),
                    deps=[ev_rd, ev_c2, k.use(pbc.name)],
                    label=f"bc{hg}_{hpl}_{qt}")
                ev_t = p.op("vector", lambda e, hpl=hpl, qsl=qsl: e.tensor_mul(
                    tmp_y[:], oTall[:, hpl * T:hpl * T + T][:, qsl],
                    pbc.h[:, 0:512]),
                    deps=[ev_bc, ev_oT, k.use("tmp_y")],
                    label=f"div{hg}_{hpl}_{qt}")
                k.rel(pbc.name, ev_t)
                ev_add = p.op("vector", lambda e: e.tensor_add(
                    xc_buf[:], xc_buf[:], tmp_y[:]),
                    deps=[ev_t, ev_x], label=f"radd{hg}_{hpl}_{qt}")
                k.rel("tmp_y", ev_add)
                p.op("gpsimd", lambda e, c=c, qsl=qsl: e.dma_start(
                    out=y_h[c, :, qsl], in_=xc_buf[:]), deps=[ev_add],
                    dma="st", label=f"sty{c}_{qt}")
                k.rel("xc_buf", Ev("st", p.cnt["st"]))
        k.rel("oTall", p.mark("vector"))
        k.rel("rd16", p.mark("tensor"))
        k.rel("rd_part", None)
        k.rel("hmodd0", p.mark("tensor"))
        k.rel("hmodd1", p.mark("tensor"))
        for kind in range(2):
            for h0 in range(4):
                k.rel(f"hm{kind}_{h0}", p.mark("tensor"))

    if standalone:
        p.finalize(k.nc, [Ev("st", p.cnt["st"])])
        return k.nc



def build_moe(pfx, E, F, topk, k=None, x_h=None, y_h=None):
    OC1 = F // 128
    NPARTS = F // 1024
    standalone = k is None
    if standalone:
        k = K()
    k.set_banks("moe")
    p = k.p
    if x_h is None:
        x_h = k.dram_in("x", [D, T]).rearrange("(c p) t -> c p t", p=128)
        y_h = k.dram_out("y", [D, T]).rearrange("(c p) t -> c p t", p=128)
    wr_h = k.dram_in(f"{pfx}_wr", [128, 8 * E])
    w1_h = k.dram_in(f"{pfx}_w1", [E * OC1, 128, 1024])
    b1_h = k.dram_in(f"{pfx}_b1", [128, E * OC1])
    w2_h = k.dram_in(f"{pfx}_w2", [E * NPARTS * 8, 128, 1024])
    b2_h = k.dram_in(f"{pfx}_b2", [128, 8 * E])
    k.load_consts(("c_ones1", "c_onesb", "c_identf", "c_sel16"))

    x_tt = [k.sbuf(f"x{c}", [128, 512], F32) for c in range(NCH)]
    xn = [k.sbuf(f"xn{c}", [128, T], BF16) for c in range(NCH)]
    delta = [k.sbuf(f"delta{c}", [128, T], F32) for c in range(NCH)]
    hT = [k.sbuf(f"hT{j}", [128, T], BF16) for j in range(8)]
    wr_sb = k.sbuf("wr_sb", [128, 8 * E], BF16)
    wr_stg = k.sbuf("wr_stg", [128, 8 * E], F32)
    b1 = k.sbuf("b1", [128, E * OC1], F32)
    b2 = k.sbuf("b2", [128, 8 * E], F32)
    g_rowsb = k.sbuf("g_rowsb", [16, T], BF16)
    exps = k.sbuf("exps", [128, E], F32)
    mask1 = k.sbuf("mask1", [128, E], F32)
    rsum = k.sbuf("rsum", [128, 2], F32)
    gtmp = k.sbuf("gtmp", [128, E], F32)
    gb_sb = k.sbuf("gb_sb", [128, T], BF16)
    ytmp_v = k.sbuf("ytmp_v", [128, 512], F32)

    pt0, pt1, psp = k.bk("A0"), k.bk("A1"), k.bk("A2")
    pgt = k.bk("A3")

    ev_cst2 = []
    for ap, sb in ((wr_h, wr_stg), (b1_h, b1), (b2_h, b2)):
        ev_cst2.append(p.op("sync", lambda e, ap=ap, sb=sb: e.dma_start(
            out=sb[:], in_=ap), dma="cst2", label="ldc2"))
    ev_c2 = Ev("cst2", p.cnt["cst2"])
    ev_wr = p.op("vector", lambda e: e.tensor_copy(wr_sb[:], wr_stg[:]),
                 deps=[ev_c2], label="wrcast")

    for tt in range(TT):
        tsl = slice(tt * 512, (tt + 1) * 512)
        x_ev = []
        for c in range(NCH):
            x_ev.append(p.op("sync", lambda e, c=c, tsl=tsl: e.dma_start(
                out=x_tt[c][:], in_=x_h[c, :, tsl]),
                deps=[k.use(f"x{c}")], dma="xld", label=f"ldx{c}_{tt}"))
        x_ev = [x_ev[-1]] * NCH
        xn_ev = emit_ln(k, [t[:] for t in x_tt], x_ev,
                        [t[:, tsl] for t in xn], pt0, pt1, psp, f"ln{tt}")
        for c in range(NCH):
            k.rel(f"x{c}", xn_ev[c])
    ev_xn = p.mark("vector")

    ev_gz = p.op("vector", lambda e: e.memset(g_rowsb[:], 0.0), label="gz")
    for tc in range(16):
        ev = None
        for kc in range(8):
            lhs = xn[kc][:, tc * 128:tc * 128 + 128]
            rhs = wr_sb[:, kc * E:kc * E + E]
            ev = p.op("tensor", lambda e, lhs=lhs, rhs=rhs, kc=kc: e.matmul(
                pt0.h[:, 0:E], lhs, rhs, start=(kc == 0), stop=(kc == 7)),
                deps=[ev_wr, ev_xn] + ([k.use(pt0.name)] if kc == 0 else []),
                label=f"rt{tc}")
        ps = pt0.h[:, 0:E]
        ev_mx = p.op("vector", lambda e: e.reduce_max(
            rsum[:, 1:2], ps, axis=AX, negate=True),
            deps=[ev, k.use("rsum")], label=f"rmx{tc}")
        ev_ex = p.op("scalar", lambda e: e.activation(
            exps[:], ps, AF.Exp, bias=rsum[:, 1:2], accum_out=rsum[:, 0:1]),
            deps=[ev_mx, k.use("exps")], label=f"rex{tc}")
        ev_m1 = p.op("vector", lambda e: e.tensor_scalar(
            mask1[:], ps, rsum[:, 1:2], 0.0, ALU.add, ALU.is_ge),
            deps=[ev_ex, k.use("mask1")], label=f"rm1{tc}")
        if topk == 2:
            ev_s2 = p.op("vector", lambda e: e.scalar_tensor_tensor(
                gtmp[:], mask1[:], -1e30, ps, ALU.mult, ALU.add),
                deps=[ev_m1, k.use("gtmp")], label=f"rs2{tc}")
            ev_mx2 = p.op("vector", lambda e: e.reduce_max(
                rsum[:, 1:2], gtmp[:], axis=AX, negate=True),
                deps=[ev_s2], label=f"rmx2{tc}")
            ev_m2 = p.op("vector", lambda e: e.tensor_scalar(
                gtmp[:], gtmp[:], rsum[:, 1:2], 0.0, ALU.add, ALU.is_ge),
                deps=[ev_mx2], label=f"rm2{tc}")
            ev_m1 = p.op("vector", lambda e: e.tensor_add(
                mask1[:], mask1[:], gtmp[:]), deps=[ev_m2], label=f"rms{tc}")
        k.rel(pt0.name, ev_m1)
        ev_rc = p.op("vector", lambda e: e.reciprocal(
            rsum[:, 0:1], rsum[:, 0:1]), deps=[ev_ex, ev_m1], label=f"rrc{tc}")
        ev_g1 = p.op("vector", lambda e: e.tensor_mul(
            gtmp[:], exps[:], mask1[:]),
            deps=[ev_rc, k.use("gtmp")], label=f"rg1{tc}")
        ev_g2 = p.op("vector", lambda e: e.tensor_scalar_mul(
            gtmp[:], gtmp[:], rsum[:, 0:1]), deps=[ev_g1], label=f"rg2{tc}")
        k.rel("rsum", ev_g2)
        k.rel("exps", ev_g2)
        k.rel("mask1", ev_g2)
        ev_tr = p.op("tensor", lambda e: e.transpose(
            pgt.h[0:E, 0:128], gtmp[:], k.sb["c_identf"][:]),
            deps=[ev_g2, k.use(pgt.name), k.use("consts")], label=f"rtr{tc}")
        ev_ge = p.op("vector", lambda e, tc=tc: e.tensor_copy(
            g_rowsb[0:E, tc * 128:tc * 128 + 128], pgt.h[0:E, 0:128]),
            deps=[ev_tr, ev_gz], label=f"rge{tc}")
        k.rel(pgt.name, ev_ge)
        k.rel("gtmp", ev_tr)
    ev_gates = p.mark("vector")

    ws1 = WCols(k, "w1", w1_h, list(range(E * OC1)), 1024, "sync",
                stg_bufs=2, col_bufs=2)
    ws2 = WCols(k, "w2", w2_h, list(range(E * NPARTS * 8)), 1024, "gpsimd",
                stg_bufs=2, col_bufs=2)
    slot = [0]

    def quads():
        qs = [k.bk(f"A{4 * (slot[0] % 2) + i}") for i in range(4)]
        slot[0] += 1
        return qs

    s1 = s2 = 0
    gb_ev = [None] * 4
    dev = {}
    for e_ in range(E):
        sel_ap = k.sb["c_sel16"][:, 128 * e_:128 * e_ + 128]
        qs = quads()
        for q in range(4):
            qsl = slice(q * 512, q * 512 + 512)
            bank = qs[q]
            ev_b = p.op("tensor", lambda e, bank=bank, qsl=qsl,
                        sel_ap=sel_ap: e.matmul(
                            bank.h[:, 0:512], sel_ap, g_rowsb[0:16, qsl],
                            start=True, stop=True),
                        deps=[ev_gates, k.use("consts"), k.use(bank.name)],
                        label=f"gb{e_}_{q}")
            ev_e = p.op("vector", lambda e, bank=bank, qsl=qsl: e.tensor_copy(
                gb_sb[:, qsl], bank.h[:, 0:512]),
                deps=[ev_b, k.use(f"gbq{q}")], label=f"gbe{e_}_{q}")
            k.rel(bank.name, ev_e)
            gb_ev[q] = ev_e
        for part in range(NPARTS):
            for j in range(8):
                ft = part * 8 + j
                col, evw = ws1.get(s1)
                qs = quads()
                tiles = [col[:, kc * 128:kc * 128 + 128] for kc in range(8)]

                def em1(e, tiles=tiles, qs=qs):
                    for kc in range(8):
                        for q in range(4):
                            inst = e.matmul(
                                qs[q].h[:, 0:512], tiles[kc],
                                xn[kc][:, q * 512:q * 512 + 512],
                                start=(kc == 0), stop=(kc == 7))
                    return inst
                ev_mm = p.op("tensor", em1,
                             deps=[evw, ev_xn] + [k.use(b.name) for b in qs],
                             label=f"s1_{e_}_{ft}")
                ws1.release(s1, Ev("tensor", p.cnt["tensor"]))
                s1 += 1
                bcol = b1[:, e_ * OC1 + ft:e_ * OC1 + ft + 1]
                for q in range(4):
                    qsl = slice(q * 512, q * 512 + 512)
                    ev_g = p.op("scalar", lambda e, bank=qs[q], qsl=qsl,
                                bcol=bcol, j=j: e.activation(
                                    hT[j][:, qsl], bank.h[:, 0:512], AF.Gelu,
                                    bias=bcol),
                                deps=[ev_mm, ev_c2, k.use("hTr")],
                                label=f"s1e{e_}_{ft}_{q}")
                    k.rel(qs[q].name, ev_g)
            ev_h = p.mark("scalar")
            for oc2 in range(8):
                col2, evw2 = ws2.get(s2)
                qs = quads()
                tiles2 = [col2[:, kc * 128:kc * 128 + 128] for kc in range(8)]

                def em2(e, tiles2=tiles2, qs=qs):
                    for kc in range(8):
                        for q in range(4):
                            inst = e.matmul(
                                qs[q].h[:, 0:512], tiles2[kc],
                                hT[kc][:, q * 512:q * 512 + 512],
                                start=(kc == 0), stop=(kc == 7))
                    return inst
                ev_mm2 = p.op("tensor", em2,
                              deps=[evw2, ev_h] + [k.use(b.name) for b in qs],
                              label=f"s2_{e_}_{part}_{oc2}")
                ws2.release(s2, Ev("tensor", p.cnt["tensor"]))
                s2 += 1
                b2col = b2[:, e_ * 8 + oc2:e_ * 8 + oc2 + 1]
                for q in range(4):
                    qsl = slice(q * 512, q * 512 + 512)
                    bank = qs[q]
                    if part == 0 and e_ == 0:
                        ev_d = p.op("vector", lambda e, bank=bank, qsl=qsl,
                                    oc2=oc2, b2col=b2col:
                                    e.scalar_tensor_tensor(
                                        delta[oc2][:, qsl], bank.h[:, 0:512],
                                        b2col, gb_sb[:, qsl],
                                        ALU.add, ALU.mult),
                                    deps=[ev_mm2, ev_c2, gb_ev[q]],
                                    label=f"s2d{e_}_{oc2}_{q}")
                    else:
                        if part == 0:
                            ev_t = p.op("vector", lambda e, bank=bank, qsl=qsl,
                                        b2col=b2col: e.scalar_tensor_tensor(
                                            ytmp_v[:], bank.h[:, 0:512],
                                            b2col, gb_sb[:, qsl],
                                            ALU.add, ALU.mult),
                                        deps=[ev_mm2, ev_c2, gb_ev[q],
                                              k.use("ytmp_v")],
                                        label=f"s2b{e_}_{oc2}_{q}")
                        else:
                            ev_t = p.op("vector", lambda e, bank=bank, qsl=qsl:
                                        e.tensor_mul(ytmp_v[:],
                                                     bank.h[:, 0:512],
                                                     gb_sb[:, qsl]),
                                        deps=[ev_mm2, gb_ev[q],
                                              k.use("ytmp_v")],
                                        label=f"s2t{e_}_{part}_{oc2}_{q}")
                        ev_d = p.op("vector", lambda e, qsl=qsl, oc2=oc2:
                                    e.tensor_add(delta[oc2][:, qsl],
                                                 delta[oc2][:, qsl], ytmp_v[:]),
                                    deps=[ev_t, dev.get((oc2, q))],
                                    label=f"s2a{e_}_{part}_{oc2}_{q}")
                        k.rel("ytmp_v", ev_d)
                    dev[(oc2, q)] = ev_d
                    k.rel(f"gbq{q}", ev_d)
                    k.rel(bank.name, ev_d)
            k.rel("hTr", Ev("tensor", p.cnt["tensor"]))

    for c in range(NCH):
        for tt in range(TT):
            tsl = slice(tt * 512, (tt + 1) * 512)
            ev_x = p.op("sync", lambda e, c=c, tsl=tsl: e.dma_start(
                out=x_tt[c][:], in_=x_h[c, :, tsl]),
                deps=[k.use(f"x{c}")], dma=f"xr{c}", label=f"ldxr{c}_{tt}")
            ev_a = p.op("vector", lambda e, c=c, tsl=tsl: e.tensor_add(
                x_tt[c][:], x_tt[c][:], delta[c][:, tsl]),
                deps=[ev_x, dev[(c, tt)]], label=f"fadd{c}_{tt}")
            p.op("gpsimd", lambda e, c=c, tsl=tsl: e.dma_start(
                out=y_h[c, :, tsl], in_=x_tt[c][:]), deps=[ev_a],
                dma="st", label=f"fst{c}_{tt}")
            k.rel(f"x{c}", Ev("st", p.cnt["st"]))

    if standalone:
        p.finalize(k.nc, [Ev("st", p.cnt["st"])])
        return k.nc


def build_l0_ffn(**kw):
    return build_moe("l0f", 8, 4096, 1, **kw)


def build_l2_ffn(**kw):
    return build_moe("l2f", 8, 4096, 2, **kw)


def build_l3_ffn(**kw):
    return build_moe("l3f", 14, 2048, 1, **kw)



def build_l0_attn(k=None, x_h=None, y_h=None):
    standalone = k is None
    if standalone:
        k = K()
    p = k.p
    if x_h is None:
        x_h = k.dram_in("x", [D, T]).rearrange("(c p) t -> c p t", p=128)
        y_h = k.dram_out("y", [D, T]).rearrange("(c p) t -> c p t", p=128)
    wqkv_h = k.dram_in("l0_wqkv", [24, 128, 1024])
    wo_h = k.dram_in("l0_wo", [8, 128, 1024])
    k.load_consts(("c_ones1", "c_onesb", "c_identb", "c_triu", "c_tril"))

    x_tt = [k.sbuf(f"x{c}", [128, 512], F32) for c in range(NCH)]
    xn_tt = [k.sbuf(f"xn{c}", [128, 512], BF16) for c in range(NCH)]
    q_hm = k.sbuf("q_hm", [64, 16 * 512], BF16)
    k_hm = k.sbuf("k_hm", [64, 16 * 512], BF16)
    v_hm = k.sbuf("v_hm", [64, 16 * 512], BF16)
    stg_odd = k.sbuf("stg_odd", [128, 8 * 512], BF16)
    Qtok = k.sbuf("Qtok", [128, 512], BF16)
    Ktok = k.sbuf("Ktok", [128, 512], BF16)
    Vtok = k.sbuf("Vtok", [128, 512], BF16)
    PT_sb = k.sbuf("PT_sb", [128, 1024], BF16)
    WT_sb = k.sbuf("WT_sb", [128, 1024], BF16)
    W_sb = k.sbuf("W_sb", [128, 1024], BF16)
    ATm_sb = k.sbuf("ATm_sb", [128, 1024], BF16)
    C1_sb = k.sbuf("C1_sb", [64, 1024], BF16)
    Z_sb = k.sbuf("Z_sb", [128, 512], BF16)
    N1_sb = k.sbuf("N1_sb", [64, 512], BF16)
    S32 = k.sbuf("S32", [64, 1024], F32)
    M32 = k.sbuf("M32", [64, 1024], F32)
    S_sb = k.sbuf("S_sb", [64, 1024], BF16)
    M_sb = k.sbuf("M_sb", [64, 1024], BF16)
    o_hm = k.sbuf("o_hm", [64, 16 * 512], BF16)
    oTall = k.sbuf("oTall", [128, 8 * 512], BF16)
    ytmp = k.sbuf("ytmp", [128, 512], F32)

    pM = k.bk("W0")
    pt0, pt1, psp = k.bk("A0"), k.bk("A1"), k.bk("A2")
    pkv = k.bk("P3")
    pSa, pSb = k.bk("A2"), k.bk("A3")

    ident = k.sb["c_identb"]
    triu, tril = k.sb["c_triu"], k.sb["c_tril"]

    ev_init = p.op("vector", lambda e: e.memset(S32[:], 0.0), label="s32i")
    ev_init = p.op("vector", lambda e: e.memset(M32[:], 0.0),
                   deps=[ev_init], label="m32i")
    k.rel("S32i", ev_init)

    ws_qkv = WCols(k, "wqkv", wqkv_h,
                   [pi * 8 + oc for _ in range(TT) for pi in range(3)
                    for oc in range(8)], 1024, "sync")
    ws_out = WCols(k, "wout", wo_h,
                   [oc for _ in range(TT) for oc in range(8)], 1024, "gpsimd")

    seq_qkv = seq_out = 0
    ev_upd = ev_updM = None
    for tt in range(TT):
        tsl = slice(tt * 512, (tt + 1) * 512)
        x_ev = []
        for c in range(NCH):
            x_ev.append(p.op("sync", lambda e, c=c, tsl=tsl: e.dma_start(
                out=x_tt[c][:], in_=x_h[c, :, tsl]),
                deps=[k.use(f"x{c}")], dma="xld", label=f"ldx{c}_{tt}"))
        x_ev = [x_ev[-1]] * NCH
        xn_ev = emit_ln(k, [t[:] for t in x_tt], x_ev, [t[:] for t in xn_tt],
                        pt0, pt1, psp, f"ln{tt}")

        hm_wr = {}
        for pi, dst in enumerate((q_hm, k_hm, v_hm)):
            for oc in range(8):
                col, evw = ws_qkv.get(seq_qkv)
                bank = (pt0, pt1)[oc % 2]
                ev = None
                for kc in range(8):
                    lhst = ws_qkv.tile(col, kc)
                    rhst = xn_tt[kc][:]
                    ev = p.op("tensor", lambda e, lhst=lhst, rhst=rhst,
                              bank=bank, kc=kc: e.matmul(
                                  bank.h[:, 0:512], lhst, rhst,
                                  start=(kc == 0), stop=(kc == 7)),
                              deps=[evw, xn_ev[kc]] +
                              ([k.use(bank.name)] if kc == 0 else []),
                              label=f"pj{pi}_{oc}_{tt}")
                h0 = 2 * oc
                ev1 = p.op("scalar", lambda e, bank=bank, dst=dst, h0=h0:
                           e.activation(dst[0:64, 512 * h0:512 * h0 + 512],
                                        bank.h[0:64, 0:512], AF.Identity),
                           deps=[ev, k.use(f"hm{pi}_{h0}")],
                           label=f"pe{pi}_{oc}a")
                ev2 = p.op("scalar", lambda e, bank=bank, oc=oc:
                           e.activation(
                               stg_odd[64:128, 512 * oc:512 * oc + 512],
                               bank.h[64:128, 0:512], AF.Identity),
                           deps=[ev, k.use(f"stgodd_{oc}")],
                           label=f"pe{pi}_{oc}b")
                k.rel(bank.name, [ev1, ev2])
                hm_wr[(pi, h0)] = ev1
                hm_wr[(pi, 2 * oc + 1, "stg")] = ev2
                ws_qkv.release(seq_qkv, Ev("tensor", p.cnt["tensor"]))
                seq_qkv += 1
            src_ap = stg_odd.ap()[64:128, :].rearrange(
                "p (oc t) -> p oc t", t=512)
            dst_ap = dst.ap().rearrange("p (oc pb t) -> p oc pb t",
                                        pb=2, t=512)[:, :, 1, :]
            odd_deps = [hm_wr[(pi, 2 * oc + 1, "stg")] for oc in range(8)]
            ev_dma = p.op("gpsimd", lambda e, src_ap=src_ap, dst_ap=dst_ap:
                          e.dma_start(out=dst_ap, in_=src_ap),
                          deps=odd_deps + [k.use(f"hmodd{pi}")],
                          dma=f"hmg{pi}", label=f"hmdma{pi}_{tt}")
            for oc in range(8):
                k.rel(f"stgodd_{oc}", ev_dma)
            for h in range(16):
                if h % 2 == 1:
                    hm_wr[(pi, h)] = ev_dma
        hm = lambda pi, h: hm_wr[(pi, h)]

        for jj in range(4):
            j = tt * 4 + jj
            ev_Sprev = k.use("S_ready")
            ev_Mprev = k.use("M_ready")
            for g in range(2):
                heads = list(range(8 * g, 8 * g + 8))
                hsl = lambda pi_, h_: slice(512 * h_ + 128 * jj,
                                            512 * h_ + 128 * jj + 128)
                ev_t1 = None
                for i_h, h in enumerate(heads):
                    qin = q_hm[0:64, hsl(0, h)]
                    kin = k_hm[0:64, hsl(1, h)]
                    ev_t1 = p.op("tensor", lambda e, qin=qin, i_h=i_h:
                                 e.transpose(pkv.h[:, 64 * i_h:64 * i_h + 64],
                                             qin, ident[0:64, 0:64]),
                                 deps=[hm(0, h), k.use(pkv.name),
                                       k.use("consts")], label=f"trQ{j}_{h}")
                    ev_t1 = p.op("tensor", lambda e, kin=kin, i_h=i_h:
                                 e.transpose(
                                     pkv.h[:, 512 + 64 * i_h:576 + 64 * i_h],
                                     kin, ident[0:64, 0:64]),
                                 deps=[hm(1, h)], label=f"trK{j}_{h}")
                ev_Qe = p.op("vector", lambda e: e.tensor_copy(
                    Qtok[:], pkv.h[:, 0:512]),
                    deps=[ev_t1, k.use("Qtok")], label=f"Qev{j}_{g}")
                ev_Ke = p.op("vector", lambda e: e.tensor_copy(
                    Ktok[:], pkv.h[:, 512:1024]),
                    deps=[ev_t1, k.use("Ktok")], label=f"Kev{j}_{g}")
                k.rel(pkv.name, [ev_Qe, ev_Ke])
                ev_t2 = None
                for i_h, h in enumerate(heads):
                    vin = v_hm[0:64, hsl(2, h)]
                    ev_t2 = p.op("tensor", lambda e, vin=vin, i_h=i_h:
                                 e.transpose(pkv.h[:, 64 * i_h:64 * i_h + 64],
                                             vin, ident[0:64, 0:64]),
                                 deps=[hm(2, h), k.use(pkv.name)],
                                 label=f"trV{j}_{h}")
                ev_Ve = p.op("vector", lambda e: e.tensor_copy(
                    Vtok[:], pkv.h[:, 0:512]),
                    deps=[ev_t2, k.use("Vtok")], label=f"Vev{j}_{g}")
                k.rel(pkv.name, ev_Ve)

                ev_pt = None
                for i_h, h in enumerate(heads):
                    bank = (pt0, pt1)[i_h // 4]
                    co = 128 * (i_h % 4)
                    kin = k_hm[0:64, hsl(1, h)]
                    qin = q_hm[0:64, hsl(0, h)]
                    ev_pt = p.op("tensor", lambda e, kin=kin, qin=qin,
                                 bank=bank, co=co: e.matmul(
                                     bank.h[:, co:co + 128], kin, qin,
                                     start=True, stop=True),
                                 deps=[hm(0, h), hm(1, h),
                                       k.use(bank.name)] if i_h % 4 == 0
                                 else [hm(0, h), hm(1, h)],
                                 label=f"PT{j}_{h}")
                ev_c0 = p.op("vector", lambda e: e.tensor_copy(
                    PT_sb[:, 0:512], pt0.h[:, 0:512]),
                    deps=[ev_pt, k.use("PT_sb")], label=f"PTe{j}0")
                ev_c1 = p.op("vector", lambda e: e.tensor_copy(
                    PT_sb[:, 512:1024], pt1.h[:, 0:512]),
                    deps=[ev_pt], label=f"PTe{j}1")
                ev_w0 = p.op("vector", lambda e: e.tensor_tensor(
                    WT_sb.ap()[:, 0:512].rearrange("p (r c) -> p r c", c=128),
                    pt0.h.ap()[:, 0:512].rearrange("p (r c) -> p r c", c=128),
                    triu.ap().unsqueeze(1).broadcast_to([128, 4, 128]),
                    ALU.mult), deps=[ev_pt, k.use("WT_sb")],
                    label=f"WTm{j}0")
                ev_w1 = p.op("vector", lambda e: e.tensor_tensor(
                    WT_sb.ap()[:, 512:1024].rearrange("p (r c) -> p r c",
                                                      c=128),
                    pt1.h.ap()[:, 0:512].rearrange("p (r c) -> p r c", c=128),
                    triu.ap().unsqueeze(1).broadcast_to([128, 4, 128]),
                    ALU.mult), deps=[ev_pt], label=f"WTm{j}1")
                k.rel(pt0.name, [ev_c0, ev_w0])
                k.rel(pt1.name, [ev_c1, ev_w1])
                ev_PT = [ev_c0, ev_c1]
                ev_WT = [ev_w0, ev_w1]
                ev_p = None
                for i_h, h in enumerate(heads):
                    bank = (pt0, pt1)[i_h // 4]
                    co = 128 * (i_h % 4)
                    kin = k_hm[0:64, hsl(1, h)]
                    qin = q_hm[0:64, hsl(0, h)]
                    ev_p = p.op("tensor", lambda e, kin=kin, qin=qin,
                                bank=bank, co=co: e.matmul(
                                    bank.h[:, co:co + 128], qin, kin,
                                    start=True, stop=True),
                                deps=[k.use(bank.name)] if i_h % 4 == 0
                                else [], label=f"P{j}_{h}")
                ev_wl0 = p.op("vector", lambda e: e.tensor_tensor(
                    W_sb.ap()[:, 0:512].rearrange("p (r c) -> p r c", c=128),
                    pt0.h.ap()[:, 0:512].rearrange("p (r c) -> p r c", c=128),
                    tril.ap().unsqueeze(1).broadcast_to([128, 4, 128]),
                    ALU.mult), deps=[ev_p, k.use("W_sb")], label=f"Wm{j}0")
                ev_wl1 = p.op("vector", lambda e: e.tensor_tensor(
                    W_sb.ap()[:, 512:1024].rearrange("p (r c) -> p r c",
                                                     c=128),
                    pt1.h.ap()[:, 0:512].rearrange("p (r c) -> p r c", c=128),
                    tril.ap().unsqueeze(1).broadcast_to([128, 4, 128]),
                    ALU.mult), deps=[ev_p], label=f"Wm{j}1")
                k.rel(pt0.name, ev_wl0)
                k.rel(pt1.name, ev_wl1)
                ev_W = [ev_wl0, ev_wl1]

                ev_C1e = None
                if j > 0:
                    ev_c = None
                    for i_h, h in enumerate(heads):
                        bank = (pt0, pt1)[i_h // 4]
                        co = 128 * (i_h % 4)
                        qin = q_hm[0:64, hsl(0, h)]
                        Sh = S_sb[0:64, 64 * h:64 * h + 64]
                        ev_c = p.op("tensor", lambda e, Sh=Sh, qin=qin,
                                    bank=bank, co=co: e.matmul(
                                        bank.h[0:64, co:co + 128], Sh, qin,
                                        start=True, stop=True),
                                    deps=[ev_Sprev, hm(0, h)] +
                                    ([k.use(bank.name)] if i_h % 4 == 0
                                     else []), label=f"C1{j}_{h}")
                    ev_C10 = p.op("vector", lambda e: e.tensor_copy(
                        C1_sb[0:64, 0:512], pt0.h[0:64, 0:512]),
                        deps=[ev_c, k.use("C1_sb")], label=f"C1e{j}0")
                    ev_C11 = p.op("vector", lambda e: e.tensor_copy(
                        C1_sb[0:64, 512:1024], pt1.h[0:64, 0:512]),
                        deps=[ev_c], label=f"C1e{j}1")
                    k.rel(pt0.name, ev_C10)
                    k.rel(pt1.name, ev_C11)
                    ev_C1e = [ev_C10, ev_C11]

                ev_at = None
                for i_h, h in enumerate(heads):
                    bank = (pt0, pt1)[i_h // 4]
                    co = 128 * (i_h % 4)
                    qin = q_hm[0:64, hsl(0, h)]
                    first = True
                    if j > 0:
                        ev_at = p.op("tensor", lambda e, qin=qin, bank=bank,
                                     co=co, i_h=i_h: e.matmul(
                                         bank.h[:, co:co + 128], qin,
                                         C1_sb[0:64, 128 * i_h:128 * i_h + 128],
                                         start=True, stop=False),
                                     deps=[ev_C1e, hm(0, h)] +
                                     ([k.use(bank.name)] if i_h % 4 == 0
                                      else []), label=f"AT1{j}_{h}")
                        first = False
                    ev_at = p.op("tensor", lambda e, bank=bank, co=co,
                                 i_h=i_h, first=first: e.matmul(
                                     bank.h[:, co:co + 128],
                                     WT_sb[:, 128 * i_h:128 * i_h + 128],
                                     PT_sb[:, 128 * i_h:128 * i_h + 128],
                                     start=first, stop=True),
                                 deps=ev_WT + ev_PT +
                                 ([k.use(bank.name)]
                                  if first and i_h % 4 == 0 else []),
                                 label=f"AT2{j}_{h}")
                ev_am0 = p.op("vector", lambda e: e.tensor_tensor(
                    ATm_sb.ap()[:, 0:512].rearrange("p (r c) -> p r c", c=128),
                    pt0.h.ap()[:, 0:512].rearrange("p (r c) -> p r c", c=128),
                    triu.ap().unsqueeze(1).broadcast_to([128, 4, 128]),
                    ALU.mult), deps=[ev_at, k.use("ATm_sb")],
                    label=f"ATm{j}0")
                ev_am1 = p.op("vector", lambda e: e.tensor_tensor(
                    ATm_sb.ap()[:, 512:1024].rearrange("p (r c) -> p r c",
                                                       c=128),
                    pt1.h.ap()[:, 0:512].rearrange("p (r c) -> p r c", c=128),
                    triu.ap().unsqueeze(1).broadcast_to([128, 4, 128]),
                    ALU.mult), deps=[ev_at], label=f"ATm{j}1")
                k.rel(pt0.name, ev_am0)
                k.rel(pt1.name, ev_am1)
                ev_ATm = [ev_am0, ev_am1]

                ev_z = None
                ev_n = None
                for i_h, h in enumerate(heads):
                    ev_z = p.op("tensor", lambda e, i_h=i_h: e.matmul(
                        pt0.h[:, 64 * i_h:64 * i_h + 64],
                        W_sb[:, 128 * i_h:128 * i_h + 128],
                        Vtok[:, 64 * i_h:64 * i_h + 64],
                        start=True, stop=True),
                        deps=ev_W + [ev_Ve] +
                        ([k.use(pt0.name)] if i_h == 0 else []),
                        label=f"Z{j}_{h}")
                    ev_n = p.op("tensor", lambda e, i_h=i_h: e.matmul(
                        pt1.h[0:64, 64 * i_h:64 * i_h + 64],
                        Qtok[:, 64 * i_h:64 * i_h + 64],
                        Vtok[:, 64 * i_h:64 * i_h + 64],
                        start=True, stop=True),
                        deps=[ev_Qe, ev_Ve] +
                        ([k.use(pt1.name)] if i_h == 0 else []),
                        label=f"N1{j}_{h}")
                ev_Ze = p.op("vector", lambda e: e.tensor_copy(
                    Z_sb[:], pt0.h[:, 0:512]),
                    deps=[ev_z, k.use("Z_sb")], label=f"Ze{j}")
                ev_Ne = p.op("vector", lambda e: e.tensor_copy(
                    N1_sb[0:64, :], pt1.h[0:64, 0:512]),
                    deps=[ev_n, k.use("N1_sb")], label=f"Ne{j}")
                k.rel(pt0.name, ev_Ze)
                k.rel(pt1.name, ev_Ne)

                ev_o = None
                for i_h, h in enumerate(heads):
                    bank = (pt0, pt1)[i_h // 4]
                    co = 128 * (i_h % 4)
                    qin = q_hm[0:64, hsl(0, h)]
                    first = True
                    if j > 0:
                        Mh = M_sb[0:64, 64 * h:64 * h + 64]
                        ev_o = p.op("tensor", lambda e, Mh=Mh, qin=qin,
                                    bank=bank, co=co: e.matmul(
                                        bank.h[0:64, co:co + 128], Mh, qin,
                                        start=True, stop=False),
                                    deps=[ev_Mprev, hm(0, h)] +
                                    ([k.use(bank.name)] if i_h % 4 == 0
                                     else []), label=f"o1_{j}_{h}")
                        first = False
                    ev_o = p.op("tensor", lambda e, bank=bank, co=co,
                                i_h=i_h, first=first: e.matmul(
                                    bank.h[0:64, co:co + 128],
                                    Vtok[:, 64 * i_h:64 * i_h + 64],
                                    ATm_sb[:, 128 * i_h:128 * i_h + 128],
                                    start=first, stop=True),
                                deps=ev_ATm + [ev_Ve] +
                                ([k.use(bank.name)]
                                 if first and i_h % 4 == 0 else []),
                                label=f"o2_{j}_{h}")
                ev_oe = None
                for i_h, h in enumerate(heads):
                    bank = (pt0, pt1)[i_h // 4]
                    co = 128 * (i_h % 4)
                    do = 512 * h + 128 * jj
                    ev_oe = p.op("vector", lambda e, bank=bank, co=co, do=do:
                                 e.tensor_copy(o_hm[0:64, do:do + 128],
                                               bank.h[0:64, co:co + 128]),
                                 deps=[ev_o, k.use("o_hm")],
                                 label=f"oev{j}_{h}")
                k.rel(pt0.name, ev_oe)
                k.rel(pt1.name, ev_oe)

                pSg = pSa if g == 0 else pSb
                for i_h, h in enumerate(heads):
                    ev_upd = p.op("tensor", lambda e, i_h=i_h, pSg=pSg:
                                  e.matmul(
                                      pSg.h[0:64, 64 * i_h:64 * i_h + 64],
                                      Ktok[:, 64 * i_h:64 * i_h + 64],
                                      Ktok[:, 64 * i_h:64 * i_h + 64],
                                      start=(i_h == 0), stop=(i_h == 7)),
                                  deps=[ev_Ke] +
                                  ([k.use(pSg.name)] if i_h == 0 else []),
                                  label=f"S{j}_{h}")
                for i_h, h in enumerate(heads):
                    first = True
                    if j > 0:
                        Sh = S_sb[0:64, 64 * h:64 * h + 64]
                        ev_updM = p.op("tensor", lambda e, Sh=Sh, h=h, i_h=i_h:
                                       e.matmul(
                                           pM.h[0:64, 64 * h:64 * h + 64],
                                           Sh,
                                           N1_sb[0:64, 64 * i_h:64 * i_h + 64],
                                           start=(i_h == 0), stop=False),
                                       deps=[ev_Sprev, ev_Ne] +
                                       ([k.use(pM.name)]
                                        if i_h == 0 and g == 0 else []),
                                       label=f"M1{j}_{h}")
                        first = False
                    ev_updM = p.op("tensor", lambda e, h=h, i_h=i_h,
                                   first=first: e.matmul(
                                       pM.h[0:64, 64 * h:64 * h + 64],
                                       Ktok[:, 64 * i_h:64 * i_h + 64],
                                       Z_sb[:, 64 * i_h:64 * i_h + 64],
                                       start=first and (i_h == 0),
                                       stop=(i_h == 7)),
                                   deps=[ev_Ke, ev_Ze] +
                                   ([k.use(pM.name)]
                                    if first and i_h == 0 and g == 0 else []),
                                   label=f"M2{j}_{h}")
                k.rel("Qtok", ev_n)
                k.rel("Ktok", ev_updM)
                k.rel("Vtok", [ev_o, ev_updM])
                k.rel("PT_sb", ev_at)
                k.rel("WT_sb", ev_at)
                k.rel("W_sb", ev_z)
                k.rel("ATm_sb", ev_o)
                k.rel("C1_sb", ev_at)
                k.rel("Z_sb", ev_updM)
                k.rel("N1_sb", ev_updM)
            ev_sa = p.op("vector", lambda e: e.tensor_add(
                S32[:, 0:512], S32[:, 0:512], pSa.h[0:64, 0:512]),
                deps=[ev_upd, k.use("S32i"), ev_Sprev], label=f"Sa{j}a")
            k.rel(pSa.name, ev_sa)
            ev_sa = p.op("vector", lambda e: e.tensor_add(
                S32[:, 512:1024], S32[:, 512:1024], pSb.h[0:64, 0:512]),
                deps=[ev_upd, ev_sa], label=f"Sa{j}b")
            k.rel(pSb.name, ev_sa)
            ev_sc = p.op("vector", lambda e: e.tensor_copy(S_sb[:], S32[:]),
                         deps=[ev_sa, ev_updM], label=f"Sc{j}")
            k.rel("S_ready", ev_sc)
            ev_ma = p.op("vector", lambda e: e.tensor_add(
                M32[:], M32[:], pM.h[0:64, :]),
                deps=[ev_updM, k.use("S32i"), ev_Mprev], label=f"Ma{j}")
            ev_mc = p.op("vector", lambda e: e.tensor_copy(M_sb[:], M32[:]),
                         deps=[ev_ma], label=f"Mc{j}")
            k.rel(pM.name, ev_ma)
            k.rel("M_ready", ev_mc)

        ev_scan = p.mark("vector")
        ohm_e = o_hm.ap().rearrange("p (hp pb t) -> p hp pb t", pb=2, t=512)
        ev_oe1 = p.op("vector", lambda e: e.tensor_copy(
            oTall.ap()[0:64, :].rearrange("p (hp t) -> p hp t", t=512),
            ohm_e[:, :, 0, :]),
            deps=[ev_scan, k.use("oTall")], label=f"oTe{tt}")
        ev_oe2 = p.op("gpsimd", lambda e: e.dma_start(
            out=oTall.ap()[64:128, :].rearrange("p (hp t) -> p hp t", t=512),
            in_=ohm_e[:, :, 1, :]),
            deps=[ev_scan, k.use("oTall")], dma="oTq", label=f"oTo{tt}")
        ev_oT = [ev_oe1, ev_oe2]
        k.rel("o_hm", ev_oT)

        def out_epi(i, oc, ps, ev):
            ev_a = p.op("scalar", lambda e, ps=ps: e.activation(
                ytmp[:], ps, AF.Identity),
                deps=[ev, k.use("ytmp")], label=f"yact{oc}")
            ev_add = p.op("vector", lambda e, oc=oc: e.tensor_add(
                x_tt[oc][:], x_tt[oc][:], ytmp[:]), deps=[ev_a],
                label=f"yadd{oc}")
            k.rel("ytmp", ev_add)
            p.op("gpsimd", lambda e, oc=oc, tsl=tsl: e.dma_start(
                out=y_h[oc, :, tsl], in_=x_tt[oc][:]), deps=[ev_add],
                dma="st", label=f"sty{oc}_{tt}")
            return ev_a

        seq_out = emit_proj(k, ws_out, seq_out, list(range(8)),
                            [oTall[:, 512 * hp:512 * hp + 512]
                             for hp in range(8)], [ev_oT] * 8,
                            [pt0, pt1], out_epi, f"out{tt}")
        k.rel("oTall", p.mark("tensor"))
        ev_store_round = Ev("st", p.cnt["st"])
        for c in range(NCH):
            k.rel(f"x{c}", ev_store_round)
            k.rel(f"xn{c}", p.mark("tensor"))
        for pi in range(3):
            k.rel(f"hmodd{pi}", p.mark("tensor"))
            for h0 in range(0, 16, 2):
                k.rel(f"hm{pi}_{h0}", p.mark("tensor"))

    if standalone:
        p.finalize(k.nc, [Ev("st", p.cnt["st"])])
        return k.nc



def emit_patt_tt(k, tag, src_tiles, src_ev, wsK, seqK, wsV, seqV, NCn,
                 E_bufs, b_sc, b_out, b_den, b_bc, drow, rb_sb, dst_fn,
                 ekey="E"):
    p = k.p
    onesb, ones1 = k.sb["c_onesb"], k.sb["c_ones1"]
    ev_den = None
    ev_exp = {}
    for ncc in range(NCn):
        colK, evK = wsK.get(seqK)
        bank = b_sc[ncc % 2]
        ev = None
        for kc in range(8):
            lhst = wsK.tile(colK, kc)
            rhst = src_tiles[kc]
            ev = p.op("tensor", lambda e, lhst=lhst, rhst=rhst, bank=bank,
                      kc=kc: e.matmul(bank.h[:, 0:512], lhst, rhst,
                                      start=(kc == 0), stop=(kc == 7)),
                      deps=[evK, src_ev] +
                      ([k.use(bank.name)] if kc == 0 else []),
                      label=f"{tag}_sc{ncc}")
        wsK.release(seqK, ev)
        seqK += 1
        ev_e = p.op("scalar", lambda e, ncc=ncc, bank=bank: e.activation(
            E_bufs[ncc][:], bank.h[:, 0:512], AF.Exp, scale=1.0 / 32.0),
            deps=[ev, k.use(f"{ekey}{ncc}")], label=f"{tag}_ex{ncc}")
        k.rel(bank.name, ev_e)
        ev_exp[ncc] = ev_e
        ev_den = p.op("tensor", lambda e, ncc=ncc: e.matmul(
            b_den.h[0:1, 0:512], onesb[:], E_bufs[ncc][:],
            start=(ncc == 0), stop=(ncc == NCn - 1)),
            deps=[ev_e] + ([k.use(b_den.name)] if ncc == 0 else []),
            label=f"{tag}_dn{ncc}")
    ev_d0 = p.op("vector", lambda e: e.tensor_copy(
        drow[0:1, 0:512], b_den.h[0:1, 0:512]),
        deps=[ev_den, k.use("drow")], label=f"{tag}_de")
    k.rel(b_den.name, ev_d0)
    ev_l = p.op("scalar", lambda e: e.activation(
        drow[0:1, 512:1024], drow[0:1, 0:512], AF.Ln), deps=[ev_d0],
        label=f"{tag}_ln")
    ev_r = p.op("scalar", lambda e: e.activation(
        drow[0:1, 1024:1536], drow[0:1, 512:1024], AF.Exp, scale=-1.0),
        deps=[ev_l], label=f"{tag}_rc")
    ev_bc = p.op("tensor", lambda e: e.matmul(
        b_bc.h[:, 0:512], ones1[:], drow[0:1, 1024:1536],
        start=True, stop=True), deps=[ev_r, k.use(b_bc.name)],
        label=f"{tag}_bc")
    ev_rb = p.op("vector", lambda e: e.tensor_copy(rb_sb[:], b_bc.h[:, 0:512]),
                 deps=[ev_bc, k.use("rb_sb")], label=f"{tag}_rbe")
    k.rel(b_bc.name, ev_rb)
    k.rel("drow", ev_rb)
    ev_eall = [ev_exp[ncc] for ncc in range(NCn)]
    for oc in range(8):
        colV, evV = wsV.get(seqV)
        bank = b_out[oc % 2]
        ev = None
        for ncc in range(NCn):
            lhst = wsV.tile(colV, ncc)
            rhst = E_bufs[ncc][:]
            ev = p.op("tensor", lambda e, lhst=lhst, rhst=rhst, bank=bank,
                      ncc=ncc: e.matmul(bank.h[:, 0:512], lhst, rhst,
                                        start=(ncc == 0),
                                        stop=(ncc == NCn - 1)),
                      deps=[evV] + (ev_eall + [k.use(bank.name)]
                                    if ncc == 0 else []),
                      label=f"{tag}_o{oc}")
        wsV.release(seqV, ev)
        seqV += 1
        ev_done = dst_fn(oc, bank.h[:, 0:512], rb_sb, [ev, ev_rb])
        k.rel(bank.name, ev_done)
    ev_t = Ev("tensor", p.cnt["tensor"])
    for ncc in range(NCn):
        k.rel(f"{ekey}{ncc}", ev_t)
    k.rel("rb_sb", ev_t)
    return seqK, seqV



def build_l3a(k=None, x_h=None, xin_h=None, qs_h=None, ks_h=None,
              vs_h=None):
    standalone = k is None
    if standalone:
        k = K()
    p = k.p
    if x_h is None:
        x_h = k.dram_in("x", [D, T]).rearrange("(c p) t -> c p t", p=128)
        xin_h = k.dram_out("xin", [D, T], BF16).rearrange(
            "(c p) t -> c p t", p=128)
        qs_h = k.dram_out("q_scr", [16, 64, T], BF16)
        ks_h = k.dram_out("k_scr", [16, 64, T], BF16)
        vs_h = k.dram_out("v_scr", [16, 64, T], BF16)
    wK_h = k.dram_in("l3_K3", [24, 128, 1024])
    wV_h = k.dram_in("l3_V3", [24, 128, 1024])
    k.load_consts(("c_ones1", "c_onesb"))

    x_tt = [k.sbuf(f"x{c}", [128, 512], F32) for c in range(NCH)]
    xin_tt = [k.sbuf(f"xi{c}", [128, 512], BF16) for c in range(NCH)]
    h_sb = [k.sbuf(f"h{c}", [128, T], BF16) for c in range(NCH)]
    E_bufs = [k.sbuf(f"E{i}", [128, 512], BF16) for i in range(8)]
    drow = k.sbuf("drow", [1, 1536], F32)
    rb_sb = k.sbuf("rb_sb", [128, 512], F32)
    pair = k.sbuf("pair", [128, 512], BF16)

    pt0, pt1, psp = k.bk("A0"), k.bk("A1"), k.bk("A2")
    po0, po1 = k.bk("A2"), k.bk("A3")
    pden = k.bk("PD")
    pbc = k.bk("A0")

    for tt in range(TT):
        tsl = slice(tt * 512, (tt + 1) * 512)
        x_ev = []
        for c in range(NCH):
            x_ev.append(p.op("sync", lambda e, c=c, tsl=tsl: e.dma_start(
                out=x_tt[c][:], in_=x_h[c, :, tsl]),
                deps=[k.use(f"x{c}")], dma="xld", label=f"ldx{c}_{tt}"))
        x_ev = [x_ev[-1]] * NCH
        xin_ev = emit_ln(k, [t[:] for t in x_tt], x_ev,
                         [t[:] for t in xin_tt], pt0, pt1, psp, f"lno{tt}",
                         wkey="xi")
        for c in range(NCH):
            ev_st = p.op("gpsimd", lambda e, c=c, tsl=tsl: e.dma_start(
                out=xin_h[c, :, tsl], in_=xin_tt[c][:]), deps=[xin_ev[c]],
                dma="st", label=f"sxin{c}_{tt}")
            k.rel(f"x{c}", xin_ev[c])
        h_ev = emit_ln(k, [t[:] for t in xin_tt], xin_ev,
                       [t[:, tsl] for t in h_sb], pt0, pt1, psp, f"lni{tt}",
                       wkey="hsb")
        ev_all = Ev("st", p.cnt["st"])
        for c in range(NCH):
            k.rel(f"xi{c}", [h_ev[c], ev_all])
    ev_h = p.mark("vector")

    wsK = WCols(k, "wK", wK_h, [pi * 8 + nc for pi in range(3)
                                for _ in range(TT) for nc in range(8)],
                1024, "sync")
    wsV = WCols(k, "wV", wV_h, [pi * 8 + oc for pi in range(3)
                                for _ in range(TT) for oc in range(8)],
                1024, "gpsimd")
    seqK = seqV = 0
    for pi, scr in enumerate((qs_h, ks_h, vs_h)):
        for tt in range(TT):
            tsl = slice(tt * 512, (tt + 1) * 512)

            def dst_fn(oc, ps, rb, evs, scr=scr, tsl=tsl):
                ev_m = p.op("vector", lambda e, ps=ps: e.tensor_mul(
                    pair[:], ps, rb[:]), deps=evs + [k.use("pair")],
                    label="pdiv")
                for half in range(2):
                    p.op("gpsimd", lambda e, oc=oc, half=half, scr=scr,
                         tsl=tsl: e.dma_start(
                             out=scr[2 * oc + half, :, tsl],
                             in_=pair[64 * half:64 * half + 64, :]),
                         deps=[ev_m], dma="st", label="pst")
                ev_s = Ev("st", p.cnt["st"])
                k.rel("pair", ev_s)
                return ev_s

            seqK, seqV = emit_patt_tt(
                k, f"p{pi}_{tt}", [t[:, tsl] for t in h_sb], ev_h,
                wsK, seqK, wsV, seqV, 8, E_bufs, (pt0, pt1), (po0, po1),
                pden, pbc, drow, rb_sb, dst_fn)

    if standalone:
        p.finalize(k.nc, [Ev("st", p.cnt["st"])])
        return k.nc



def build_l3b(k=None, qs_h=None, ks_h=None, vs_h=None, os_h=None):
    standalone = k is None
    if standalone:
        k = K()
    p = k.p
    if qs_h is None:
        qs_h = k.dram_in("q_scr", [16, 64, T], BF16)
        ks_h = k.dram_in("k_scr", [16, 64, T], BF16)
        vs_h = k.dram_in("v_scr", [16, 64, T], BF16)
        os_h = k.dram_out("o_scr", [8, 128, T], BF16)
    k.load_consts(("c_ones1", "c_onesb", "c_identb", "c_sdpmask", "c_sel8"))

    q_hm = k.sbuf("q_hm", [64, 4 * T], BF16)
    k_hm = k.sbuf("k_hm", [64, 4 * T], BF16)
    v_hm = k.sbuf("v_hm", [64, 4 * T], BF16)
    V_tok = [k.sbuf(f"V{tc}", [128, 4 * 65], BF16) for tc in range(16)]
    o_hm = k.sbuf("o_hm", [64, 4 * T], BF16)
    oTall = k.sbuf("oTall", [128, 2 * T], BF16)
    E_sb = [k.sbuf(f"E{i}", [128, 512], BF16) for i in range(4)]
    sel8 = k.sb["c_sel8"]
    dtmp = k.sbuf("dtmp", [65, 512], F32)
    rd16 = k.sbuf("rd16", [16, T], F32)

    pt0, pt1, pt2 = k.bk("A0"), k.bk("A1"), k.bk("A3")
    pav = k.bk("W0")
    pkv = k.bk("P3")
    pbc = k.bk("A2")

    ev_c2 = k.use("consts")
    ident = k.sb["c_identb"]
    sdpm = k.sb["c_sdpmask"]

    for hg in range(4):
        hev = {}
        for nm, hm, scr in (("q", q_hm, qs_h), ("k", k_hm, ks_h),
                            ("v", v_hm, vs_h)):
            dst = hm.ap().rearrange("p (h t) -> p h t", t=T)
            src = scr[4 * hg:4 * hg + 4].rearrange("h p t -> p h t")
            hev[nm] = p.op("sync", lambda e, dst=dst, src=src: e.dma_start(
                out=dst, in_=src), deps=[k.use(f"hm_{nm}")],
                dma=f"hq{nm}", label=f"ld{nm}{hg}")
        ev_tr_all = []
        for tc in range(16):
            ev_vs = p.op("vector", lambda e, tc=tc: e.memset(
                V_tok[tc].ap().rearrange("p (h s) -> p h s", s=65)[:, :, 64:65],
                1.0), deps=[k.use(f"Vt{tc}")], label=f"vone{tc}_{hg}")
            ev_tr = None
            for hi in range(4):
                vin = v_hm[0:64, hi * T + tc * 128: hi * T + tc * 128 + 128]
                ev_tr = p.op("tensor", lambda e, vin=vin, hi=hi: e.transpose(
                    pkv.h[:, 64 * hi:64 * hi + 64], vin, ident[0:64, 0:64]),
                    deps=[hev["v"], k.use(pkv.name), k.use("consts")],
                    label=f"vtr{tc}_{hi}")
            dst = V_tok[tc].ap().rearrange("p (h s) -> p h s", s=65)[:, :, 0:64]
            src = pkv.h.ap()[:, 0:256].rearrange("p (h s) -> p h s", s=64)
            ev_ve = p.op("vector", lambda e, dst=dst, src=src: e.tensor_copy(
                dst, src), deps=[ev_tr, ev_vs], label=f"vev{tc}_{hg}")
            k.rel(pkv.name, ev_ve)
            ev_tr_all.append(ev_ve)
        ev_V = p.mark("vector")

        for hi in range(4):
            h = 4 * hg + hi
            for qt in range(TT):
                qsl = slice(qt * 512, qt * 512 + 512)
                nkc = 4 * qt + 4
                ev_av = None
                LAG = 2
                pend = []

                def emit_av(kq, evq):
                    lhsv = V_tok[kq][:, 65 * hi:65 * hi + 65]
                    ev = p.op("tensor", lambda e, lhsv=lhsv, eb=E_sb[kq % 4],
                              kq=kq, nkc=nkc: e.matmul(
                                  pav.h[:, 0:512], lhsv, eb[:],
                                  start=(kq == 0), stop=(kq == nkc - 1)),
                              deps=[evq, ev_V] +
                              ([k.use(pav.name)] if kq == 0 else []),
                              label=f"AV{h}_{qt}_{kq}")
                    k.rel(f"E{kq % 4}", ev)
                    return ev

                for kc in range(nkc):
                    bank = (pt0, pt1, pt2)[kc % 3]
                    lhsp = k_hm[0:64, hi * T + kc * 128: hi * T + kc * 128 + 128]
                    rhsp = q_hm[0:64, hi * T:hi * T + T][:, qsl]
                    ev_pt = p.op("tensor", lambda e, lhsp=lhsp, rhsp=rhsp,
                                 bank=bank: e.matmul(
                                     bank.h[:, 0:512], lhsp, rhsp,
                                     start=True, stop=True),
                                 deps=[hev["k"], hev["q"], k.use(bank.name)],
                                 label=f"PT{h}_{qt}_{kc}")
                    eb = E_sb[kc % 4]
                    ev_exp = p.op("scalar", lambda e, eb=eb, bank=bank:
                                  e.activation(eb[:], bank.h[:, 0:512],
                                               AF.Exp, scale=0.125),
                                  deps=[ev_pt, k.use(f"E{kc % 4}")],
                                  label=f"ex{h}_{qt}_{kc}")
                    k.rel(bank.name, ev_exp)
                    if kc >= 4 * qt:
                        r_ = kc - 4 * qt
                        ev_exp = p.op("vector", lambda e, eb=eb, r_=r_:
                                      e.tensor_mul(eb[:], eb[:],
                                                   sdpm[:, 512 * r_:
                                                        512 * r_ + 512]),
                                      deps=[ev_exp, k.use("consts")],
                                      label=f"msk{h}_{qt}_{kc}")
                    pend.append((kc, ev_exp))
                    if len(pend) > LAG:
                        ev_av = emit_av(*pend.pop(0))
                for kq, evq in pend:
                    ev_av = emit_av(kq, evq)
                ev_oe = p.op("vector", lambda e, hi=hi, qsl=qsl: e.tensor_copy(
                    o_hm[0:64, hi * T:hi * T + T][:, qsl], pav.h[0:64, 0:512]),
                    deps=[ev_av, k.use("o_hm")], label=f"oev{h}_{qt}")
                ev_de = p.op("vector", lambda e: e.tensor_copy(
                    dtmp[64:65, 0:512], pav.h[64:65, 0:512]),
                    deps=[ev_av, k.use("dtmp")], label=f"dev{h}_{qt}")
                k.rel(pav.name, [ev_oe, ev_de])
                ev_dd = p.op("gpsimd", lambda e, hi=hi, qsl=qsl: e.dma_start(
                    out=rd16[hi:hi + 1, qsl], in_=dtmp[64:65, 0:512]),
                    deps=[ev_de, k.use("rd16")], dma="dq",
                    label=f"dmv{h}_{qt}")
                k.rel("dtmp", ev_dd)
                k.rel("rd_part", ev_dd)
        ev_rd = p.op("vector", lambda e: e.reciprocal(
            rd16[0:4, :], rd16[0:4, :]), deps=[k.use("rd_part")],
            label=f"rcp{hg}")

        ev_scan = p.mark("vector")
        ohm_e = o_hm.ap().rearrange("p (hp pb t) -> p hp pb t", pb=2, t=T)
        ev_oe1 = p.op("vector", lambda e: e.tensor_copy(
            oTall.ap()[0:64, :].rearrange("p (hp t) -> p hp t", t=T),
            ohm_e[:, :, 0, :]),
            deps=[ev_scan, k.use("oTall")], label=f"oTe{hg}")
        ev_oe2 = p.op("gpsimd", lambda e: e.dma_start(
            out=oTall.ap()[64:128, :].rearrange("p (hp t) -> p hp t", t=T),
            in_=ohm_e[:, :, 1, :]),
            deps=[ev_scan, k.use("oTall")], dma="oTq", label=f"oTo{hg}")
        ev_oT = [ev_oe1, ev_oe2]
        k.rel("o_hm", ev_oT)

        for hpl in range(2):
            c = 2 * hg + hpl
            last = None
            for qt in range(TT):
                qsl = slice(qt * 512, qt * 512 + 512)
                ev_bc = p.op("tensor", lambda e, hpl=hpl, qsl=qsl: e.matmul(
                    pbc.h[:, 0:512], sel8[0:4, 128 * hpl:128 * hpl + 128],
                    rd16[0:4, qsl], start=True, stop=True),
                    deps=[ev_rd, ev_c2, k.use(pbc.name)],
                    label=f"bc{hg}_{hpl}_{qt}")
                ev_t = p.op("vector", lambda e, hpl=hpl, qsl=qsl: e.tensor_mul(
                    oTall[:, hpl * T:hpl * T + T][:, qsl],
                    oTall[:, hpl * T:hpl * T + T][:, qsl],
                    pbc.h[:, 0:512]),
                    deps=[ev_bc, ev_oT], label=f"dv{hg}_{hpl}_{qt}")
                k.rel(pbc.name, ev_t)
                last = ev_t
            p.op("gpsimd", lambda e, c=c, hpl=hpl: e.dma_start(
                out=os_h[c], in_=oTall[:, hpl * T:hpl * T + T]),
                deps=[last], dma="st", label=f"sto{c}")
        ev_done = Ev("st", p.cnt["st"])
        k.rel("oTall", ev_done)
        k.rel("rd16", p.mark("tensor"))
        k.rel("rd_part", None)
        for nm in ("q", "k", "v"):
            k.rel(f"hm_{nm}", p.mark("tensor"))
        for tc in range(16):
            k.rel(f"Vt{tc}", p.mark("tensor"))

    if standalone:
        p.finalize(k.nc, [Ev("st", p.cnt["st"])])
        return k.nc



def build_l3c(k=None, x_h=None, xin_h=None, os_h=None, y_h=None):
    standalone = k is None
    if standalone:
        k = K()
    p = k.p
    if x_h is None:
        x_h = k.dram_in("x", [D, T]).rearrange("(c p) t -> c p t", p=128)
        xin_h = k.dram_in("xin", [D, T], BF16).rearrange(
            "(c p) t -> c p t", p=128)
        os_h = k.dram_in("o_scr", [8, 128, T], BF16)
        y_h = k.dram_out("y", [D, T]).rearrange("(c p) t -> c p t", p=128)
    wpK_h = k.dram_in("l3_pK", [8, 128, 1024])
    wpV_h = k.dram_in("l3_pV", [8, 128, 1024])
    wfK_h = k.dram_in("l3_fK", [32, 128, 1024])
    wfV_h = k.dram_in("l3_fV", [8, 128, 4096])
    k.load_consts(("c_ones1", "c_onesb"))

    o_tt = [k.sbuf(f"o{c}", [128, 512], BF16) for c in range(NCH)]
    xi_tt = [k.sbuf(f"xi{c}", [128, 512], BF16) for c in range(NCH)]
    s_tt = [k.sbuf(f"s{c}", [128, 512], F32) for c in range(NCH)]
    h2_tt = [k.sbuf(f"h2{c}", [128, 512], BF16) for c in range(NCH)]
    x_tile = k.sbuf("x_tile", [128, 512], F32)
    tmp_f = k.sbuf("tmp_f", [128, 512], F32)
    E_p = [k.sbuf(f"Ep{i}", [128, 512], BF16) for i in range(8)]
    E_f = [k.sbuf(f"Ef{i}", [128, 512], BF16) for i in range(32)]
    drow = k.sbuf("drow", [1, 1536], F32)
    rb_sb = k.sbuf("rb_sb", [128, 512], F32)

    pt0, pt1, psp = k.bk("A0"), k.bk("A1"), k.bk("A2")
    po0, po1 = k.bk("A2"), k.bk("A3")
    pden = k.bk("PD")
    pbc = k.bk("A0")

    wspK = WCols(k, "wpK", wpK_h, [nc for _ in range(TT) for nc in range(8)],
                 1024, "sync")
    wspV = WCols(k, "wpV", wpV_h, [oc for _ in range(TT) for oc in range(8)],
                 1024, "gpsimd")
    wsfK = WCols(k, "wfK", wfK_h, [nc for _ in range(TT) for nc in range(32)],
                 1024, "sync")
    wsfV = WCols(k, "wfV", wfV_h, [oc for _ in range(TT) for oc in range(8)],
                 4096, "gpsimd", stg_bufs=1, col_bufs=1)
    spK = spV = sfK = sfV = 0

    for tt in range(TT):
        tsl = slice(tt * 512, (tt + 1) * 512)
        o_ev = []
        xi_ev = []
        for c in range(NCH):
            o_ev.append(p.op("sync", lambda e, c=c, tsl=tsl: e.dma_start(
                out=o_tt[c][:], in_=os_h[c, :, tsl]),
                deps=[k.use(f"o{c}")], dma="old", label=f"ldo{c}_{tt}"))
            xi_ev.append(p.op("sync", lambda e, c=c, tsl=tsl: e.dma_start(
                out=xi_tt[c][:], in_=xin_h[c, :, tsl]),
                deps=[k.use(f"xi{c}")], dma="old", label=f"ldxi{c}_{tt}"))
        ev_ld = Ev("old", p.cnt["old"])

        def dst_p(oc, ps, rb, evs):
            ev_m = p.op("vector", lambda e, ps=ps: e.tensor_mul(
                tmp_f[:], ps, rb[:]), deps=evs + [k.use("tmp_f")],
                label="pm")
            ev_s = p.op("vector", lambda e, oc=oc: e.tensor_add(
                s_tt[oc][:], tmp_f[:], xi_tt[oc][:]),
                deps=[ev_m, ev_ld, k.use(f"s{oc}")], label="ps")
            k.rel("tmp_f", ev_s)
            k.rel(f"o{oc}", ev_s)
            k.rel(f"s{oc}", ev_s)
            return ev_s

        spK, spV = emit_patt_tt(
            k, f"pp{tt}", [t[:] for t in o_tt], ev_ld, wspK, spK, wspV, spV,
            8, E_p, (pt0, pt1), (po0, po1), pden, pbc, drow, rb_sb, dst_p,
            ekey="Ep")
        s_ev = [k.use(f"s{c}") for c in range(NCH)]

        h2_ev = emit_ln(k, [t[:] for t in s_tt], s_ev,
                        [t[:] for t in h2_tt], pt0, pt1, psp, f"ln2{tt}",
                        wkey="h2w")
        for c in range(NCH):
            k.rel(f"h2w{c}", h2_ev[c])

        def dst_f(oc, ps, rb, evs):
            ev_m = p.op("vector", lambda e, ps=ps: e.tensor_mul(
                tmp_f[:], ps, rb[:]), deps=evs + [k.use("tmp_f")],
                label="fm")
            ev_x = p.op("sync", lambda e, oc=oc, tsl=tsl: e.dma_start(
                out=x_tile[:], in_=x_h[oc, :, tsl]),
                deps=[k.use("x_tile")], dma="xr", label=f"ldx{oc}_{tt}")
            ev_a1 = p.op("vector", lambda e, oc=oc: e.tensor_add(
                x_tile[:], x_tile[:], s_tt[oc][:]),
                deps=[ev_x, k.use(f"s{oc}")], label="fa1")
            ev_a2 = p.op("vector", lambda e: e.tensor_add(
                x_tile[:], x_tile[:], tmp_f[:]), deps=[ev_a1, ev_m],
                label="fa2")
            k.rel("tmp_f", ev_a2)
            k.rel(f"s{oc}", ev_a2)
            p.op("gpsimd", lambda e, oc=oc, tsl=tsl: e.dma_start(
                out=y_h[oc, :, tsl], in_=x_tile[:]), deps=[ev_a2],
                dma="st", label=f"sty{oc}_{tt}")
            k.rel("x_tile", Ev("st", p.cnt["st"]))
            return ev_a2

        sfK, sfV = emit_patt_tt(
            k, f"pf{tt}", [t[:] for t in h2_tt],
            [h2_ev, k.use("h2all")], wsfK, sfK, wsfV, sfV,
            32, E_f, (pt0, pt1), (po0, po1), pden, pbc, drow, rb_sb, dst_f,
            ekey="Ef")
        ev_t = Ev("tensor", p.cnt["tensor"])
        for c in range(NCH):
            k.rel(f"xi{c}", ev_t)
            k.rel(f"h2w{c}", ev_t)
        k.rel("h2all", ev_t)

    if standalone:
        p.finalize(k.nc, [Ev("st", p.cnt["st"])])
        return k.nc



ALL_CONSTS = ("c_ones1", "c_onesb", "c_identb", "c_identf", "c_triu",
              "c_tril", "c_sdpmask", "c_palace", "c_sel8", "c_sel16")


def build_fused(upto=None):
    k = K(fused=True)
    p = k.p
    k.load_consts(ALL_CONSTS)
    x_ext = k.dram_in("x", [D, T])
    y_ext = k.dram_out("y", [D, T])
    xi = k.dram_tmp("x_int", [D, T])
    xin_s = k.dram_tmp("xin_s", [D, T], BF16)
    q_s = k.dram_tmp("q_s", [16, 64, T], BF16)
    k_s = k.dram_tmp("k_s", [16, 64, T], BF16)
    v_s = k.dram_tmp("v_s", [16, 64, T], BF16)
    o_s = k.dram_tmp("o_s", [8, 128, T], BF16)
    r = lambda ap: ap.rearrange("(c p) t -> c p t", p=128)

    seq = [
        ("L0A_", build_l0_attn, dict(x_h=r(x_ext), y_h=r(xi))),
        ("L0F_", build_l0_ffn, dict(x_h=r(xi), y_h=r(xi))),
        ("L1A_", build_l1_attn, dict(x_h=r(xi), y_h=r(xi))),
        ("L1F_", build_l1_ffn, dict(x_h=r(xi), y_h=r(xi))),
        ("L2A_", build_l2_attn, dict(x_h=r(xi), y_h=r(xi))),
        ("L2F_", build_l2_ffn, dict(x_h=r(xi), y_h=r(xi))),
        ("L3A_", build_l3a, dict(x_h=r(xi), xin_h=r(xin_s), qs_h=q_s,
                                 ks_h=k_s, vs_h=v_s)),
        ("L3B_", build_l3b, dict(qs_h=q_s, ks_h=k_s, vs_h=v_s, os_h=o_s)),
        ("L3C_", build_l3c, dict(x_h=r(xi), xin_h=r(xin_s), os_h=o_s,
                                 y_h=r(xi))),
        ("L3F_", build_l3_ffn, dict(x_h=r(xi), y_h=r(y_ext))),
    ]
    if upto is not None:
        seq = seq[:upto]
    for pfx, fn, io in seq:
        k.new_block(pfx)
        fn(k=k, **io)
    if upto is not None and seq[-1][2].get("y_h") is not r(y_ext):
        k.barrier()
        src = xi if seq[-1][0] != "L3A_" else xi
        p.op("gpsimd", lambda e: e.dma_start(out=y_ext, in_=src),
             dma="st", label="dbgcopy")
    p.finalize(k.nc, [Ev("st", p.cnt["st"])])
    return k.nc



def _host_prep(inputs):
    pr = {}
    f = np.float32
    pr["c_ones1"] = np.ones((1, 128), f)
    pr["c_onesb"] = np.ones((128, 1), f)
    pr["c_identb"] = np.eye(128, dtype=f)
    i_ = np.arange(128)[:, None]
    j_ = np.arange(128)[None, :]
    pr["c_tril"] = (j_ <= i_).astype(f)
    pr["c_triu"] = (j_ >= i_).astype(f)
    jj = np.arange(512)[None, :]
    sdp = np.zeros((128, 4, 512), f)
    for r_ in range(4):
        sdp[:, r_, :] = (jj >= (i_ + 128 * r_)).astype(f)
    pr["c_sdpmask"] = sdp.reshape(128, 2048)
    jj64 = np.arange(64)[None, :]
    pr["c_palace"] = (((i_ % 64) // 8) == (jj64 // 8)).astype(f)

    g = lambda n: np.asarray(inputs[n], f)
    pr["l1_wqkv"] = colmajor(g("L1_attn_Wqkv_w").T)
    pr["l1_bqkv"] = pp_bias(g("L1_attn_Wqkv_b"), 24)
    pr["l1_wout"] = colmajor(g("L1_attn_out_w").T)
    pr["l1_bout"] = pp_bias(g("L1_attn_out_b"), 8)

    idx = np.arange(D, dtype=np.int64)
    m_ = np.outer(idx, idx) % D
    ang = (2.0 * np.pi / D) * m_
    FR = np.cos(ang).astype(f)
    FI = (-np.sin(ang)).astype(f)
    pr["c_fr"] = rowmajor(FR)
    pr["c_fi"] = rowmajor(FI)
    pr["c_fin"] = rowmajor(-FI)
    pr["l1f_wr"] = colmajor(g("L1_ffn_w_freq_real"))
    pr["l1f_wi"] = colmajor(g("L1_ffn_w_freq_imag"))
    pr["l1f_bias"] = pp_bias(g("L1_ffn_bias"), 8)

    pr["l2_wqk"] = np.concatenate(
        [colmajor(g("L2_attn_Wq").T), colmajor(g("L2_attn_Wk").T)], axis=0)
    pr["l2_wv"] = rowmajor(g("L2_attn_Wv").T)
    pr["l2_ipw"] = np.asarray(inputs["L2_attn_inter_palace_weight"],
                              f).reshape(1, 1)
    pr["c_identf"] = np.eye(128, dtype=f)
    sel16 = np.zeros((16, 2048), ml_dtypes.bfloat16)
    for e in range(16):
        sel16[e, 128 * e:128 * e + 128] = 1.0
    pr["c_sel16"] = sel16
    sel8 = np.zeros((8, 512), f)
    for hpl in range(4):
        for m in range(128):
            sel8[2 * hpl + m // 64, 128 * hpl + m] = 1.0
    pr["c_sel8"] = sel8

    def moe_prep(pfx, wr, w1, b1, w2, b2, E, F):
        OC1 = F // 128
        pr[f"{pfx}_wr"] = np.ascontiguousarray(
            rowmajor(wr).transpose(1, 0, 2).reshape(128, 8 * E))
        pr[f"{pfx}_w1"] = np.concatenate(
            [colmajor(w1[e]) for e in range(E)], axis=0)
        pr[f"{pfx}_b1"] = np.concatenate(
            [pp_bias(b1[e], OC1) for e in range(E)], axis=1)
        NPARTS = F // 1024
        w2c = np.stack([colmajor(w2[e]) for e in range(E)])
        pr[f"{pfx}_w2"] = np.ascontiguousarray(
            w2c.reshape(E, 8, 128, NPARTS, 1024).transpose(0, 3, 1, 2, 4)
            .reshape(E * NPARTS * 8, 128, 1024))
        pr[f"{pfx}_b2"] = np.concatenate(
            [pp_bias(b2[e], 8) for e in range(E)], axis=1)

    moe_prep("l0f", g("L0_ffn_Wg").T,
             np.transpose(g("L0_ffn_W1"), (0, 2, 1)), g("L0_ffn_b1"),
             np.transpose(g("L0_ffn_W2"), (0, 2, 1)), g("L0_ffn_b2"),
             8, 4096)
    moe_prep("l2f", g("L2_ffn_Wr").T, g("L2_ffn_W1"), g("L2_ffn_b1"),
             g("L2_ffn_W2"), g("L2_ffn_b2"), 8, 4096)
    moe_prep("l3f", g("L3_ffn_router_w"), g("L3_ffn_w1"), g("L3_ffn_b1"),
             g("L3_ffn_w2"), g("L3_ffn_b2"), 14, 2048)

    pr["l0_wqkv"] = np.concatenate([colmajor(g("L0_attn_W_q")),
                                    colmajor(g("L0_attn_W_k")),
                                    colmajor(g("L0_attn_W_v"))], axis=0)
    pr["l0_wo"] = colmajor(g("L0_attn_W_o"))

    pr["l3_K3"] = np.concatenate([colmajor(g("L3_attn_qK").T),
                                  colmajor(g("L3_attn_kK").T),
                                  colmajor(g("L3_attn_vK").T)], axis=0)
    pr["l3_V3"] = np.concatenate([colmajor(g("L3_attn_qV")),
                                  colmajor(g("L3_attn_kV")),
                                  colmajor(g("L3_attn_vV"))], axis=0)
    pr["l3_pK"] = colmajor(g("L3_attn_pK").T)
    pr["l3_pV"] = colmajor(g("L3_attn_pV"))
    pr["l3_fK"] = colmajor(g("L3_attn_fK").T)
    pr["l3_fV"] = colmajor(g("L3_attn_fV"))
    return pr


_BUILD_CACHE = {}


LAUNCHES = {
    "l1_attn": (build_l1_attn,
                ["c_ones1", "c_onesb", "c_identb", "c_triu",
                 "l1_wqkv", "l1_bqkv", "l1_wout", "l1_bout"]),
    "l1_ffn": (build_l1_ffn,
               ["c_ones1", "c_onesb", "c_fr", "c_fi", "c_fin",
                "l1f_wr", "l1f_wi", "l1f_bias"]),
    "l0_ffn": (build_l0_ffn,
               ["c_ones1", "c_onesb", "c_identf", "c_sel16",
                "l0f_wr", "l0f_w1", "l0f_b1", "l0f_w2", "l0f_b2"]),
    "l2_ffn": (build_l2_ffn,
               ["c_ones1", "c_onesb", "c_identf", "c_sel16",
                "l2f_wr", "l2f_w1", "l2f_b1", "l2f_w2", "l2f_b2"]),
    "l3_ffn": (build_l3_ffn,
               ["c_ones1", "c_onesb", "c_identf", "c_sel16",
                "l3f_wr", "l3f_w1", "l3f_b1", "l3f_w2", "l3f_b2"]),
    "l0_attn": (build_l0_attn,
                ["c_ones1", "c_onesb", "c_identb", "c_triu", "c_tril",
                 "l0_wqkv", "l0_wo"]),
    "l3a": (build_l3a, ["c_ones1", "c_onesb", "l3_K3", "l3_V3"],
            [("x", "x")],
            [("xin", "xin"), ("q", "q_scr"), ("k", "k_scr"), ("v", "v_scr")]),
    "l3b": (build_l3b,
            ["c_ones1", "c_onesb", "c_identb", "c_sdpmask", "c_sel8"],
            [("q", "q_scr"), ("k", "k_scr"), ("v", "v_scr")],
            [("o", "o_scr")]),
    "l3c": (build_l3c,
            ["c_ones1", "c_onesb", "l3_pK", "l3_pV", "l3_fK", "l3_fV"],
            [("x", "x"), ("xin", "xin"), ("o", "o_scr")],
            [("x", "y")]),
    "l2_attn": (build_l2_attn,
                ["c_ones1", "c_onesb", "c_palace", "c_sel8",
                 "l2_wqk", "l2_wv", "l2_ipw"]),
}
LAUNCH_ORDER = ["l0_attn", "l0_ffn", "l1_attn", "l1_ffn", "l2_attn", "l2_ffn", "l3a", "l3b", "l3c", "l3_ffn"]


def run_launch(name, xc, pr):
    state = [{"x": xc[b]} for b in range(len(xc))]
    run_launch_state(name, state, pr)
    return [st["x"] for st in state]


def run_launch_state(name, state, pr):
    ent = LAUNCHES[name]
    builder, keys = ent[0], ent[1]
    sin = ent[2] if len(ent) > 2 else [("x", "x")]
    sout = ent[3] if len(ent) > 3 else [("x", "y")]
    if name not in _BUILD_CACHE:
        _BUILD_CACHE[name] = builder()
    nc = _BUILD_CACHE[name]
    ncore = len(state)
    in_maps = []
    for b in range(ncore):
        m = {}
        for skey, hname in sin:
            m[hname] = state[b][skey]
        for key in keys:
            m[key] = pr[key]
        in_maps.append(m)
    res = run_bass_kernel_spmd(nc, in_maps, core_ids=list(range(ncore)))
    for b in range(ncore):
        for skey, hname in sout:
            state[b][skey] = np.asarray(res.results[b][hname])


FUSED = True


def kernel(**inputs):
    pr = _host_prep(inputs)
    x = np.asarray(inputs["L0_attn_x"], np.float32)
    if not FUSED:
        state = [{"x": np.ascontiguousarray(x[b].T)} for b in range(NCORES)]
        for name in LAUNCH_ORDER:
            run_launch_state(name, state, pr)
        return np.stack([st["x"].T for st in state]).astype(np.float32)
    if "fused" not in _BUILD_CACHE:
        _BUILD_CACHE["fused"] = build_fused()
    nc = _BUILD_CACHE["fused"]
    wkeys = set()
    for ent in LAUNCHES.values():
        wkeys.update(ent[1])
    wkeys.update(ALL_CONSTS)
    in_maps = []
    for b in range(NCORES):
        m = {"x": np.ascontiguousarray(x[b].T)}
        for key in wkeys:
            m[key] = pr[key]
        in_maps.append(m)
    res = run_bass_kernel_spmd(nc, in_maps, core_ids=list(range(NCORES)))
    return np.stack([np.asarray(r["y"]).T
                     for r in res.results]).astype(np.float32)


# version 07 (final)
# speedup vs baseline: 1.4899x; 1.0477x over previous
import numpy as np
import ml_dtypes
import concourse.bass as bass
import concourse.mybir as mybir
from concourse.bass_utils import run_bass_kernel_spmd

F32 = mybir.dt.float32
BF16 = mybir.dt.bfloat16
AF = mybir.ActivationFunctionType
ALU = mybir.AluOpType
AX = mybir.AxisListType.X

T = 2048
D = 1024
NCH = 8
TT = 4
NCORES = 8

ENGINES = ("tensor", "vector", "scalar", "gpsimd", "sync")
DQ = {e: e + "_q" for e in ENGINES}


class Ev:
    __slots__ = ("sem", "val")

    def __init__(self, sem, val):
        self.sem = sem
        self.val = val


class Prog:
    def __init__(self):
        self.ops = {e: [] for e in ENGINES}
        self.cnt = {}
        self.waited = {e: {} for e in ENGINES}

    def op(self, eng, emit, deps=(), sig=True, dma=False, label=""):
        byname = {}

        def add(ev):
            if ev is None:
                return
            if isinstance(ev, (list, tuple)):
                for e2 in ev:
                    add(e2)
            elif ev.val > byname.get(ev.sem, 0):
                byname[ev.sem] = ev.val
        for ev in deps:
            add(ev)
        waits = []
        for s, v in sorted(byname.items()):
            if v > self.waited[eng].get(s, 0):
                waits.append((s, v))
                self.waited[eng][s] = v
        ev = None
        if sig:
            sem = dma if dma else eng
            self.cnt[sem] = self.cnt.get(sem, 0) + (16 if dma else 1)
            ev = Ev(sem, self.cnt[sem])
        self.ops[eng].append((waits, emit, ev, dma, label))
        return ev

    def mark(self, eng):
        return Ev(eng, self.cnt.get(eng, 0))

    def finalize(self, nc, final_deps):
        self.op("sync", None, deps=final_deps, sig=False, label="final_wait")
        from contextlib import ExitStack
        stack = ExitStack()
        sems = {}
        for name in sorted(self.cnt.keys()):
            sems[name] = stack.enter_context(nc.semaphore(name))
        block = stack.enter_context(nc.Block())

        def mk(eng_name):
            def body(eng):
                for waits, emit, ev, dma, label in self.ops[eng_name]:
                    for s, v in waits:
                        eng.wait_ge(sems[s], v)
                    if emit is None:
                        continue
                    inst = emit(eng)
                    if ev is not None:
                        inst.then_inc(sems[ev.sem], 16 if dma else 1)
            return body

        for e in ENGINES:
            getattr(block, e)(mk(e))
        stack.close()
        return nc


SB_BASE = 16512
SB_LIMIT = 229344
PERSIST_SZ = 14336


class K:
    def __init__(self, fused=False):
        self.nc = bass.Bass("TRN2", target_bir_lowering=False)
        self.p = Prog()
        self.sb = {}
        self.ps = {}
        self._free = {}
        self.fused = fused
        self.pfx = ""
        self._persist = set()
        self._sb_ptr = SB_BASE
        self._persist_ptr = SB_LIMIT - PERSIST_SZ
        self._loaded_consts = set()
        self._sem_pool = 0
        self.banks = {}
        self._psum_cms = []
        self._bank_gen = 0
        self._bank_spec = None
        self.set_banks("std")

    BANK_SPECS = {
        "std": (("A0", [128, 512], F32), ("A1", [128, 512], F32),
                ("A2", [128, 512], F32), ("A3", [128, 512], F32),
                ("W0", [65, 1024], F32), ("P3", [128, 1024], BF16),
                ("PD", [1, 512], F32)),
        "moe": tuple((f"A{i}", [128, 512], F32) for i in range(8)),
    }

    def set_banks(self, spec):
        if spec == self._bank_spec:
            return
        for cm in reversed(self._psum_cms):
            cm.__exit__(None, None, None)
        self._psum_cms = []
        self.banks = {}
        self._bank_gen += 1
        for nm, shape, dt in self.BANK_SPECS[spec]:
            cm = self.nc.psum_tensor(f"{nm}_g{self._bank_gen}",
                                     list(shape), dt)
            self.banks[nm] = Bank(nm, cm.__enter__())
            self._psum_cms.append(cm)
        self._bank_spec = spec

    def bk(self, nm):
        return self.banks[nm]

    def use(self, key):
        return self._free.get(key)

    def rel(self, key, ev):
        self._free[key] = ev

    def stream_sem(self):
        self._sem_pool += 1
        return f"ws{self._sem_pool % 10}"

    def dram_in(self, name, shape, dtype=F32):
        return self.nc.dram_tensor(name, list(shape), dtype,
                                   kind="ExternalInput").ap()

    def dram_out(self, name, shape, dtype=F32):
        return self.nc.dram_tensor(name, list(shape), dtype,
                                   kind="ExternalOutput").ap()

    def dram_tmp(self, name, shape, dtype=F32):
        return self.nc.dram_tensor(name, list(shape), dtype,
                                   kind="Internal").ap()

    def sbuf(self, name, shape, dtype, persist=False):
        per_part = int(np.prod(shape[1:])) * (2 if dtype == BF16 else 4)
        per_part = (per_part + 31) & ~31
        if persist:
            off = self._persist_ptr
            self._persist_ptr += per_part
            assert self._persist_ptr <= SB_LIMIT, "persist region overflow"
            self._persist.add(name)
        else:
            off = self._sb_ptr
            self._sb_ptr += per_part
            assert self._sb_ptr <= SB_LIMIT - PERSIST_SZ,                 f"sbuf scratch overflow at {self.pfx}{name}: {self._sb_ptr}"
        h = self.nc.alloc_sbuf_tensor_at("sb_" + self.pfx + name, list(shape),
                                         dtype, offset=off)
        self.sb[name] = h
        return h

    def new_block(self, pfx):
        self.barrier()
        self.set_banks("std")
        self.pfx = pfx
        self._sb_ptr = SB_BASE
        for key in [kk for kk in self.sb if kk not in self._persist]:
            del self.sb[key]
        keep = {kk: v for kk, v in self._free.items()
                if kk in ("consts", "ln_eps_ev")}
        self._free = keep

    def barrier(self):
        p = self.p
        evs = [Ev(sname, v) for sname, v in p.cnt.items()]
        for eng in ENGINES:
            p.op(eng, None, deps=evs, sig=False, label="barrier")

    def psum(self, name, shape, dtype=F32):
        h = self.nc.alloc_psum_tensor(name, list(shape), dtype)
        self.ps[name] = h
        return h

    BIGC = {"c_sdpmask", "c_sel16"}

    def load_consts(self, keys):
        shp = {"c_ones1": (1, 128), "c_onesb": (128, 1), "c_identb": (128, 128),
               "c_identf": (128, 128), "c_triu": (128, 128),
               "c_tril": (128, 128), "c_sdpmask": (128, 2048),
               "c_palace": (128, 64), "c_sel8": (8, 512), "c_sel16": (16, 2048)}
        dt = {"c_onesb": BF16, "c_identb": BF16, "c_sel16": BF16}
        keys = [kk for kk in keys
                if (kk in self.BIGC and kk not in self.sb)
                or (kk not in self.BIGC and kk not in self._loaded_consts)]
        if not keys:
            return self.use("consts")
        casts = []
        for key in keys:
            if key in self.BIGC:
                if not hasattr(self, "_const_dram"):
                    self._const_dram = {}
                if key not in self._const_dram:
                    self._const_dram[key] = self.dram_in(
                        key, shp[key], BF16 if key == "c_sel16" else F32)
                d = self._const_dram[key]
                sb = self.sbuf(key, shp[key],
                               BF16 if key == "c_sel16" else F32)
                self.p.op("sync", lambda e, sb=sb, d=d: e.dma_start(
                    out=sb[:], in_=d), dma="cst", label=f"ld_{key}")
                continue
            self._loaded_consts.add(key)
            d = self.dram_in(key, shp[key], F32)
            sb = self.sbuf(key, shp[key], dt.get(key, F32), persist=True)
            if dt.get(key) is None:
                self.p.op("sync", lambda e, sb=sb, d=d: e.dma_start(
                    out=sb[:], in_=d), dma="cst", label=f"ld_{key}")
            else:
                stg = self.sbuf(key + "_stg", shp[key], F32)
                self.p.op("sync", lambda e, stg=stg, d=d: e.dma_start(
                    out=stg[:], in_=d), dma="cst", label=f"ld_{key}")
                casts.append((sb, stg, key))
        ev_all = Ev("cst", self.p.cnt["cst"])
        evs = [ev_all]
        for sb, stg, key in casts:
            evs.append(self.p.op("vector", lambda e, sb=sb, stg=stg:
                                 e.tensor_copy(sb[:], stg[:]), deps=[ev_all],
                                 label=f"cast_{key}"))
        prev = self.use("consts")
        self.rel("consts", evs + ([prev] if prev else []))
        return evs



def colmajor(W):
    Din, Dout = W.shape
    KC, OC = Din // 128, (Dout + 127) // 128
    if Dout % 128:
        Wp = np.zeros((Din, OC * 128), dtype=W.dtype)
        Wp[:, :Dout] = W
        W = Wp
    return np.ascontiguousarray(
        W.reshape(KC, 128, OC, 128).transpose(2, 1, 0, 3).reshape(OC, 128, KC * 128))


def rowmajor(W):
    Din, Dout = W.shape
    return np.ascontiguousarray(W.reshape(Din // 128, 128, Dout))


def pp_bias(b, OC):
    return np.ascontiguousarray(np.asarray(b).reshape(OC, 128).T)



class WCols:
    def __init__(self, k, name, hbm_ap, plan, KCW, queue="sync",
                 cast_eng="vector", stg_bufs=2, col_bufs=2):
        self.k, self.name, self.ap = k, name, hbm_ap
        self.plan, self.KCW, self.queue = plan, KCW, queue
        self.cast_eng = cast_eng
        self.nstg = stg_bufs
        self.ncol = col_bufs
        self.stg = [k.sbuf(f"{name}_stg{i}", [128, KCW], F32)
                    for i in range(stg_bufs)]
        self.col = [k.sbuf(f"{name}_col{i}", [128, KCW], BF16)
                    for i in range(col_bufs)]
        self.stg_free = [None] * stg_bufs
        self.col_free = [None] * col_bufs
        self.ready = {}
        self.emitted = 0

    def _emit(self, i):
        if i >= len(self.plan):
            return
        p = self.k.p
        s, c, oc = self.stg[i % self.nstg], self.col[i % self.ncol], self.plan[i]
        ev_d = p.op(self.queue, lambda e, s=s, oc=oc: e.dma_start(
            out=s[:], in_=self.ap[oc]), deps=[self.stg_free[i % self.nstg]],
            dma=f"wq_{self.name}{i % self.nstg}", label=f"{self.name}_dma{i}")
        ev_c = p.op(self.cast_eng, lambda e, c=c, s=s: e.tensor_copy(c[:], s[:]),
                    deps=[ev_d, self.col_free[i % self.ncol]],
                    label=f"{self.name}_cast{i}")
        self.stg_free[i % self.nstg] = ev_c
        self.ready[i] = ev_c
        self.emitted = i + 1

    def get(self, i):
        tgt = min(i + self.ncol, len(self.plan))
        while self.emitted < tgt:
            self._emit(self.emitted)
        return self.col[i % self.ncol], self.ready[i]

    def release(self, i, ev):
        self.col_free[i % self.ncol] = ev
        if self.emitted == i + self.ncol:
            self._emit(i + self.ncol)

    def tile(self, col, kc):
        return col[:, kc * 128:(kc + 1) * 128]


class Bank:
    def __init__(self, name, handle):
        self.name, self.h = name, handle



def emit_ln(k, x_tiles, x_ev, xn_tiles, b_sum, b_ss, b_bc, tag, wkey="xn"):
    p = k.p
    if "ln_xb" not in k.sb:
        k.sbuf("ln_xb", [128, 512], BF16)
        k.sbuf("ln_sq", [128, 512], BF16)
        if "g_ln_rows" not in k._persist:
            k.sbuf("g_ln_rows", [1, 4 * 512], F32, persist=True)
            k._persist.add("g_ln_rows")
        k.sb["ln_rows"] = k.sb["g_ln_rows"]
        k.sbuf("ln_mb", [128, 512], BF16)
        k.sbuf("ln_rb", [128, 512], BF16)
        k.sbuf("ln_tmp", [128, 512], F32)
        eps = k.sbuf("ln_eps", [1, 1], F32)
        ev = p.op("vector", lambda e: e.memset(eps[:], 1e-5), label="ln_eps")
        k.rel("ln_eps_ev", ev)
    xb, sq = k.sb["ln_xb"], k.sb["ln_sq"]
    rows, mb, rb, tmp = (k.sb["ln_rows"], k.sb["ln_mb"], k.sb["ln_rb"],
                         k.sb["ln_tmp"])
    onesb, ones1 = k.sb["c_onesb"], k.sb["c_ones1"]
    ps_sum, ps_ss, ps_bc = b_sum.h, b_ss.h, b_bc.h

    ev_m1 = ev_m2 = None
    prev_xb = k.use("ln_xb0")
    prev_sq = k.use("ln_sq0")
    for c in range(NCH):
        ev_cast = p.op("vector", lambda e, c=c: e.tensor_copy(
            xb[:, 0:512], x_tiles[c]),
            deps=[x_ev[c], prev_xb, k.use("consts")], label=f"{tag}_cast{c}")
        ev_sq = p.op("scalar", lambda e, c=c: e.activation(
            sq[:, 0:512], x_tiles[c], AF.Square), deps=[x_ev[c], prev_sq],
            label=f"{tag}_sq{c}")
        ev_m1 = p.op("tensor", lambda e, c=c: e.matmul(
            ps_sum[0:1, 0:512], onesb[:], xb[:, 0:512],
            start=(c == 0), stop=(c == NCH - 1)),
            deps=[ev_cast] + ([k.use(b_sum.name)] if c == 0 else []),
            label=f"{tag}_s1_{c}")
        ev_m2 = p.op("tensor", lambda e, c=c: e.matmul(
            ps_ss[0:1, 0:512], onesb[:], sq[:, 0:512],
            start=(c == 0), stop=(c == NCH - 1)),
            deps=[ev_sq] + ([k.use(b_ss.name)] if c == 0 else []),
            label=f"{tag}_s2_{c}")
        prev_xb = ev_m1
        prev_sq = ev_m2
    k.rel("ln_xb0", prev_xb)
    k.rel("ln_sq0", prev_sq)

    r = lambda i: rows[0:1, 512 * i:512 * i + 512]
    ev_rows0 = k.use("ln_rows")
    ev_m = p.op("vector", lambda e: e.tensor_scalar_mul(
        r(0), ps_sum[0:1, 0:512], 1.0 / D), deps=[ev_m1, ev_rows0],
        label=f"{tag}_m")
    k.rel(b_sum.name, ev_m)
    ev_ss = p.op("vector", lambda e: e.tensor_scalar_mul(
        r(2), ps_ss[0:1, 0:512], 1.0 / D), deps=[ev_m2, ev_rows0],
        label=f"{tag}_ss")
    k.rel(b_ss.name, ev_ss)
    ev_msq = p.op("scalar", lambda e: e.activation(r(1), r(0), AF.Square),
                  deps=[ev_m], label=f"{tag}_msq")
    ev_var = p.op("vector", lambda e: e.tensor_sub(r(2), r(2), r(1)),
                  deps=[ev_msq, ev_ss], label=f"{tag}_var")
    eps_ap = k.sb["ln_eps"][0:1, 0:1]
    ev_l = p.op("scalar", lambda e, eps_ap=eps_ap: e.activation(
        r(1), r(2), AF.Ln, bias=eps_ap),
        deps=[ev_var, k.use("ln_eps_ev")], label=f"{tag}_lnv")
    ev_r = p.op("scalar", lambda e: e.activation(r(3), r(1), AF.Exp,
                scale=-0.5), deps=[ev_l], label=f"{tag}_rstd")
    k.rel("ln_rows", ev_r)

    ev_b1 = p.op("tensor", lambda e: e.matmul(ps_bc[:, 0:512], ones1[:], r(0),
                 start=True, stop=True), deps=[ev_m, k.use(b_bc.name)],
                 label=f"{tag}_bm")
    ev_e1 = p.op("vector", lambda e: e.tensor_copy(mb[:], ps_bc[:, 0:512]),
                 deps=[ev_b1, k.use("ln_mb")], label=f"{tag}_bme")
    ev_b2 = p.op("tensor", lambda e: e.matmul(ps_bc[:, 0:512], ones1[:], r(3),
                 start=True, stop=True), deps=[ev_r, ev_e1],
                 label=f"{tag}_br")
    ev_e2 = p.op("vector", lambda e: e.tensor_copy(rb[:], ps_bc[:, 0:512]),
                 deps=[ev_b2, k.use("ln_rb")], label=f"{tag}_bre")
    k.rel(b_bc.name, ev_e2)

    out_ev = []
    last_sub = None
    last_mul = None
    for c in range(NCH):
        ev_s = p.op("vector", lambda e, c=c: e.tensor_sub(
            tmp[:], x_tiles[c], mb[:]),
            deps=[x_ev[c], ev_e1, k.use("ln_tmp")], label=f"{tag}_sub{c}")
        last_sub = ev_s
        ev_mu = p.op("vector", lambda e, c=c: e.tensor_mul(
            xn_tiles[c], tmp[:], rb[:]),
            deps=[ev_s, ev_e2, k.use(f"{wkey}{c}")], label=f"{tag}_mul{c}")
        k.rel("ln_tmp", ev_mu)
        last_mul = ev_mu
        out_ev.append(ev_mu)
    k.rel("ln_mb", last_sub)
    k.rel("ln_rb", last_mul)
    return out_ev



def emit_proj(k, ws, seq0, ocs, rhs_tiles, rhs_ev, banks, epilogue, tag,
              nt=512):
    p = k.p
    KC = len(rhs_tiles)
    for i, oc in enumerate(ocs):
        col, wev = ws.get(seq0 + i)
        bank = banks[i % len(banks)]
        ev = None
        for kc in range(KC):
            deps = [wev, rhs_ev[kc] if rhs_ev else None]
            if kc == 0:
                deps.append(k.use(bank.name))
            ev = p.op("tensor", lambda e, col=col, kc=kc, bank=bank: e.matmul(
                bank.h[:, 0:nt], ws.tile(col, kc), rhs_tiles[kc],
                start=(kc == 0), stop=(kc == KC - 1)), deps=deps,
                label=f"{tag}_mm{oc}_{kc}")
        ws.release(seq0 + i, ev)
        ev_done = epilogue(i, oc, bank.h[:, 0:nt], ev)
        k.rel(bank.name, ev_done)
    return seq0 + len(ocs)



def build_l1_attn(upto=None, parts=("kv", "pt", "o", "s"), k=None,
                  x_h=None, y_h=None):
    standalone = k is None
    if standalone:
        k = K()
    p = k.p
    if x_h is None:
        x_h = k.dram_in("x", [D, T]).rearrange("(c p) t -> c p t", p=128)
        y_h = k.dram_out("y", [D, T]).rearrange("(c p) t -> c p t", p=128)
    wqkv_h = k.dram_in("l1_wqkv", [24, 128, 1024])
    bqkv_h = k.dram_in("l1_bqkv", [128, 24])
    wout_h = k.dram_in("l1_wout", [8, 128, 1024])
    bout_h = k.dram_in("l1_bout", [128, 8])
    k.load_consts(("c_ones1", "c_onesb", "c_identb", "c_triu"))

    x_tt = [k.sbuf(f"x{c}", [128, 512], F32) for c in range(NCH)]
    xn_tt = [k.sbuf(f"xn{c}", [128, 512], BF16) for c in range(NCH)]
    q_hm = k.sbuf("q_hm", [64, 16 * 512], BF16)
    k_hm = k.sbuf("k_hm", [64, 16 * 512], BF16)
    v_hm = k.sbuf("v_hm", [64, 16 * 512], BF16)
    stg_odd = [k.sbuf(f"stgodd{kk}", [128, 8 * 512], BF16) for kk in range(3)]
    o_hm = k.sbuf("o_hm", [64, 16 * 512], BF16)
    K_sb = k.sbuf("K_sb", [128, 512], BF16)
    V_sb = k.sbuf("V_sb", [128, 512], BF16)
    WT_sb = k.sbuf("WT_sb", [128, 1024], BF16)
    S_sb = k.sbuf("S_sb", [64, 1024], BF16)
    S32 = k.sbuf("S32", [64, 1024], F32)
    oTall = k.sbuf("oTall", [128, 8 * 512], BF16)
    ytmp = k.sbuf("ytmp", [128, 512], F32)
    bqkv = k.sbuf("bqkv", [128, 24], F32)
    bqkv_s = k.sbuf("bqkv_s", [128, 8], F32)
    bout = k.sbuf("bout", [128, 8], F32)

    pS = k.bk("W0")
    pt0, pt1, psp = k.bk("A0"), k.bk("A1"), k.bk("A2")
    pkv = k.bk("P3")
    poA, poB = k.bk("A2"), k.bk("A3")

    ident = k.sb["c_identb"]
    triu = k.sb["c_triu"]

    ev_b = p.op("sync", lambda e: e.dma_start(out=bqkv[:], in_=bqkv_h),
                dma="cst2", label="ld_bqkv")
    ev_bout = p.op("sync", lambda e: e.dma_start(out=bout[:], in_=bout_h),
                   dma="cst2", label="ld_bout")
    ev_b = ev_bout = Ev("cst2", p.cnt["cst2"])
    ev_bs = p.op("vector", lambda e: e.tensor_scalar_mul(
        bqkv_s[:], bqkv[:, 8:16], 0.125), deps=[ev_b], label="bscale")

    ws_qkv = WCols(k, "wqkv", wqkv_h,
                   [oc for _ in range(TT) for oc in range(24)], 1024, "sync")
    ws_out = WCols(k, "wout", wout_h,
                   [oc for _ in range(TT) for oc in range(8)], 1024, "gpsimd")

    seq_qkv = seq_out = 0
    ev_upd = None
    ev_s32i = p.op("vector", lambda e: e.memset(S32[:], 0.0), label="s32init")
    k.rel("S32_init", ev_s32i)
    for tt in range(TT):
        tsl = slice(tt * 512, (tt + 1) * 512)
        x_ev = []
        for c in range(NCH):
            x_ev.append(p.op("sync", lambda e, c=c, tsl=tsl: e.dma_start(
                out=x_tt[c][:], in_=x_h[c, :, tsl]),
                deps=[k.use(f"x{c}")], dma="xld", label=f"ldx{c}_{tt}"))
        x_ev = [x_ev[-1]] * NCH
        xn_ev = emit_ln(k, [t[:] for t in x_tt], x_ev, [t[:] for t in xn_tt],
                        pt0, pt1, psp, f"ln{tt}")

        if upto == "ln":
            for c in range(NCH):
                ev_nn = p.op("vector", lambda e, c=c: e.tensor_add(
                    x_tt[c][:], x_tt[c][:], xn_tt[c][:]), deps=[xn_ev[c]],
                    label=f"dbg_add{c}")
                p.op("gpsimd", lambda e, c=c, tsl=tsl: e.dma_start(
                    out=y_h[c, :, tsl], in_=x_tt[c][:]), deps=[ev_nn],
                    dma="st", label=f"sty{c}_{tt}")
            ev_store_round = Ev("st", p.cnt["st"])
            for c in range(NCH):
                k.rel(f"x{c}", ev_store_round)
                k.rel(f"xn{c}", ev_nn)
            continue

        qkv_wr = {}

        def qkv_epi(i, oc, ps, ev):
            kind = oc // 8
            dst = (q_hm, k_hm, v_hm)[kind]
            if kind == 1:
                b_ap, sc, dep_b = bqkv_s[:, oc - 8:oc - 7], 0.125, ev_bs
            else:
                b_ap, sc, dep_b = bqkv[:, oc:oc + 1], 1.0, ev_b
            h0 = (oc % 8) * 2
            so = stg_odd[kind]
            oc8 = oc % 8
            ev1 = p.op("scalar", lambda e, ps=ps, b_ap=b_ap, sc=sc, h0=h0,
                       dst=dst: e.activation(
                           dst[0:64, 512 * h0:512 * h0 + 512], ps[0:64, :],
                           AF.Identity, bias=b_ap[0:64, :], scale=sc),
                       deps=[ev, dep_b, k.use(f"hm{kind}_{h0}")],
                       label=f"qkvev{oc}a")
            ev2 = p.op("scalar", lambda e, ps=ps, b_ap=b_ap, sc=sc, oc8=oc8,
                       so=so: e.activation(
                           so[64:128, 512 * oc8:512 * oc8 + 512],
                           ps[64:128, :], AF.Identity, bias=b_ap[64:128, :],
                           scale=sc),
                       deps=[ev, dep_b, k.use(f"stg{kind}")],
                       label=f"qkvev{oc}b")
            qkv_wr[oc] = [ev1, ev2]
            return [ev1, ev2]

        seq_qkv = emit_proj(k, ws_qkv, seq_qkv, list(range(24)),
                            [t[:] for t in xn_tt], xn_ev, [pt0, pt1],
                            qkv_epi, f"qkv{tt}")
        for c in range(NCH):
            k.rel(f"xn{c}", p.mark("tensor"))
        hm_ev = {}
        for kind in range(3):
            dst = (q_hm, k_hm, v_hm)[kind]
            src_ap = stg_odd[kind].ap()[64:128, :].rearrange(
                "p (oc t) -> p oc t", t=512)
            dst_ap = dst.ap().rearrange("p (oc pb t) -> p oc pb t",
                                        pb=2, t=512)[:, :, 1, :]
            odd_deps = [qkv_wr[kind * 8 + i][1] for i in range(8)]
            ev_dma = p.op("gpsimd", lambda e, src_ap=src_ap, dst_ap=dst_ap:
                          e.dma_start(out=dst_ap, in_=src_ap),
                          deps=odd_deps + [k.use(f"hmodd{kind}")],
                          dma=f"hmg{kind}", label=f"hmdma{kind}_{tt}")
            k.rel(f"stg{kind}", ev_dma)
            for h in range(16):
                if h % 2 == 0:
                    hm_ev[(kind, h)] = qkv_wr[kind * 8 + h // 2][0]
                else:
                    hm_ev[(kind, h)] = ev_dma

        if upto == "qkv":
            ev_all = [p.mark("scalar"), p.mark("gpsimd"), Ev("hmg0", p.cnt.get("hmg0", 0)),
                      Ev("hmg1", p.cnt.get("hmg1", 0)), Ev("hmg2", p.cnt.get("hmg2", 0))]
            for c in range(NCH):
                ev_nn = p.op("vector", lambda e, c=c: e.tensor_add(
                    x_tt[c][0:64, :], x_tt[c][0:64, :],
                    q_hm[0:64, c * 512:(c + 1) * 512]),
                    deps=ev_all, label=f"dbg_add{c}")
                p.op("gpsimd", lambda e, c=c, tsl=tsl: e.dma_start(
                    out=y_h[c, :, tsl], in_=x_tt[c][:]), deps=[ev_nn],
                    dma="st", label=f"sty{c}_{tt}")
            ev_store_round = Ev("st", p.cnt["st"])
            for c in range(NCH):
                k.rel(f"x{c}", ev_store_round)
            for kind in range(3):
                k.rel(f"hmodd{kind}", p.mark("vector"))
            continue

        for jj in range(4):
            j = tt * 4 + jj
            jsl = lambda h: slice(512 * h + 128 * jj, 512 * h + 128 * jj + 128)
            ev_Sprev = k.use("S_ready")
            for g in range(2):
                heads = list(range(8 * g, 8 * g + 8))
                ev_tk = {}
                ev_tv = {}
                for i_h, h in enumerate(heads):
                    kin = k_hm[0:64, jsl(h)]
                    vin = v_hm[0:64, jsl(h)]
                    ev_tk[h] = p.op("tensor", lambda e, kin=kin, i_h=i_h:
                                    e.transpose(pkv.h[:, 64 * i_h:64 * i_h + 64],
                                                kin, ident[0:64, 0:64]),
                                    deps=[hm_ev[(1, h)], k.use(pkv.name)],
                                    label=f"trK{j}_{h}")
                    ev_tv[h] = p.op("tensor", lambda e, vin=vin, i_h=i_h:
                                    e.transpose(
                                        pkv.h[:, 512 + 64 * i_h:576 + 64 * i_h],
                                        vin, ident[0:64, 0:64]),
                                    deps=[hm_ev[(2, h)]],
                                    label=f"trV{j}_{h}")
                ev_tall = [ev_tk[heads[-1]], ev_tv[heads[-1]]]
                ev_Ke = p.op("vector", lambda e: e.tensor_copy(
                    K_sb[:], pkv.h[:, 0:512]),
                    deps=ev_tall + [k.use("K_sb")], label=f"Kev{j}_{g}")
                ev_Ve = p.op("vector", lambda e: e.tensor_copy(
                    V_sb[:], pkv.h[:, 512:1024]),
                    deps=ev_tall + [k.use("V_sb")], label=f"Vev{j}_{g}")
                k.rel(pkv.name, [ev_Ke, ev_Ve])

                if "pt" not in parts:
                    continue
                ev_pt = {}
                for i_h, h in enumerate(heads):
                    bank = pt0 if i_h < 4 else pt1
                    co = 128 * (i_h % 4)
                    kin = k_hm[0:64, jsl(h)]
                    qin = q_hm[0:64, jsl(h)]
                    ev_pt[h] = p.op("tensor", lambda e, kin=kin, qin=qin,
                                    bank=bank, co=co:
                                    e.matmul(bank.h[:, co:co + 128],
                                             kin, qin, start=True, stop=True),
                                    deps=[hm_ev[(1, h)], hm_ev[(0, h)],
                                          k.use(bank.name)],
                                    label=f"PT{j}_{h}")
                ev_w0 = p.op("vector", lambda e: e.tensor_tensor(
                    WT_sb.ap()[:, 0:512].rearrange("p (r c) -> p r c", c=128),
                    pt0.h.ap()[:, 0:512].rearrange("p (r c) -> p r c", c=128),
                    triu.ap().unsqueeze(1).broadcast_to([128, 4, 128]),
                    ALU.mult),
                    deps=[ev_pt[heads[3]], k.use("WT0")], label=f"WT{j}_{g}0")
                ev_w1 = p.op("vector", lambda e: e.tensor_tensor(
                    WT_sb.ap()[:, 512:1024].rearrange("p (r c) -> p r c", c=128),
                    pt1.h.ap()[:, 0:512].rearrange("p (r c) -> p r c", c=128),
                    triu.ap().unsqueeze(1).broadcast_to([128, 4, 128]),
                    ALU.mult),
                    deps=[ev_pt[heads[7]], k.use("WT1")], label=f"WT{j}_{g}1")
                k.rel(pt0.name, ev_w0)
                k.rel(pt1.name, ev_w1)

                if "o" not in parts:
                    k.rel(pt0.name, ev_w0)
                    k.rel(pt1.name, ev_w1)
                    continue
                do_o1 = "noo1" not in parts
                do_o2 = "noo2" not in parts
                ev_o = {}
                for i_h, h in enumerate(heads):
                    pog = poA if i_h < 4 else poB
                    co = 128 * (i_h % 4)
                    first = True
                    last = None
                    if j > 0 and do_o1:
                        qin = q_hm[0:64, jsl(h)]
                        last = p.op("tensor", lambda e, h=h, co=co, qin=qin,
                                    pog=pog: e.matmul(
                                        pog.h[0:64, co:co + 128],
                                        S_sb[0:64, 64 * h:64 * h + 64],
                                        qin, start=True, stop=False),
                                    deps=[ev_Sprev, hm_ev[(0, h)]] +
                                    ([k.use(pog.name)] if i_h % 4 == 0
                                     else []), label=f"o1_{j}_{h}")
                        first = False
                    wm = ev_w0 if i_h < 4 else ev_w1
                    if do_o2 or last is None:
                        last = p.op("tensor", lambda e, h=h, co=co, i_h=i_h,
                                    first=first, pog=pog: e.matmul(
                                        pog.h[0:64, co:co + 128],
                                        V_sb[:, 64 * i_h:64 * i_h + 64],
                                        WT_sb[:, 128 * i_h:128 * i_h + 128],
                                        start=first, stop=True),
                                    deps=[ev_Ve, wm] +
                                    ([k.use(pog.name)]
                                     if first and i_h % 4 == 0 else []),
                                    label=f"o2_{j}_{h}")
                    ev_o[h] = last
                ev_oe = None
                for i_h, h in enumerate(heads):
                    pog = poA if i_h < 4 else poB
                    co = 128 * (i_h % 4)
                    do = 512 * h + 128 * jj
                    ev_oe = p.op("vector", lambda e, co=co, do=do, pog=pog:
                                 e.tensor_copy(
                                     o_hm[0:64, do:do + 128],
                                     pog.h[0:64, co:co + 128]),
                                 deps=[ev_o[heads[-1]], k.use("o_hm")],
                                 label=f"oev{j}_{h}")
                k.rel(poA.name, ev_oe)
                k.rel(poB.name, ev_oe)

                if "s" not in parts:
                    continue
                for i_h, h in enumerate(heads):
                    ev_upd = p.op("tensor", lambda e, h=h, i_h=i_h: e.matmul(
                        pS.h[0:64, 64 * h:64 * h + 64],
                        K_sb[:, 64 * i_h:64 * i_h + 64],
                        V_sb[:, 64 * i_h:64 * i_h + 64],
                        start=(i_h == 0), stop=(i_h == 7)),
                        deps=[ev_Ke, ev_Ve] +
                        ([k.use(pS.name)] if i_h == 0 and g == 0 else []),
                        label=f"Supd{j}_{h}")
                k.rel("K_sb", ev_upd)
                k.rel("V_sb", ev_upd)
                if "o" in parts:
                    k.rel("WT0", ev_o[heads[3]])
                    k.rel("WT1", ev_o[heads[7]])
            if "s" not in parts:
                continue
            ev_sadd = p.op("vector", lambda e: e.tensor_add(
                S32[:], S32[:], pS.h[0:64, :]),
                deps=[ev_upd, k.use("S32_init"), ev_Sprev], label=f"Sadd{j}")
            ev_scast = p.op("vector", lambda e: e.tensor_copy(S_sb[:], S32[:]),
                            deps=[ev_sadd], label=f"Scast{j}")
            k.rel(pS.name, ev_sadd)
            k.rel("S_ready", ev_scast)

        if upto == "scan":
            ev_all = [p.mark("vector")]
            for c in range(NCH):
                ev_nn = p.op("vector", lambda e, c=c: e.tensor_add(
                    x_tt[c][0:64, :], x_tt[c][0:64, :],
                    o_hm[0:64, c * 512:(c + 1) * 512]),
                    deps=ev_all, label=f"dbg_add{c}")
                p.op("gpsimd", lambda e, c=c, tsl=tsl: e.dma_start(
                    out=y_h[c, :, tsl], in_=x_tt[c][:]), deps=[ev_nn],
                    dma="st", label=f"sty{c}_{tt}")
            ev_store_round = Ev("st", p.cnt["st"])
            for c in range(NCH):
                k.rel(f"x{c}", ev_store_round)
            k.rel("o_hm", p.mark("vector"))
            continue

        ev_scan_done = p.mark("vector")
        ohm_e = o_hm.ap().rearrange("p (hp pb t) -> p hp pb t", pb=2, t=512)
        ev_oe1 = p.op("vector", lambda e: e.tensor_copy(
            oTall.ap()[0:64, :].rearrange("p (hp t) -> p hp t", t=512),
            ohm_e[:, :, 0, :]),
            deps=[ev_scan_done, k.use("oTall")], label=f"oTe{tt}")
        ev_oe2 = p.op("gpsimd", lambda e: e.dma_start(
            out=oTall.ap()[64:128, :].rearrange("p (hp t) -> p hp t", t=512),
            in_=ohm_e[:, :, 1, :]),
            deps=[ev_scan_done, k.use("oTall")], dma="oTq",
            label=f"oTo{tt}")
        k.rel("o_hm", [ev_oe1, ev_oe2])
        ev_oT = [ev_oe1, ev_oe2]

        def out_epi(i, oc, ps, ev):
            ev_a = p.op("scalar", lambda e, oc=oc, ps=ps: e.activation(
                ytmp[:], ps, AF.Identity, bias=bout[:, oc:oc + 1]),
                deps=[ev, ev_bout, k.use("ytmp")], label=f"yact{oc}")
            ev_add = p.op("vector", lambda e, oc=oc: e.tensor_add(
                x_tt[oc][:], x_tt[oc][:], ytmp[:]), deps=[ev_a],
                label=f"yadd{oc}")
            k.rel("ytmp", ev_add)
            p.op("gpsimd", lambda e, oc=oc, tsl=tsl: e.dma_start(
                out=y_h[oc, :, tsl], in_=x_tt[oc][:]), deps=[ev_add],
                dma="st", label=f"sty{oc}_{tt}")
            return ev_a

        seq_out = emit_proj(k, ws_out, seq_out, list(range(8)),
                            [oTall[:, 512 * hp:512 * hp + 512]
                             for hp in range(8)], [ev_oT] * 8,
                            [pt0, pt1], out_epi, f"out{tt}")
        k.rel("oTall", p.mark("tensor"))
        ev_store_round = Ev("st", p.cnt["st"])
        for c in range(NCH):
            k.rel(f"x{c}", ev_store_round)
        for kind in range(3):
            k.rel(f"hmodd{kind}", p.mark("tensor"))
            for h in range(16):
                k.rel(f"hm{kind}_{h}", p.mark("tensor"))

    if standalone:
        p.finalize(k.nc, [Ev("st", p.cnt["st"])])
        return k.nc



class WStatic:

    def __init__(self, plan, tile_fn, ready_ev):
        self.plan, self.tile_fn, self.ev = plan, tile_fn, ready_ev

    def get(self, i):
        return self.plan[i], self.ev

    def release(self, i, ev):
        pass

    def tile(self, oc, kc):
        return self.tile_fn(oc, kc)


def load_full(k, name, hbm_ap, n, width, queue="sync", bufs=2):
    p = k.p
    stg = [k.sbuf(f"{name}_stg{i}", [128, width], F32) for i in range(bufs)]
    w = k.sbuf(f"{name}_w", [128, n * width], BF16)
    stg_free = [None] * bufs
    last = None
    for i in range(n):
        ev_d = p.op(queue, lambda e, i=i, s=stg[i % bufs]: e.dma_start(
            out=s[:], in_=hbm_ap[i]), deps=[stg_free[i % bufs]],
            dma=f"wf_{name}{i % bufs}", label=f"{name}_dma{i}")
        ev_c = p.op("vector", lambda e, i=i, s=stg[i % bufs]: e.tensor_copy(
            w[:, i * width:(i + 1) * width], s[:]), deps=[ev_d],
            label=f"{name}_cast{i}")
        stg_free[i % bufs] = ev_c
        last = ev_c
    return w, last



def build_l1_ffn(k=None, x_h=None, y_h=None):
    standalone = k is None
    if standalone:
        k = K()
    p = k.p
    if x_h is None:
        x_h = k.dram_in("x", [D, T]).rearrange("(c p) t -> c p t", p=128)
        y_h = k.dram_out("y", [D, T]).rearrange("(c p) t -> c p t", p=128)
    wr_h = k.dram_in("l1f_wr", [8, 128, 1024])
    wi_h = k.dram_in("l1f_wi", [8, 128, 1024])
    fr_h = k.dram_in("c_fr", [8, 128, 1024])
    fi_h = k.dram_in("c_fi", [8, 128, 1024])
    fin_h = k.dram_in("c_fin", [8, 128, 1024])
    bias_h = k.dram_in("l1f_bias", [128, 8])
    k.load_consts(("c_ones1", "c_onesb"))

    x_tt = [k.sbuf(f"x{c}", [128, 512], F32) for c in range(NCH)]
    xn_tt = [k.sbuf(f"xn{c}", [128, 512], BF16) for c in range(NCH)]
    ytmp = k.sbuf("ytmp", [128, 512], F32)
    bias = k.sbuf("bias", [128, 8], F32)
    At_r = k.sbuf("At_r", [128, 8192], BF16)
    At_i = k.sbuf("At_i", [128, 8192], BF16)
    weff = k.sbuf("weff", [128, 8192], BF16)

    pt0, pt1, psp = k.bk("A0"), k.bk("A1"), k.bk("A2")

    ev_bias = p.op("sync", lambda e: e.dma_start(out=bias[:], in_=bias_h),
                   dma="cst2", label="ld_bias")

    FRr, ev_fr = load_full(k, "fr", fr_h, 8, 1024, "sync")
    FIr, ev_fi = load_full(k, "fi", fi_h, 8, 1024, "sync")
    FInr, ev_fin = load_full(k, "fin", fin_h, 8, 1024, "sync")
    ev_F = [ev_fr, ev_fi, ev_fin]

    ws_wr = WCols(k, "wwr", wr_h, list(range(8)), 1024, "gpsimd")
    ws_wi = WCols(k, "wwi", wi_h, list(range(8)), 1024, "gpsimd")

    banks = [pt0, pt1]
    bi = 0
    for jc in range(8):
        cr, evr = ws_wr.get(jc)
        ci, evi = ws_wi.get(jc)
        last_r = last_i = None
        for comp in range(2):
            for dt_ in range(2):
                bank = banks[bi % 2]
                bi += 1
                dsl = slice(dt_ * 512, dt_ * 512 + 512)
                ev = None
                for kc in range(8):
                    for wsel in range(2):
                        if comp == 0:
                            lhs = ws_wr.tile(cr, kc) if wsel == 0 else \
                                ws_wi.tile(ci, kc)
                            rhs = FRr[:, kc * 1024:kc * 1024 + 1024][:, dsl] \
                                if wsel == 0 else \
                                FInr[:, kc * 1024:kc * 1024 + 1024][:, dsl]
                        else:
                            lhs = ws_wi.tile(ci, kc) if wsel == 0 else \
                                ws_wr.tile(cr, kc)
                            rhs = FRr[:, kc * 1024:kc * 1024 + 1024][:, dsl] \
                                if wsel == 0 else \
                                FIr[:, kc * 1024:kc * 1024 + 1024][:, dsl]
                        first = (kc == 0 and wsel == 0)
                        ev = p.op("tensor", lambda e, lhs=lhs, rhs=rhs,
                                  bank=bank, first=first, kc=kc, wsel=wsel:
                                  e.matmul(bank.h[:, 0:512], lhs, rhs,
                                           start=first,
                                           stop=(kc == 7 and wsel == 1)),
                                  deps=[evr, evi] + ev_F +
                                  ([k.use(bank.name)] if first else []),
                                  label=f"At{comp}_{jc}_{dt_}")
                dst = (At_r, At_i)[comp]
                ev_e = p.op("vector", lambda e, dst=dst, jc=jc, dsl=dsl,
                            bank=bank: e.tensor_copy(
                                dst[:, jc * 1024:jc * 1024 + 1024][:, dsl],
                                bank.h[:, 0:512]),
                            deps=[ev], label=f"Atev{comp}_{jc}_{dt_}")
                k.rel(bank.name, ev_e)
                if comp == 0:
                    last_r = ev_e
                else:
                    last_i = ev_e
        ws_wr.release(jc, last_i)
        ws_wi.release(jc, last_i)
    ev_At = p.mark("vector")

    for dc in range(8):
        for ot in range(2):
            bank = banks[bi % 2]
            bi += 1
            osl = slice(ot * 512, ot * 512 + 512)
            ev = None
            for jc in range(8):
                for wsel in range(2):
                    lhs = (At_r if wsel == 0 else At_i)[
                        :, jc * 1024 + dc * 128: jc * 1024 + dc * 128 + 128]
                    rhs = (FRr if wsel == 0 else FIr)[
                        :, jc * 1024:jc * 1024 + 1024][:, osl]
                    first = (jc == 0 and wsel == 0)
                    ev = p.op("tensor", lambda e, lhs=lhs, rhs=rhs, bank=bank,
                              first=first, jc=jc, wsel=wsel: e.matmul(
                                  bank.h[:, 0:512], lhs, rhs, start=first,
                                  stop=(jc == 7 and wsel == 1)),
                              deps=[ev_At] +
                              ([k.use(bank.name)] if first else []),
                              label=f"B_{dc}_{ot}")
            ev_e = p.op("scalar", lambda e, dc=dc, osl=osl, bank=bank:
                        e.activation(
                            weff[:, dc * 1024:dc * 1024 + 1024][:, osl],
                            bank.h[:, 0:512], AF.Identity, scale=1.0 / D),
                        deps=[ev], label=f"Bev_{dc}_{ot}")
            k.rel(bank.name, ev_e)
    ev_weff = p.mark("scalar")

    ws_eff = WStatic([oc for _ in range(TT) for oc in range(8)],
                     lambda oc, kc: weff[:, kc * 1024 + oc * 128:
                                         kc * 1024 + oc * 128 + 128],
                     ev_weff)

    seq = 0
    for tt in range(TT):
        tsl = slice(tt * 512, (tt + 1) * 512)
        x_ev = []
        for c in range(NCH):
            x_ev.append(p.op("sync", lambda e, c=c, tsl=tsl: e.dma_start(
                out=x_tt[c][:], in_=x_h[c, :, tsl]),
                deps=[k.use(f"x{c}")], dma="xld", label=f"ldx{c}_{tt}"))
        x_ev = [x_ev[-1]] * NCH
        xn_ev = emit_ln(k, [t[:] for t in x_tt], x_ev, [t[:] for t in xn_tt],
                        pt0, pt1, psp, f"ln{tt}")

        def out_epi(i, oc, ps, ev):
            ev_a = p.op("scalar", lambda e, oc=oc, ps=ps: e.activation(
                ytmp[:], ps, AF.Identity, bias=bias[:, oc:oc + 1]),
                deps=[ev, ev_bias, k.use("ytmp")], label=f"yact{oc}")
            ev_add = p.op("vector", lambda e, oc=oc: e.tensor_add(
                x_tt[oc][:], x_tt[oc][:], ytmp[:]), deps=[ev_a],
                label=f"yadd{oc}")
            k.rel("ytmp", ev_add)
            p.op("gpsimd", lambda e, oc=oc, tsl=tsl: e.dma_start(
                out=y_h[oc, :, tsl], in_=x_tt[oc][:]), deps=[ev_add],
                dma="st", label=f"sty{oc}_{tt}")
            return ev_a

        seq = emit_proj(k, ws_eff, seq, list(range(8)),
                        [t[:] for t in xn_tt], xn_ev, [pt0, pt1],
                        out_epi, f"o{tt}")
        ev_store_round = Ev("st", p.cnt["st"])
        for c in range(NCH):
            k.rel(f"x{c}", ev_store_round)
            k.rel(f"xn{c}", p.mark("tensor"))

    if standalone:
        p.finalize(k.nc, [Ev("st", p.cnt["st"])])
        return k.nc



def build_l2_attn(k=None, x_h=None, y_h=None):
    standalone = k is None
    if standalone:
        k = K()
    p = k.p
    if x_h is None:
        x_h = k.dram_in("x", [D, T]).rearrange("(c p) t -> c p t", p=128)
        y_h = k.dram_out("y", [D, T]).rearrange("(c p) t -> c p t", p=128)
    wqk_h = k.dram_in("l2_wqk", [16, 128, 1024])
    wv_h = k.dram_in("l2_wv", [8, 128, 1024])
    ipw_h = k.dram_in("l2_ipw", [1, 1])
    k.load_consts(("c_ones1", "c_onesb", "c_palace", "c_sel8"))

    x_tt = [k.sbuf(f"x{c}", [128, 512], F32) for c in range(NCH)]
    xn = [k.sbuf(f"xn{c}", [128, T], BF16) for c in range(NCH)]
    q_hm = k.sbuf("q_hm", [64, 4 * T], BF16)
    k_hm = k.sbuf("k_hm", [64, 4 * T], BF16)
    stg_odd = k.sbuf("stg_odd", [128, T], BF16)
    V_sb = [k.sbuf(f"V{tc}", [128, 16 * 65], BF16) for tc in range(16)]
    o_hm = k.sbuf("o_hm", [64, 4 * T], BF16)
    oTall = k.sbuf("oTall", [128, 2 * T], BF16)
    E_sb = [k.sbuf(f"E{i}", [128, 512], BF16) for i in range(4)]
    Epre = [k.sbuf(f"Epre{i}", [128, 512], F32) for i in range(2)]
    sel8 = k.sb["c_sel8"]
    sgv = k.sbuf("sgv", [1, 1], F32)
    sg128 = k.sbuf("sg128", [128, 1], F32)
    s1m = k.sbuf("s1m", [128, 1], F32)
    meff = k.sbuf("meff", [128, 64], F32)
    dtmp = k.sbuf("dtmp", [65, 512], F32)
    rd16 = k.sbuf("rd16", [16, T], F32)
    xc_buf = x_tt[0]
    tmp_y = x_tt[1]

    pt0, pt1, psp = k.bk("A0"), k.bk("A1"), k.bk("A2")
    pav = k.bk("W0")
    pbc = k.bk("A3")

    ev_ipw = p.op("sync", lambda e: e.dma_start(out=sgv[:], in_=ipw_h),
                  dma="cst2", label="ld_ipw")
    ev_c2 = [Ev("cst2", p.cnt["cst2"]), k.use("consts")]

    ev_sg = p.op("scalar", lambda e: e.activation(sgv[:], sgv[:], AF.Sigmoid),
                 deps=[ev_c2], label="sig")
    ev_b = p.op("tensor", lambda e: e.matmul(psp.h[:, 0:1], k.sb["c_ones1"][:],
                sgv[:], start=True, stop=True),
                deps=[ev_sg, k.use("consts"), k.use(psp.name)], label="sgb")
    ev_sg128 = p.op("vector", lambda e: e.tensor_copy(sg128[:], psp.h[:, 0:1]),
                    deps=[ev_b], label="sgev")
    k.rel(psp.name, ev_sg128)
    ev_s1 = p.op("vector", lambda e: e.tensor_scalar(
        s1m[:], sg128[:], -1.0, 1.0, ALU.mult, ALU.add), deps=[ev_sg128],
        label="s1m")
    ev_me = p.op("vector", lambda e: e.tensor_scalar(
        meff[:], k.sb["c_palace"][:], s1m[:, 0:1], sg128[:, 0:1],
        ALU.mult, ALU.add), deps=[ev_s1, k.use("consts")], label="meff")

    for tt in range(TT):
        tsl = slice(tt * 512, (tt + 1) * 512)
        x_ev = []
        for c in range(NCH):
            x_ev.append(p.op("sync", lambda e, c=c, tsl=tsl: e.dma_start(
                out=x_tt[c][:], in_=x_h[c, :, tsl]),
                deps=[k.use(f"x{c}")], dma="xld", label=f"ldx{c}_{tt}"))
        x_ev = [x_ev[-1]] * NCH
        xn_ev = emit_ln(k, [t[:] for t in x_tt], x_ev,
                        [t[:, tsl] for t in xn], pt0, pt1, psp, f"ln{tt}")
        for c in range(NCH):
            k.rel(f"x{c}", xn_ev[c])
    ev_xn = p.mark("vector")
    k.rel("xc_buf", ev_xn)
    k.rel("tmp_y", ev_xn)

    wv_sb, ev_wv = load_full(k, "wv", wv_h, 8, 1024, "sync", bufs=1)
    ev_vset = []
    for tc in range(16):
        ev_vset.append(p.op("vector", lambda e, tc=tc: e.memset(
            V_sb[tc].ap().rearrange("p (h s) -> p h s", s=65)[:, :, 64:65],
            1.0), label=f"vones{tc}"))
    for tc in range(16):
        csl = slice((tc % 4) * 128, (tc % 4) * 128 + 128)
        for half in range(2):
            bank = (pt0, pt1)[half]
            ev = None
            for kc in range(8):
                lhs = xn[kc][:, tc * 128:tc * 128 + 128]
                rhs = wv_sb[:, kc * 1024 + half * 512:kc * 1024 + half * 512 + 512]
                ev = p.op("tensor", lambda e, lhs=lhs, rhs=rhs, bank=bank,
                          kc=kc: e.matmul(bank.h[:, 0:512], lhs, rhs,
                                          start=(kc == 0), stop=(kc == 7)),
                          deps=[ev_wv, ev_xn] +
                          ([k.use(bank.name)] if kc == 0 else []),
                          label=f"V{tc}_{half}")
            dst = V_sb[tc].ap().rearrange("p (h s) -> p h s", s=65)[
                :, half * 8:half * 8 + 8, 0:64]
            src = bank.h.ap()[:, 0:512].rearrange("p (h s) -> p h s", s=64)
            ev_e = p.op("vector", lambda e, dst=dst, src=src: e.tensor_copy(
                dst, src), deps=[ev, ev_vset[tc]], label=f"Vev{tc}_{half}")
            k.rel(bank.name, ev_e)
    ev_V = p.mark("vector")

    plan_qk = []
    for hg_ in range(4):
        plan_qk += [2 * hg_, 2 * hg_ + 1, 8 + 2 * hg_, 8 + 2 * hg_ + 1]
    ws_qk = WCols(k, "wqk", wqk_h, plan_qk, 1024, "sync", stg_bufs=1)
    seq_qk = 0
    for hg in range(4):
        for kind in range(2):
            ws = ws_qk
            dst = (q_hm, k_hm)[kind]
            for oci in range(2):
                oc = 2 * hg + oci
                col, evw = ws.get(seq_qk)
                last = None
                for tt in range(TT):
                    tsl = slice(tt * 512, (tt + 1) * 512)
                    bank = (pt0, pt1)[tt % 2]
                    ev = None
                    for kc in range(8):
                        lhst = ws.tile(col, kc)
                        rhst = xn[kc][:, tsl]
                        ev = p.op("tensor", lambda e, lhst=lhst, rhst=rhst,
                                  bank=bank, kc=kc: e.matmul(
                                      bank.h[:, 0:512], lhst, rhst,
                                      start=(kc == 0), stop=(kc == 7)),
                                  deps=[evw, ev_xn] +
                                  ([k.use(bank.name)] if kc == 0 else []),
                                  label=f"p{kind}_{oc}_{tt}")
                    h0 = 2 * oci
                    ev1 = p.op("scalar", lambda e, bank=bank, dst=dst, h0=h0,
                               tsl=tsl: e.activation(
                                   dst[0:64, h0 * T:h0 * T + T][:, tsl],
                                   bank.h[0:64, 0:512], AF.Identity),
                               deps=[ev, k.use(f"hm{kind}_{h0}")],
                               label=f"pe{kind}_{oc}_{tt}a")
                    ev2 = p.op("scalar", lambda e, bank=bank,
                               tsl=tsl: e.activation(
                                   stg_odd[64:128, tsl],
                                   bank.h[64:128, 0:512], AF.Identity),
                               deps=[ev, k.use("stgodd")],
                               label=f"pe{kind}_{oc}_{tt}b")
                    k.rel(bank.name, [ev1, ev2])
                    last = [ev1, ev2]
                ws.release(seq_qk, Ev("tensor", p.cnt["tensor"]))
                seq_qk += 1
                dst_ap = dst.ap().rearrange(
                    "p (oc pb t) -> p oc pb t", pb=2, t=T)[:, oci, 1, :]
                ev_dma = p.op("gpsimd", lambda e, dst_ap=dst_ap: e.dma_start(
                    out=dst_ap, in_=stg_odd[64:128, :]),
                    deps=[last, k.use(f"hmodd{kind}")],
                    dma=f"hmg{kind}", label=f"hmdma{kind}_{hg}_{oci}")
                k.rel("stgodd", ev_dma)
                k.rel(f"podd{kind}_{oci}", ev_dma)
            ev_dma_all = [k.use(f"podd{kind}_{oci}") for oci in range(2)]
            k.rel(f"hmk{kind}", ev_dma_all)

        ev_hm = [k.use("hmk0"), k.use("hmk1")]

        for hi in range(4):
            h = 4 * hg + hi
            for qt in range(TT):
                qsl = slice(qt * 512, qt * 512 + 512)
                ev_av = None
                LAG = 2
                pend = []

                def emit_av(kq, evq):
                    lhsv = V_sb[kq][:, 65 * h:65 * h + 65]
                    ev = p.op("tensor", lambda e, lhsv=lhsv, eb=E_sb[kq % 4],
                              kq=kq: e.matmul(pav.h[:, 0:512], lhsv, eb[:],
                                              start=(kq == 0),
                                              stop=(kq == 15)),
                              deps=[evq, ev_V] +
                              ([k.use(pav.name)] if kq == 0 else []),
                              label=f"AV{h}_{qt}_{kq}")
                    k.rel(f"E{kq % 4}", ev)
                    return ev

                for kc in range(16):
                    bank = (pt0, pt1, psp)[kc % 3]
                    lhsp = k_hm[0:64, hi * T + kc * 128: hi * T + kc * 128 + 128]
                    rhsp = q_hm[0:64, hi * T:hi * T + T][:, qsl]
                    ev_pt = p.op("tensor", lambda e, lhsp=lhsp, rhsp=rhsp,
                                 bank=bank: e.matmul(bank.h[:, 0:512], lhsp,
                                                     rhsp, start=True,
                                                     stop=True),
                                 deps=ev_hm + [k.use(bank.name)],
                                 label=f"PT{h}_{qt}_{kc}")
                    eb = E_sb[kc % 4]
                    ep = Epre[kc % 2]
                    ev_pre = p.op("vector", lambda e, bank=bank, ep=ep: (
                        e.scalar_tensor_tensor(
                            ep.ap().rearrange("p (r c) -> p r c", c=64),
                            bank.h.ap()[:, 0:512].rearrange(
                                "p (r c) -> p r c", c=64),
                            400.0,
                            meff.ap().unsqueeze(1).broadcast_to([128, 8, 64]),
                            ALU.min, ALU.mult)),
                        deps=[ev_pt, ev_me, k.use(f"Epre{kc % 2}")],
                        label=f"Epre{h}_{qt}_{kc}")
                    k.rel(bank.name, ev_pre)
                    ev_exp = p.op("scalar", lambda e, eb=eb, ep=ep:
                                  e.activation(eb[:], ep[:], AF.Exp,
                                               scale=0.125),
                                  deps=[ev_pre, k.use(f"E{kc % 4}")],
                                  label=f"exp{h}_{qt}_{kc}")
                    k.rel(f"Epre{kc % 2}", ev_exp)
                    pend.append((kc, ev_exp))
                    if len(pend) > LAG:
                        ev_av = emit_av(*pend.pop(0))
                for kq, evq in pend:
                    ev_av = emit_av(kq, evq)
                ev_oe = p.op("vector", lambda e, hi=hi, qsl=qsl: e.tensor_copy(
                    o_hm[0:64, hi * T:hi * T + T][:, qsl], pav.h[0:64, 0:512]),
                    deps=[ev_av, k.use("o_hm")], label=f"oev{h}_{qt}")
                ev_de = p.op("vector", lambda e: e.tensor_copy(
                    dtmp[64:65, 0:512], pav.h[64:65, 0:512]),
                    deps=[ev_av, k.use("dtmp")], label=f"dev{h}_{qt}")
                k.rel(pav.name, [ev_oe, ev_de])
                ev_dd = p.op("gpsimd", lambda e, hi=hi, qsl=qsl: e.dma_start(
                    out=rd16[hi:hi + 1, qsl], in_=dtmp[64:65, 0:512]),
                    deps=[ev_de, k.use("rd16")], dma="dq",
                    label=f"dmv{h}_{qt}")
                k.rel("dtmp", ev_dd)
                k.rel("rd_part", ev_dd)

        ev_rd = p.op("vector", lambda e: e.reciprocal(rd16[0:4, :],
                     rd16[0:4, :]), deps=[k.use("rd_part")], label=f"rcp{hg}")

        ev_scan = p.mark("vector")
        ohm_e = o_hm.ap().rearrange("p (hp pb t) -> p hp pb t", pb=2, t=T)
        ev_oe1 = p.op("vector", lambda e: e.tensor_copy(
            oTall.ap()[0:64, :].rearrange("p (hp t) -> p hp t", t=T),
            ohm_e[:, :, 0, :]),
            deps=[ev_scan, k.use("oTall")], label=f"oTe{hg}")
        ev_oe2 = p.op("gpsimd", lambda e: e.dma_start(
            out=oTall.ap()[64:128, :].rearrange("p (hp t) -> p hp t", t=T),
            in_=ohm_e[:, :, 1, :]),
            deps=[ev_scan, k.use("oTall")], dma="oTq", label=f"oTo{hg}")
        ev_oT = [ev_oe1, ev_oe2]
        k.rel("o_hm", ev_oT)

        for hpl in range(2):
            c = 2 * hg + hpl
            for qt in range(TT):
                qsl = slice(qt * 512, qt * 512 + 512)
                ev_x = p.op("sync", lambda e, c=c, qsl=qsl: e.dma_start(
                    out=xc_buf[:], in_=x_h[c, :, qsl]),
                    deps=[k.use("xc_buf")], dma="xr", label=f"ldxr{c}_{qt}")
                ev_bc = p.op("tensor", lambda e, hpl=hpl, qsl=qsl: e.matmul(
                    pbc.h[:, 0:512], sel8[0:4, 128 * hpl:128 * hpl + 128],
                    rd16[0:4, qsl], start=True, stop=True),
                    deps=[ev_rd, ev_c2, k.use(pbc.name)],
                    label=f"bc{hg}_{hpl}_{qt}")
                ev_t = p.op("vector", lambda e, hpl=hpl, qsl=qsl: e.tensor_mul(
                    tmp_y[:], oTall[:, hpl * T:hpl * T + T][:, qsl],
                    pbc.h[:, 0:512]),
                    deps=[ev_bc, ev_oT, k.use("tmp_y")],
                    label=f"div{hg}_{hpl}_{qt}")
                k.rel(pbc.name, ev_t)
                ev_add = p.op("vector", lambda e: e.tensor_add(
                    xc_buf[:], xc_buf[:], tmp_y[:]),
                    deps=[ev_t, ev_x], label=f"radd{hg}_{hpl}_{qt}")
                k.rel("tmp_y", ev_add)
                p.op("gpsimd", lambda e, c=c, qsl=qsl: e.dma_start(
                    out=y_h[c, :, qsl], in_=xc_buf[:]), deps=[ev_add],
                    dma="st", label=f"sty{c}_{qt}")
                k.rel("xc_buf", Ev("st", p.cnt["st"]))
        k.rel("oTall", p.mark("vector"))
        k.rel("rd16", p.mark("tensor"))
        k.rel("rd_part", None)
        k.rel("hmodd0", p.mark("tensor"))
        k.rel("hmodd1", p.mark("tensor"))
        for kind in range(2):
            for h0 in range(4):
                k.rel(f"hm{kind}_{h0}", p.mark("tensor"))

    if standalone:
        p.finalize(k.nc, [Ev("st", p.cnt["st"])])
        return k.nc



def build_moe(pfx, E, F, topk, k=None, x_h=None, y_h=None):
    OC1 = F // 128
    NPARTS = F // 1024
    standalone = k is None
    if standalone:
        k = K()
    k.set_banks("moe")
    p = k.p
    if x_h is None:
        x_h = k.dram_in("x", [D, T]).rearrange("(c p) t -> c p t", p=128)
        y_h = k.dram_out("y", [D, T]).rearrange("(c p) t -> c p t", p=128)
    wr_h = k.dram_in(f"{pfx}_wr", [128, 8 * E])
    w1_h = k.dram_in(f"{pfx}_w1", [E * OC1, 128, 1024])
    b1_h = k.dram_in(f"{pfx}_b1", [128, E * OC1])
    w2_h = k.dram_in(f"{pfx}_w2", [E * NPARTS * 8, 128, 1024])
    b2_h = k.dram_in(f"{pfx}_b2", [128, 8 * E])
    k.load_consts(("c_ones1", "c_onesb", "c_identf", "c_sel16"))

    x_tt = [k.sbuf(f"x{c}", [128, 512], F32) for c in range(NCH)]
    xn = [k.sbuf(f"xn{c}", [128, T], BF16) for c in range(NCH)]
    delta = [k.sbuf(f"delta{c}", [128, T], F32) for c in range(NCH)]
    hT = [k.sbuf(f"hT{j}", [128, T], BF16) for j in range(8)]
    wr_sb = k.sbuf("wr_sb", [128, 8 * E], BF16)
    wr_stg = k.sbuf("wr_stg", [128, 8 * E], F32)
    b1 = k.sbuf("b1", [128, E * OC1], F32)
    b2 = k.sbuf("b2", [128, 8 * E], F32)
    g_rowsb = k.sbuf("g_rowsb", [16, T], BF16)
    exps = k.sbuf("exps", [128, E], F32)
    mask1 = k.sbuf("mask1", [128, E], F32)
    rsum = k.sbuf("rsum", [128, 2], F32)
    gtmp = k.sbuf("gtmp", [128, E], F32)
    gb_sb = k.sbuf("gb_sb", [128, T], BF16)
    ytmp_v = k.sbuf("ytmp_v", [128, 512], F32)

    pt0, pt1, psp = k.bk("A0"), k.bk("A1"), k.bk("A2")
    pgt = k.bk("A3")

    ev_cst2 = []
    for ap, sb in ((wr_h, wr_stg), (b1_h, b1), (b2_h, b2)):
        ev_cst2.append(p.op("sync", lambda e, ap=ap, sb=sb: e.dma_start(
            out=sb[:], in_=ap), dma="cst2", label="ldc2"))
    ev_c2 = Ev("cst2", p.cnt["cst2"])
    ev_wr = p.op("vector", lambda e: e.tensor_copy(wr_sb[:], wr_stg[:]),
                 deps=[ev_c2], label="wrcast")

    for tt in range(TT):
        tsl = slice(tt * 512, (tt + 1) * 512)
        x_ev = []
        for c in range(NCH):
            x_ev.append(p.op("sync", lambda e, c=c, tsl=tsl: e.dma_start(
                out=x_tt[c][:], in_=x_h[c, :, tsl]),
                deps=[k.use(f"x{c}")], dma="xld", label=f"ldx{c}_{tt}"))
        x_ev = [x_ev[-1]] * NCH
        xn_ev = emit_ln(k, [t[:] for t in x_tt], x_ev,
                        [t[:, tsl] for t in xn], pt0, pt1, psp, f"ln{tt}")
        for c in range(NCH):
            k.rel(f"x{c}", xn_ev[c])
    ev_xn = p.mark("vector")

    ev_gz = p.op("vector", lambda e: e.memset(g_rowsb[:], 0.0), label="gz")
    for tc in range(16):
        ev = None
        for kc in range(8):
            lhs = xn[kc][:, tc * 128:tc * 128 + 128]
            rhs = wr_sb[:, kc * E:kc * E + E]
            ev = p.op("tensor", lambda e, lhs=lhs, rhs=rhs, kc=kc: e.matmul(
                pt0.h[:, 0:E], lhs, rhs, start=(kc == 0), stop=(kc == 7)),
                deps=[ev_wr, ev_xn] + ([k.use(pt0.name)] if kc == 0 else []),
                label=f"rt{tc}")
        ps = pt0.h[:, 0:E]
        ev_mx = p.op("vector", lambda e: e.reduce_max(
            rsum[:, 1:2], ps, axis=AX, negate=True),
            deps=[ev, k.use("rsum")], label=f"rmx{tc}")
        ev_ex = p.op("scalar", lambda e: e.activation(
            exps[:], ps, AF.Exp, bias=rsum[:, 1:2], accum_out=rsum[:, 0:1]),
            deps=[ev_mx, k.use("exps")], label=f"rex{tc}")
        ev_m1 = p.op("vector", lambda e: e.tensor_scalar(
            mask1[:], ps, rsum[:, 1:2], 0.0, ALU.add, ALU.is_ge),
            deps=[ev_ex, k.use("mask1")], label=f"rm1{tc}")
        if topk == 2:
            ev_s2 = p.op("vector", lambda e: e.scalar_tensor_tensor(
                gtmp[:], mask1[:], -1e30, ps, ALU.mult, ALU.add),
                deps=[ev_m1, k.use("gtmp")], label=f"rs2{tc}")
            ev_mx2 = p.op("vector", lambda e: e.reduce_max(
                rsum[:, 1:2], gtmp[:], axis=AX, negate=True),
                deps=[ev_s2], label=f"rmx2{tc}")
            ev_m2 = p.op("vector", lambda e: e.tensor_scalar(
                gtmp[:], gtmp[:], rsum[:, 1:2], 0.0, ALU.add, ALU.is_ge),
                deps=[ev_mx2], label=f"rm2{tc}")
            ev_m1 = p.op("vector", lambda e: e.tensor_add(
                mask1[:], mask1[:], gtmp[:]), deps=[ev_m2], label=f"rms{tc}")
        k.rel(pt0.name, ev_m1)
        ev_rc = p.op("vector", lambda e: e.reciprocal(
            rsum[:, 0:1], rsum[:, 0:1]), deps=[ev_ex, ev_m1], label=f"rrc{tc}")
        ev_g1 = p.op("vector", lambda e: e.tensor_mul(
            gtmp[:], exps[:], mask1[:]),
            deps=[ev_rc, k.use("gtmp")], label=f"rg1{tc}")
        ev_g2 = p.op("vector", lambda e: e.tensor_scalar_mul(
            gtmp[:], gtmp[:], rsum[:, 0:1]), deps=[ev_g1], label=f"rg2{tc}")
        k.rel("rsum", ev_g2)
        k.rel("exps", ev_g2)
        k.rel("mask1", ev_g2)
        ev_tr = p.op("tensor", lambda e: e.transpose(
            pgt.h[0:E, 0:128], gtmp[:], k.sb["c_identf"][:]),
            deps=[ev_g2, k.use(pgt.name), k.use("consts")], label=f"rtr{tc}")
        ev_ge = p.op("vector", lambda e, tc=tc: e.tensor_copy(
            g_rowsb[0:E, tc * 128:tc * 128 + 128], pgt.h[0:E, 0:128]),
            deps=[ev_tr, ev_gz], label=f"rge{tc}")
        k.rel(pgt.name, ev_ge)
        k.rel("gtmp", ev_tr)
    ev_gates = p.mark("vector")

    ws1 = WCols(k, "w1", w1_h, list(range(E * OC1)), 1024, "sync",
                stg_bufs=2, col_bufs=2)
    ws2 = WCols(k, "w2", w2_h, list(range(E * NPARTS * 8)), 1024, "gpsimd",
                stg_bufs=2, col_bufs=2)
    slot = [0]

    def quads():
        qs = [k.bk(f"A{4 * (slot[0] % 2) + i}") for i in range(4)]
        slot[0] += 1
        return qs

    s1 = s2 = 0
    gb_ev = [None] * 4
    dev = {}
    for e_ in range(E):
        sel_ap = k.sb["c_sel16"][:, 128 * e_:128 * e_ + 128]
        qs = quads()
        for q in range(4):
            qsl = slice(q * 512, q * 512 + 512)
            bank = qs[q]
            ev_b = p.op("tensor", lambda e, bank=bank, qsl=qsl,
                        sel_ap=sel_ap: e.matmul(
                            bank.h[:, 0:512], sel_ap, g_rowsb[0:16, qsl],
                            start=True, stop=True),
                        deps=[ev_gates, k.use("consts"), k.use(bank.name)],
                        label=f"gb{e_}_{q}")
            ev_e = p.op("vector", lambda e, bank=bank, qsl=qsl: e.tensor_copy(
                gb_sb[:, qsl], bank.h[:, 0:512]),
                deps=[ev_b, k.use(f"gbq{q}")], label=f"gbe{e_}_{q}")
            k.rel(bank.name, ev_e)
            gb_ev[q] = ev_e
        for part in range(NPARTS):
            for j in range(8):
                ft = part * 8 + j
                col, evw = ws1.get(s1)
                qs = quads()
                tiles = [col[:, kc * 128:kc * 128 + 128] for kc in range(8)]

                def em1(e, tiles=tiles, qs=qs):
                    for kc in range(8):
                        for q in range(4):
                            inst = e.matmul(
                                qs[q].h[:, 0:512], tiles[kc],
                                xn[kc][:, q * 512:q * 512 + 512],
                                start=(kc == 0), stop=(kc == 7))
                    return inst
                ev_mm = p.op("tensor", em1,
                             deps=[evw, ev_xn] + [k.use(b.name) for b in qs],
                             label=f"s1_{e_}_{ft}")
                ws1.release(s1, Ev("tensor", p.cnt["tensor"]))
                s1 += 1
                bcol = b1[:, e_ * OC1 + ft:e_ * OC1 + ft + 1]
                for q in range(4):
                    qsl = slice(q * 512, q * 512 + 512)
                    ev_g = p.op("scalar", lambda e, bank=qs[q], qsl=qsl,
                                bcol=bcol, j=j: e.activation(
                                    hT[j][:, qsl], bank.h[:, 0:512], AF.Gelu,
                                    bias=bcol),
                                deps=[ev_mm, ev_c2, k.use("hTr")],
                                label=f"s1e{e_}_{ft}_{q}")
                    k.rel(qs[q].name, ev_g)
            ev_h = p.mark("scalar")
            for oc2 in range(8):
                col2, evw2 = ws2.get(s2)
                qs = quads()
                tiles2 = [col2[:, kc * 128:kc * 128 + 128] for kc in range(8)]

                def em2(e, tiles2=tiles2, qs=qs):
                    for kc in range(8):
                        for q in range(4):
                            inst = e.matmul(
                                qs[q].h[:, 0:512], tiles2[kc],
                                hT[kc][:, q * 512:q * 512 + 512],
                                start=(kc == 0), stop=(kc == 7))
                    return inst
                ev_mm2 = p.op("tensor", em2,
                              deps=[evw2, ev_h] + [k.use(b.name) for b in qs],
                              label=f"s2_{e_}_{part}_{oc2}")
                ws2.release(s2, Ev("tensor", p.cnt["tensor"]))
                s2 += 1
                b2col = b2[:, e_ * 8 + oc2:e_ * 8 + oc2 + 1]
                for q in range(4):
                    qsl = slice(q * 512, q * 512 + 512)
                    bank = qs[q]
                    if part == 0 and e_ == 0:
                        ev_d = p.op("vector", lambda e, bank=bank, qsl=qsl,
                                    oc2=oc2, b2col=b2col:
                                    e.scalar_tensor_tensor(
                                        delta[oc2][:, qsl], bank.h[:, 0:512],
                                        b2col, gb_sb[:, qsl],
                                        ALU.add, ALU.mult),
                                    deps=[ev_mm2, ev_c2, gb_ev[q]],
                                    label=f"s2d{e_}_{oc2}_{q}")
                    else:
                        if part == 0:
                            ev_t = p.op("vector", lambda e, bank=bank, qsl=qsl,
                                        b2col=b2col: e.scalar_tensor_tensor(
                                            ytmp_v[:], bank.h[:, 0:512],
                                            b2col, gb_sb[:, qsl],
                                            ALU.add, ALU.mult),
                                        deps=[ev_mm2, ev_c2, gb_ev[q],
                                              k.use("ytmp_v")],
                                        label=f"s2b{e_}_{oc2}_{q}")
                        else:
                            ev_t = p.op("vector", lambda e, bank=bank, qsl=qsl:
                                        e.tensor_mul(ytmp_v[:],
                                                     bank.h[:, 0:512],
                                                     gb_sb[:, qsl]),
                                        deps=[ev_mm2, gb_ev[q],
                                              k.use("ytmp_v")],
                                        label=f"s2t{e_}_{part}_{oc2}_{q}")
                        ev_d = p.op("vector", lambda e, qsl=qsl, oc2=oc2:
                                    e.tensor_add(delta[oc2][:, qsl],
                                                 delta[oc2][:, qsl], ytmp_v[:]),
                                    deps=[ev_t, dev.get((oc2, q))],
                                    label=f"s2a{e_}_{part}_{oc2}_{q}")
                        k.rel("ytmp_v", ev_d)
                    dev[(oc2, q)] = ev_d
                    k.rel(f"gbq{q}", ev_d)
                    k.rel(bank.name, ev_d)
            k.rel("hTr", Ev("tensor", p.cnt["tensor"]))

    for c in range(NCH):
        for tt in range(TT):
            tsl = slice(tt * 512, (tt + 1) * 512)
            ev_x = p.op("sync", lambda e, c=c, tsl=tsl: e.dma_start(
                out=x_tt[c][:], in_=x_h[c, :, tsl]),
                deps=[k.use(f"x{c}")], dma=f"xr{c}", label=f"ldxr{c}_{tt}")
            ev_a = p.op("vector", lambda e, c=c, tsl=tsl: e.tensor_add(
                x_tt[c][:], x_tt[c][:], delta[c][:, tsl]),
                deps=[ev_x, dev[(c, tt)]], label=f"fadd{c}_{tt}")
            p.op("gpsimd", lambda e, c=c, tsl=tsl: e.dma_start(
                out=y_h[c, :, tsl], in_=x_tt[c][:]), deps=[ev_a],
                dma="st", label=f"fst{c}_{tt}")
            k.rel(f"x{c}", Ev("st", p.cnt["st"]))

    if standalone:
        p.finalize(k.nc, [Ev("st", p.cnt["st"])])
        return k.nc


def build_l0_ffn(**kw):
    return build_moe("l0f", 8, 4096, 1, **kw)


def build_l2_ffn(**kw):
    return build_moe("l2f", 8, 4096, 2, **kw)


def build_l3_ffn(**kw):
    return build_moe("l3f", 14, 2048, 1, **kw)



def build_l0_attn(k=None, x_h=None, y_h=None):
    standalone = k is None
    if standalone:
        k = K()
    p = k.p
    if x_h is None:
        x_h = k.dram_in("x", [D, T]).rearrange("(c p) t -> c p t", p=128)
        y_h = k.dram_out("y", [D, T]).rearrange("(c p) t -> c p t", p=128)
    wqkv_h = k.dram_in("l0_wqkv", [24, 128, 1024])
    wo_h = k.dram_in("l0_wo", [8, 128, 1024])
    k.load_consts(("c_ones1", "c_onesb", "c_identb", "c_triu", "c_tril"))

    x_tt = [k.sbuf(f"x{c}", [128, 512], F32) for c in range(NCH)]
    xn_tt = [k.sbuf(f"xn{c}", [128, 512], BF16) for c in range(NCH)]
    q_hm = k.sbuf("q_hm", [64, 16 * 512], BF16)
    k_hm = k.sbuf("k_hm", [64, 16 * 512], BF16)
    v_hm = k.sbuf("v_hm", [64, 16 * 512], BF16)
    stg_odd = k.sbuf("stg_odd", [128, 8 * 512], BF16)
    Qtok = k.sbuf("Qtok", [128, 512], BF16)
    Ktok = k.sbuf("Ktok", [128, 512], BF16)
    Vtok = k.sbuf("Vtok", [128, 512], BF16)
    PT_sb = k.sbuf("PT_sb", [128, 1024], BF16)
    WT_sb = k.sbuf("WT_sb", [128, 1024], BF16)
    W_sb = k.sbuf("W_sb", [128, 1024], BF16)
    ATm_sb = k.sbuf("ATm_sb", [128, 1024], BF16)
    C1_sb = k.sbuf("C1_sb", [64, 1024], BF16)
    Z_sb = k.sbuf("Z_sb", [128, 512], BF16)
    N1_sb = k.sbuf("N1_sb", [64, 512], BF16)
    S32 = k.sbuf("S32", [64, 1024], F32)
    M32 = k.sbuf("M32", [64, 1024], F32)
    S_sb = k.sbuf("S_sb", [64, 1024], BF16)
    M_sb = k.sbuf("M_sb", [64, 1024], BF16)
    o_hm = k.sbuf("o_hm", [64, 16 * 512], BF16)
    oTall = k.sbuf("oTall", [128, 8 * 512], BF16)
    ytmp = k.sbuf("ytmp", [128, 512], F32)

    pM = k.bk("W0")
    pt0, pt1, psp = k.bk("A0"), k.bk("A1"), k.bk("A2")
    pkv = k.bk("P3")
    pSa, pSb = k.bk("A2"), k.bk("A3")

    ident = k.sb["c_identb"]
    triu, tril = k.sb["c_triu"], k.sb["c_tril"]

    ev_init = p.op("vector", lambda e: e.memset(S32[:], 0.0), label="s32i")
    ev_init = p.op("vector", lambda e: e.memset(M32[:], 0.0),
                   deps=[ev_init], label="m32i")
    k.rel("S32i", ev_init)

    ws_qkv = WCols(k, "wqkv", wqkv_h,
                   [pi * 8 + oc for _ in range(TT) for pi in range(3)
                    for oc in range(8)], 1024, "sync")
    ws_out = WCols(k, "wout", wo_h,
                   [oc for _ in range(TT) for oc in range(8)], 1024, "gpsimd")

    seq_qkv = seq_out = 0
    ev_upd = ev_updM = None
    for tt in range(TT):
        tsl = slice(tt * 512, (tt + 1) * 512)
        x_ev = []
        for c in range(NCH):
            x_ev.append(p.op("sync", lambda e, c=c, tsl=tsl: e.dma_start(
                out=x_tt[c][:], in_=x_h[c, :, tsl]),
                deps=[k.use(f"x{c}")], dma="xld", label=f"ldx{c}_{tt}"))
        x_ev = [x_ev[-1]] * NCH
        xn_ev = emit_ln(k, [t[:] for t in x_tt], x_ev, [t[:] for t in xn_tt],
                        pt0, pt1, psp, f"ln{tt}")

        hm_wr = {}
        for pi, dst in enumerate((q_hm, k_hm, v_hm)):
            for oc in range(8):
                col, evw = ws_qkv.get(seq_qkv)
                bank = (pt0, pt1)[oc % 2]
                ev = None
                for kc in range(8):
                    lhst = ws_qkv.tile(col, kc)
                    rhst = xn_tt[kc][:]
                    ev = p.op("tensor", lambda e, lhst=lhst, rhst=rhst,
                              bank=bank, kc=kc: e.matmul(
                                  bank.h[:, 0:512], lhst, rhst,
                                  start=(kc == 0), stop=(kc == 7)),
                              deps=[evw, xn_ev[kc]] +
                              ([k.use(bank.name)] if kc == 0 else []),
                              label=f"pj{pi}_{oc}_{tt}")
                h0 = 2 * oc
                ev1 = p.op("scalar", lambda e, bank=bank, dst=dst, h0=h0:
                           e.activation(dst[0:64, 512 * h0:512 * h0 + 512],
                                        bank.h[0:64, 0:512], AF.Identity),
                           deps=[ev, k.use(f"hm{pi}_{h0}")],
                           label=f"pe{pi}_{oc}a")
                ev2 = p.op("scalar", lambda e, bank=bank, oc=oc:
                           e.activation(
                               stg_odd[64:128, 512 * oc:512 * oc + 512],
                               bank.h[64:128, 0:512], AF.Identity),
                           deps=[ev, k.use(f"stgodd_{oc}")],
                           label=f"pe{pi}_{oc}b")
                k.rel(bank.name, [ev1, ev2])
                hm_wr[(pi, h0)] = ev1
                hm_wr[(pi, 2 * oc + 1, "stg")] = ev2
                ws_qkv.release(seq_qkv, Ev("tensor", p.cnt["tensor"]))
                seq_qkv += 1
            src_ap = stg_odd.ap()[64:128, :].rearrange(
                "p (oc t) -> p oc t", t=512)
            dst_ap = dst.ap().rearrange("p (oc pb t) -> p oc pb t",
                                        pb=2, t=512)[:, :, 1, :]
            odd_deps = [hm_wr[(pi, 2 * oc + 1, "stg")] for oc in range(8)]
            ev_dma = p.op("gpsimd", lambda e, src_ap=src_ap, dst_ap=dst_ap:
                          e.dma_start(out=dst_ap, in_=src_ap),
                          deps=odd_deps + [k.use(f"hmodd{pi}")],
                          dma=f"hmg{pi}", label=f"hmdma{pi}_{tt}")
            for oc in range(8):
                k.rel(f"stgodd_{oc}", ev_dma)
            for h in range(16):
                if h % 2 == 1:
                    hm_wr[(pi, h)] = ev_dma
        hm = lambda pi, h: hm_wr[(pi, h)]

        for jj in range(4):
            j = tt * 4 + jj
            ev_Sprev = k.use("S_ready")
            ev_Mprev = k.use("M_ready")
            for g in range(2):
                heads = list(range(8 * g, 8 * g + 8))
                hsl = lambda pi_, h_: slice(512 * h_ + 128 * jj,
                                            512 * h_ + 128 * jj + 128)
                ev_t1 = None
                for i_h, h in enumerate(heads):
                    qin = q_hm[0:64, hsl(0, h)]
                    kin = k_hm[0:64, hsl(1, h)]
                    ev_t1 = p.op("tensor", lambda e, qin=qin, i_h=i_h:
                                 e.transpose(pkv.h[:, 64 * i_h:64 * i_h + 64],
                                             qin, ident[0:64, 0:64]),
                                 deps=[hm(0, h), k.use(pkv.name),
                                       k.use("consts")], label=f"trQ{j}_{h}")
                    ev_t1 = p.op("tensor", lambda e, kin=kin, i_h=i_h:
                                 e.transpose(
                                     pkv.h[:, 512 + 64 * i_h:576 + 64 * i_h],
                                     kin, ident[0:64, 0:64]),
                                 deps=[hm(1, h)], label=f"trK{j}_{h}")
                ev_Qe = p.op("vector", lambda e: e.tensor_copy(
                    Qtok[:], pkv.h[:, 0:512]),
                    deps=[ev_t1, k.use("Qtok")], label=f"Qev{j}_{g}")
                ev_Ke = p.op("vector", lambda e: e.tensor_copy(
                    Ktok[:], pkv.h[:, 512:1024]),
                    deps=[ev_t1, k.use("Ktok")], label=f"Kev{j}_{g}")
                k.rel(pkv.name, [ev_Qe, ev_Ke])
                ev_t2 = None
                for i_h, h in enumerate(heads):
                    vin = v_hm[0:64, hsl(2, h)]
                    ev_t2 = p.op("tensor", lambda e, vin=vin, i_h=i_h:
                                 e.transpose(pkv.h[:, 64 * i_h:64 * i_h + 64],
                                             vin, ident[0:64, 0:64]),
                                 deps=[hm(2, h), k.use(pkv.name)],
                                 label=f"trV{j}_{h}")
                ev_Ve = p.op("vector", lambda e: e.tensor_copy(
                    Vtok[:], pkv.h[:, 0:512]),
                    deps=[ev_t2, k.use("Vtok")], label=f"Vev{j}_{g}")
                k.rel(pkv.name, ev_Ve)

                ev_pt = None
                for i_h, h in enumerate(heads):
                    bank = (pt0, pt1)[i_h // 4]
                    co = 128 * (i_h % 4)
                    kin = k_hm[0:64, hsl(1, h)]
                    qin = q_hm[0:64, hsl(0, h)]
                    ev_pt = p.op("tensor", lambda e, kin=kin, qin=qin,
                                 bank=bank, co=co: e.matmul(
                                     bank.h[:, co:co + 128], kin, qin,
                                     start=True, stop=True),
                                 deps=[hm(0, h), hm(1, h),
                                       k.use(bank.name)] if i_h % 4 == 0
                                 else [hm(0, h), hm(1, h)],
                                 label=f"PT{j}_{h}")
                ev_c0 = p.op("vector", lambda e: e.tensor_copy(
                    PT_sb[:, 0:512], pt0.h[:, 0:512]),
                    deps=[ev_pt, k.use("PT_sb")], label=f"PTe{j}0")
                ev_c1 = p.op("vector", lambda e: e.tensor_copy(
                    PT_sb[:, 512:1024], pt1.h[:, 0:512]),
                    deps=[ev_pt], label=f"PTe{j}1")
                ev_w0 = p.op("vector", lambda e: e.tensor_tensor(
                    WT_sb.ap()[:, 0:512].rearrange("p (r c) -> p r c", c=128),
                    pt0.h.ap()[:, 0:512].rearrange("p (r c) -> p r c", c=128),
                    triu.ap().unsqueeze(1).broadcast_to([128, 4, 128]),
                    ALU.mult), deps=[ev_pt, k.use("WT_sb")],
                    label=f"WTm{j}0")
                ev_w1 = p.op("vector", lambda e: e.tensor_tensor(
                    WT_sb.ap()[:, 512:1024].rearrange("p (r c) -> p r c",
                                                      c=128),
                    pt1.h.ap()[:, 0:512].rearrange("p (r c) -> p r c", c=128),
                    triu.ap().unsqueeze(1).broadcast_to([128, 4, 128]),
                    ALU.mult), deps=[ev_pt], label=f"WTm{j}1")
                k.rel(pt0.name, [ev_c0, ev_w0])
                k.rel(pt1.name, [ev_c1, ev_w1])
                ev_PT = [ev_c0, ev_c1]
                ev_WT = [ev_w0, ev_w1]
                ev_p = None
                for i_h, h in enumerate(heads):
                    bank = (pt0, pt1)[i_h // 4]
                    co = 128 * (i_h % 4)
                    kin = k_hm[0:64, hsl(1, h)]
                    qin = q_hm[0:64, hsl(0, h)]
                    ev_p = p.op("tensor", lambda e, kin=kin, qin=qin,
                                bank=bank, co=co: e.matmul(
                                    bank.h[:, co:co + 128], qin, kin,
                                    start=True, stop=True),
                                deps=[k.use(bank.name)] if i_h % 4 == 0
                                else [], label=f"P{j}_{h}")
                ev_wl0 = p.op("vector", lambda e: e.tensor_tensor(
                    W_sb.ap()[:, 0:512].rearrange("p (r c) -> p r c", c=128),
                    pt0.h.ap()[:, 0:512].rearrange("p (r c) -> p r c", c=128),
                    tril.ap().unsqueeze(1).broadcast_to([128, 4, 128]),
                    ALU.mult), deps=[ev_p, k.use("W_sb")], label=f"Wm{j}0")
                ev_wl1 = p.op("vector", lambda e: e.tensor_tensor(
                    W_sb.ap()[:, 512:1024].rearrange("p (r c) -> p r c",
                                                     c=128),
                    pt1.h.ap()[:, 0:512].rearrange("p (r c) -> p r c", c=128),
                    tril.ap().unsqueeze(1).broadcast_to([128, 4, 128]),
                    ALU.mult), deps=[ev_p], label=f"Wm{j}1")
                k.rel(pt0.name, ev_wl0)
                k.rel(pt1.name, ev_wl1)
                ev_W = [ev_wl0, ev_wl1]

                ev_C1e = None
                if j > 0:
                    ev_c = None
                    for i_h, h in enumerate(heads):
                        bank = (pt0, pt1)[i_h // 4]
                        co = 128 * (i_h % 4)
                        qin = q_hm[0:64, hsl(0, h)]
                        Sh = S_sb[0:64, 64 * h:64 * h + 64]
                        ev_c = p.op("tensor", lambda e, Sh=Sh, qin=qin,
                                    bank=bank, co=co: e.matmul(
                                        bank.h[0:64, co:co + 128], Sh, qin,
                                        start=True, stop=True),
                                    deps=[ev_Sprev, hm(0, h)] +
                                    ([k.use(bank.name)] if i_h % 4 == 0
                                     else []), label=f"C1{j}_{h}")
                    ev_C10 = p.op("vector", lambda e: e.tensor_copy(
                        C1_sb[0:64, 0:512], pt0.h[0:64, 0:512]),
                        deps=[ev_c, k.use("C1_sb")], label=f"C1e{j}0")
                    ev_C11 = p.op("vector", lambda e: e.tensor_copy(
                        C1_sb[0:64, 512:1024], pt1.h[0:64, 0:512]),
                        deps=[ev_c], label=f"C1e{j}1")
                    k.rel(pt0.name, ev_C10)
                    k.rel(pt1.name, ev_C11)
                    ev_C1e = [ev_C10, ev_C11]

                ev_at = None
                for i_h, h in enumerate(heads):
                    bank = (pt0, pt1)[i_h // 4]
                    co = 128 * (i_h % 4)
                    qin = q_hm[0:64, hsl(0, h)]
                    first = True
                    if j > 0:
                        ev_at = p.op("tensor", lambda e, qin=qin, bank=bank,
                                     co=co, i_h=i_h: e.matmul(
                                         bank.h[:, co:co + 128], qin,
                                         C1_sb[0:64, 128 * i_h:128 * i_h + 128],
                                         start=True, stop=False),
                                     deps=[ev_C1e, hm(0, h)] +
                                     ([k.use(bank.name)] if i_h % 4 == 0
                                      else []), label=f"AT1{j}_{h}")
                        first = False
                    ev_at = p.op("tensor", lambda e, bank=bank, co=co,
                                 i_h=i_h, first=first: e.matmul(
                                     bank.h[:, co:co + 128],
                                     WT_sb[:, 128 * i_h:128 * i_h + 128],
                                     PT_sb[:, 128 * i_h:128 * i_h + 128],
                                     start=first, stop=True),
                                 deps=ev_WT + ev_PT +
                                 ([k.use(bank.name)]
                                  if first and i_h % 4 == 0 else []),
                                 label=f"AT2{j}_{h}")
                ev_am0 = p.op("vector", lambda e: e.tensor_tensor(
                    ATm_sb.ap()[:, 0:512].rearrange("p (r c) -> p r c", c=128),
                    pt0.h.ap()[:, 0:512].rearrange("p (r c) -> p r c", c=128),
                    triu.ap().unsqueeze(1).broadcast_to([128, 4, 128]),
                    ALU.mult), deps=[ev_at, k.use("ATm_sb")],
                    label=f"ATm{j}0")
                ev_am1 = p.op("vector", lambda e: e.tensor_tensor(
                    ATm_sb.ap()[:, 512:1024].rearrange("p (r c) -> p r c",
                                                       c=128),
                    pt1.h.ap()[:, 0:512].rearrange("p (r c) -> p r c", c=128),
                    triu.ap().unsqueeze(1).broadcast_to([128, 4, 128]),
                    ALU.mult), deps=[ev_at], label=f"ATm{j}1")
                k.rel(pt0.name, ev_am0)
                k.rel(pt1.name, ev_am1)
                ev_ATm = [ev_am0, ev_am1]

                ev_z = None
                ev_n = None
                for i_h, h in enumerate(heads):
                    ev_z = p.op("tensor", lambda e, i_h=i_h: e.matmul(
                        pt0.h[:, 64 * i_h:64 * i_h + 64],
                        W_sb[:, 128 * i_h:128 * i_h + 128],
                        Vtok[:, 64 * i_h:64 * i_h + 64],
                        start=True, stop=True),
                        deps=ev_W + [ev_Ve] +
                        ([k.use(pt0.name)] if i_h == 0 else []),
                        label=f"Z{j}_{h}")
                    ev_n = p.op("tensor", lambda e, i_h=i_h: e.matmul(
                        pt1.h[0:64, 64 * i_h:64 * i_h + 64],
                        Qtok[:, 64 * i_h:64 * i_h + 64],
                        Vtok[:, 64 * i_h:64 * i_h + 64],
                        start=True, stop=True),
                        deps=[ev_Qe, ev_Ve] +
                        ([k.use(pt1.name)] if i_h == 0 else []),
                        label=f"N1{j}_{h}")
                ev_Ze = p.op("vector", lambda e: e.tensor_copy(
                    Z_sb[:], pt0.h[:, 0:512]),
                    deps=[ev_z, k.use("Z_sb")], label=f"Ze{j}")
                ev_Ne = p.op("vector", lambda e: e.tensor_copy(
                    N1_sb[0:64, :], pt1.h[0:64, 0:512]),
                    deps=[ev_n, k.use("N1_sb")], label=f"Ne{j}")
                k.rel(pt0.name, ev_Ze)
                k.rel(pt1.name, ev_Ne)

                ev_o = None
                for i_h, h in enumerate(heads):
                    bank = (pt0, pt1)[i_h // 4]
                    co = 128 * (i_h % 4)
                    qin = q_hm[0:64, hsl(0, h)]
                    first = True
                    if j > 0:
                        Mh = M_sb[0:64, 64 * h:64 * h + 64]
                        ev_o = p.op("tensor", lambda e, Mh=Mh, qin=qin,
                                    bank=bank, co=co: e.matmul(
                                        bank.h[0:64, co:co + 128], Mh, qin,
                                        start=True, stop=False),
                                    deps=[ev_Mprev, hm(0, h)] +
                                    ([k.use(bank.name)] if i_h % 4 == 0
                                     else []), label=f"o1_{j}_{h}")
                        first = False
                    ev_o = p.op("tensor", lambda e, bank=bank, co=co,
                                i_h=i_h, first=first: e.matmul(
                                    bank.h[0:64, co:co + 128],
                                    Vtok[:, 64 * i_h:64 * i_h + 64],
                                    ATm_sb[:, 128 * i_h:128 * i_h + 128],
                                    start=first, stop=True),
                                deps=ev_ATm + [ev_Ve] +
                                ([k.use(bank.name)]
                                 if first and i_h % 4 == 0 else []),
                                label=f"o2_{j}_{h}")
                ev_oe = None
                for i_h, h in enumerate(heads):
                    bank = (pt0, pt1)[i_h // 4]
                    co = 128 * (i_h % 4)
                    do = 512 * h + 128 * jj
                    ev_oe = p.op("vector", lambda e, bank=bank, co=co, do=do:
                                 e.tensor_copy(o_hm[0:64, do:do + 128],
                                               bank.h[0:64, co:co + 128]),
                                 deps=[ev_o, k.use("o_hm")],
                                 label=f"oev{j}_{h}")
                k.rel(pt0.name, ev_oe)
                k.rel(pt1.name, ev_oe)

                pSg = pSa if g == 0 else pSb
                for i_h, h in enumerate(heads):
                    ev_upd = p.op("tensor", lambda e, i_h=i_h, pSg=pSg:
                                  e.matmul(
                                      pSg.h[0:64, 64 * i_h:64 * i_h + 64],
                                      Ktok[:, 64 * i_h:64 * i_h + 64],
                                      Ktok[:, 64 * i_h:64 * i_h + 64],
                                      start=(i_h == 0), stop=(i_h == 7)),
                                  deps=[ev_Ke] +
                                  ([k.use(pSg.name)] if i_h == 0 else []),
                                  label=f"S{j}_{h}")
                for i_h, h in enumerate(heads):
                    first = True
                    if j > 0:
                        Sh = S_sb[0:64, 64 * h:64 * h + 64]
                        ev_updM = p.op("tensor", lambda e, Sh=Sh, h=h, i_h=i_h:
                                       e.matmul(
                                           pM.h[0:64, 64 * h:64 * h + 64],
                                           Sh,
                                           N1_sb[0:64, 64 * i_h:64 * i_h + 64],
                                           start=(i_h == 0), stop=False),
                                       deps=[ev_Sprev, ev_Ne] +
                                       ([k.use(pM.name)]
                                        if i_h == 0 and g == 0 else []),
                                       label=f"M1{j}_{h}")
                        first = False
                    ev_updM = p.op("tensor", lambda e, h=h, i_h=i_h,
                                   first=first: e.matmul(
                                       pM.h[0:64, 64 * h:64 * h + 64],
                                       Ktok[:, 64 * i_h:64 * i_h + 64],
                                       Z_sb[:, 64 * i_h:64 * i_h + 64],
                                       start=first and (i_h == 0),
                                       stop=(i_h == 7)),
                                   deps=[ev_Ke, ev_Ze] +
                                   ([k.use(pM.name)]
                                    if first and i_h == 0 and g == 0 else []),
                                   label=f"M2{j}_{h}")
                k.rel("Qtok", ev_n)
                k.rel("Ktok", ev_updM)
                k.rel("Vtok", [ev_o, ev_updM])
                k.rel("PT_sb", ev_at)
                k.rel("WT_sb", ev_at)
                k.rel("W_sb", ev_z)
                k.rel("ATm_sb", ev_o)
                k.rel("C1_sb", ev_at)
                k.rel("Z_sb", ev_updM)
                k.rel("N1_sb", ev_updM)
            ev_sa = p.op("vector", lambda e: e.tensor_add(
                S32[:, 0:512], S32[:, 0:512], pSa.h[0:64, 0:512]),
                deps=[ev_upd, k.use("S32i"), ev_Sprev], label=f"Sa{j}a")
            k.rel(pSa.name, ev_sa)
            ev_sa = p.op("vector", lambda e: e.tensor_add(
                S32[:, 512:1024], S32[:, 512:1024], pSb.h[0:64, 0:512]),
                deps=[ev_upd, ev_sa], label=f"Sa{j}b")
            k.rel(pSb.name, ev_sa)
            ev_sc = p.op("vector", lambda e: e.tensor_copy(S_sb[:], S32[:]),
                         deps=[ev_sa, ev_updM], label=f"Sc{j}")
            k.rel("S_ready", ev_sc)
            ev_ma = p.op("vector", lambda e: e.tensor_add(
                M32[:], M32[:], pM.h[0:64, :]),
                deps=[ev_updM, k.use("S32i"), ev_Mprev], label=f"Ma{j}")
            ev_mc = p.op("vector", lambda e: e.tensor_copy(M_sb[:], M32[:]),
                         deps=[ev_ma], label=f"Mc{j}")
            k.rel(pM.name, ev_ma)
            k.rel("M_ready", ev_mc)

        ev_scan = p.mark("vector")
        ohm_e = o_hm.ap().rearrange("p (hp pb t) -> p hp pb t", pb=2, t=512)
        ev_oe1 = p.op("vector", lambda e: e.tensor_copy(
            oTall.ap()[0:64, :].rearrange("p (hp t) -> p hp t", t=512),
            ohm_e[:, :, 0, :]),
            deps=[ev_scan, k.use("oTall")], label=f"oTe{tt}")
        ev_oe2 = p.op("gpsimd", lambda e: e.dma_start(
            out=oTall.ap()[64:128, :].rearrange("p (hp t) -> p hp t", t=512),
            in_=ohm_e[:, :, 1, :]),
            deps=[ev_scan, k.use("oTall")], dma="oTq", label=f"oTo{tt}")
        ev_oT = [ev_oe1, ev_oe2]
        k.rel("o_hm", ev_oT)

        def out_epi(i, oc, ps, ev):
            ev_a = p.op("scalar", lambda e, ps=ps: e.activation(
                ytmp[:], ps, AF.Identity),
                deps=[ev, k.use("ytmp")], label=f"yact{oc}")
            ev_add = p.op("vector", lambda e, oc=oc: e.tensor_add(
                x_tt[oc][:], x_tt[oc][:], ytmp[:]), deps=[ev_a],
                label=f"yadd{oc}")
            k.rel("ytmp", ev_add)
            p.op("gpsimd", lambda e, oc=oc, tsl=tsl: e.dma_start(
                out=y_h[oc, :, tsl], in_=x_tt[oc][:]), deps=[ev_add],
                dma="st", label=f"sty{oc}_{tt}")
            return ev_a

        seq_out = emit_proj(k, ws_out, seq_out, list(range(8)),
                            [oTall[:, 512 * hp:512 * hp + 512]
                             for hp in range(8)], [ev_oT] * 8,
                            [pt0, pt1], out_epi, f"out{tt}")
        k.rel("oTall", p.mark("tensor"))
        ev_store_round = Ev("st", p.cnt["st"])
        for c in range(NCH):
            k.rel(f"x{c}", ev_store_round)
            k.rel(f"xn{c}", p.mark("tensor"))
        for pi in range(3):
            k.rel(f"hmodd{pi}", p.mark("tensor"))
            for h0 in range(0, 16, 2):
                k.rel(f"hm{pi}_{h0}", p.mark("tensor"))

    if standalone:
        p.finalize(k.nc, [Ev("st", p.cnt["st"])])
        return k.nc



def emit_patt_tt(k, tag, src_tiles, src_ev, wsK, seqK, wsV, seqV, NCn,
                 E_bufs, b_sc, b_out, b_den, b_bc, drow, rb_sb, dst_fn,
                 ekey="E"):
    p = k.p
    onesb, ones1 = k.sb["c_onesb"], k.sb["c_ones1"]
    ev_den = None
    ev_exp = {}
    for ncc in range(NCn):
        colK, evK = wsK.get(seqK)
        bank = b_sc[ncc % 2]
        ev = None
        for kc in range(8):
            lhst = wsK.tile(colK, kc)
            rhst = src_tiles[kc]
            ev = p.op("tensor", lambda e, lhst=lhst, rhst=rhst, bank=bank,
                      kc=kc: e.matmul(bank.h[:, 0:512], lhst, rhst,
                                      start=(kc == 0), stop=(kc == 7)),
                      deps=[evK, src_ev] +
                      ([k.use(bank.name)] if kc == 0 else []),
                      label=f"{tag}_sc{ncc}")
        wsK.release(seqK, ev)
        seqK += 1
        ev_e = p.op("scalar", lambda e, ncc=ncc, bank=bank: e.activation(
            E_bufs[ncc][:], bank.h[:, 0:512], AF.Exp, scale=1.0 / 32.0),
            deps=[ev, k.use(f"{ekey}{ncc}")], label=f"{tag}_ex{ncc}")
        k.rel(bank.name, ev_e)
        ev_exp[ncc] = ev_e
        ev_den = p.op("tensor", lambda e, ncc=ncc: e.matmul(
            b_den.h[0:1, 0:512], onesb[:], E_bufs[ncc][:],
            start=(ncc == 0), stop=(ncc == NCn - 1)),
            deps=[ev_e] + ([k.use(b_den.name)] if ncc == 0 else []),
            label=f"{tag}_dn{ncc}")
    ev_d0 = p.op("vector", lambda e: e.tensor_copy(
        drow[0:1, 0:512], b_den.h[0:1, 0:512]),
        deps=[ev_den, k.use("drow")], label=f"{tag}_de")
    k.rel(b_den.name, ev_d0)
    ev_l = p.op("scalar", lambda e: e.activation(
        drow[0:1, 512:1024], drow[0:1, 0:512], AF.Ln), deps=[ev_d0],
        label=f"{tag}_ln")
    ev_r = p.op("scalar", lambda e: e.activation(
        drow[0:1, 1024:1536], drow[0:1, 512:1024], AF.Exp, scale=-1.0),
        deps=[ev_l], label=f"{tag}_rc")
    ev_bc = p.op("tensor", lambda e: e.matmul(
        b_bc.h[:, 0:512], ones1[:], drow[0:1, 1024:1536],
        start=True, stop=True), deps=[ev_r, k.use(b_bc.name)],
        label=f"{tag}_bc")
    ev_rb = p.op("vector", lambda e: e.tensor_copy(rb_sb[:], b_bc.h[:, 0:512]),
                 deps=[ev_bc, k.use("rb_sb")], label=f"{tag}_rbe")
    k.rel(b_bc.name, ev_rb)
    k.rel("drow", ev_rb)
    ev_eall = [ev_exp[ncc] for ncc in range(NCn)]
    for oc in range(8):
        colV, evV = wsV.get(seqV)
        bank = b_out[oc % 2]
        ev = None
        for ncc in range(NCn):
            lhst = wsV.tile(colV, ncc)
            rhst = E_bufs[ncc][:]
            ev = p.op("tensor", lambda e, lhst=lhst, rhst=rhst, bank=bank,
                      ncc=ncc: e.matmul(bank.h[:, 0:512], lhst, rhst,
                                        start=(ncc == 0),
                                        stop=(ncc == NCn - 1)),
                      deps=[evV] + (ev_eall + [k.use(bank.name)]
                                    if ncc == 0 else []),
                      label=f"{tag}_o{oc}")
        wsV.release(seqV, ev)
        seqV += 1
        ev_done = dst_fn(oc, bank.h[:, 0:512], rb_sb, [ev, ev_rb])
        k.rel(bank.name, ev_done)
    ev_t = Ev("tensor", p.cnt["tensor"])
    for ncc in range(NCn):
        k.rel(f"{ekey}{ncc}", ev_t)
    k.rel("rb_sb", ev_t)
    return seqK, seqV



def build_l3a(k=None, x_h=None, xin_h=None, qs_h=None, ks_h=None,
              vs_h=None):
    standalone = k is None
    if standalone:
        k = K()
    p = k.p
    if x_h is None:
        x_h = k.dram_in("x", [D, T]).rearrange("(c p) t -> c p t", p=128)
        xin_h = k.dram_out("xin", [D, T], BF16).rearrange(
            "(c p) t -> c p t", p=128)
        qs_h = k.dram_out("q_scr", [16, 64, T], BF16)
        ks_h = k.dram_out("k_scr", [16, 64, T], BF16)
        vs_h = k.dram_out("v_scr", [16, 64, T], BF16)
    wK_h = k.dram_in("l3_K3", [24, 128, 1024])
    wV_h = k.dram_in("l3_V3", [24, 128, 1024])
    k.load_consts(("c_ones1", "c_onesb"))

    x_tt = [k.sbuf(f"x{c}", [128, 512], F32) for c in range(NCH)]
    xin_tt = [k.sbuf(f"xi{c}", [128, 512], BF16) for c in range(NCH)]
    h_sb = [k.sbuf(f"h{c}", [128, T], BF16) for c in range(NCH)]
    E_bufs = [k.sbuf(f"E{i}", [128, 512], BF16) for i in range(8)]
    drow = k.sbuf("drow", [1, 1536], F32)
    rb_sb = k.sbuf("rb_sb", [128, 512], F32)
    pair = k.sbuf("pair", [128, 512], BF16)

    pt0, pt1, psp = k.bk("A0"), k.bk("A1"), k.bk("A2")
    po0, po1 = k.bk("A2"), k.bk("A3")
    pden = k.bk("PD")
    pbc = k.bk("A0")

    for tt in range(TT):
        tsl = slice(tt * 512, (tt + 1) * 512)
        x_ev = []
        for c in range(NCH):
            x_ev.append(p.op("sync", lambda e, c=c, tsl=tsl: e.dma_start(
                out=x_tt[c][:], in_=x_h[c, :, tsl]),
                deps=[k.use(f"x{c}")], dma="xld", label=f"ldx{c}_{tt}"))
        x_ev = [x_ev[-1]] * NCH
        xin_ev = emit_ln(k, [t[:] for t in x_tt], x_ev,
                         [t[:] for t in xin_tt], pt0, pt1, psp, f"lno{tt}",
                         wkey="xi")
        for c in range(NCH):
            ev_st = p.op("gpsimd", lambda e, c=c, tsl=tsl: e.dma_start(
                out=xin_h[c, :, tsl], in_=xin_tt[c][:]), deps=[xin_ev[c]],
                dma="st", label=f"sxin{c}_{tt}")
            k.rel(f"x{c}", xin_ev[c])
        h_ev = emit_ln(k, [t[:] for t in xin_tt], xin_ev,
                       [t[:, tsl] for t in h_sb], pt0, pt1, psp, f"lni{tt}",
                       wkey="hsb")
        ev_all = Ev("st", p.cnt["st"])
        for c in range(NCH):
            k.rel(f"xi{c}", [h_ev[c], ev_all])
    ev_h = p.mark("vector")

    wsK = WCols(k, "wK", wK_h, [pi * 8 + nc for pi in range(3)
                                for _ in range(TT) for nc in range(8)],
                1024, "sync")
    wsV = WCols(k, "wV", wV_h, [pi * 8 + oc for pi in range(3)
                                for _ in range(TT) for oc in range(8)],
                1024, "gpsimd")
    seqK = seqV = 0
    for pi, scr in enumerate((qs_h, ks_h, vs_h)):
        for tt in range(TT):
            tsl = slice(tt * 512, (tt + 1) * 512)

            def dst_fn(oc, ps, rb, evs, scr=scr, tsl=tsl):
                ev_m = p.op("vector", lambda e, ps=ps: e.tensor_mul(
                    pair[:], ps, rb[:]), deps=evs + [k.use("pair")],
                    label="pdiv")
                for half in range(2):
                    p.op("gpsimd", lambda e, oc=oc, half=half, scr=scr,
                         tsl=tsl: e.dma_start(
                             out=scr[2 * oc + half, :, tsl],
                             in_=pair[64 * half:64 * half + 64, :]),
                         deps=[ev_m], dma="st", label="pst")
                ev_s = Ev("st", p.cnt["st"])
                k.rel("pair", ev_s)
                return ev_s

            seqK, seqV = emit_patt_tt(
                k, f"p{pi}_{tt}", [t[:, tsl] for t in h_sb], ev_h,
                wsK, seqK, wsV, seqV, 8, E_bufs, (pt0, pt1), (po0, po1),
                pden, pbc, drow, rb_sb, dst_fn)

    if standalone:
        p.finalize(k.nc, [Ev("st", p.cnt["st"])])
        return k.nc



def build_l3b(k=None, qs_h=None, ks_h=None, vs_h=None, os_h=None):
    standalone = k is None
    if standalone:
        k = K()
    p = k.p
    if qs_h is None:
        qs_h = k.dram_in("q_scr", [16, 64, T], BF16)
        ks_h = k.dram_in("k_scr", [16, 64, T], BF16)
        vs_h = k.dram_in("v_scr", [16, 64, T], BF16)
        os_h = k.dram_out("o_scr", [8, 128, T], BF16)
    k.load_consts(("c_ones1", "c_onesb", "c_identb", "c_sdpmask", "c_sel8"))

    q_hm = k.sbuf("q_hm", [64, 4 * T], BF16)
    k_hm = k.sbuf("k_hm", [64, 4 * T], BF16)
    v_hm = k.sbuf("v_hm", [64, 4 * T], BF16)
    V_tok = [k.sbuf(f"V{tc}", [128, 4 * 65], BF16) for tc in range(16)]
    o_hm = k.sbuf("o_hm", [64, 4 * T], BF16)
    oTall = k.sbuf("oTall", [128, 2 * T], BF16)
    E_sb = [k.sbuf(f"E{i}", [128, 512], BF16) for i in range(4)]
    sel8 = k.sb["c_sel8"]
    dtmp = k.sbuf("dtmp", [65, 512], F32)
    rd16 = k.sbuf("rd16", [16, T], F32)

    pt0, pt1, pt2 = k.bk("A0"), k.bk("A1"), k.bk("A3")
    pav = k.bk("W0")
    pkv = k.bk("P3")
    pbc = k.bk("A2")

    ev_c2 = k.use("consts")
    ident = k.sb["c_identb"]
    sdpm = k.sb["c_sdpmask"]

    for hg in range(4):
        hev = {}
        for nm, hm, scr in (("q", q_hm, qs_h), ("k", k_hm, ks_h),
                            ("v", v_hm, vs_h)):
            dst = hm.ap().rearrange("p (h t) -> p h t", t=T)
            src = scr[4 * hg:4 * hg + 4].rearrange("h p t -> p h t")
            hev[nm] = p.op("sync", lambda e, dst=dst, src=src: e.dma_start(
                out=dst, in_=src), deps=[k.use(f"hm_{nm}")],
                dma=f"hq{nm}", label=f"ld{nm}{hg}")
        ev_tr_all = []
        for tc in range(16):
            ev_vs = p.op("vector", lambda e, tc=tc: e.memset(
                V_tok[tc].ap().rearrange("p (h s) -> p h s", s=65)[:, :, 64:65],
                1.0), deps=[k.use(f"Vt{tc}")], label=f"vone{tc}_{hg}")
            ev_tr = None
            for hi in range(4):
                vin = v_hm[0:64, hi * T + tc * 128: hi * T + tc * 128 + 128]
                ev_tr = p.op("tensor", lambda e, vin=vin, hi=hi: e.transpose(
                    pkv.h[:, 64 * hi:64 * hi + 64], vin, ident[0:64, 0:64]),
                    deps=[hev["v"], k.use(pkv.name), k.use("consts")],
                    label=f"vtr{tc}_{hi}")
            dst = V_tok[tc].ap().rearrange("p (h s) -> p h s", s=65)[:, :, 0:64]
            src = pkv.h.ap()[:, 0:256].rearrange("p (h s) -> p h s", s=64)
            ev_ve = p.op("vector", lambda e, dst=dst, src=src: e.tensor_copy(
                dst, src), deps=[ev_tr, ev_vs], label=f"vev{tc}_{hg}")
            k.rel(pkv.name, ev_ve)
            ev_tr_all.append(ev_ve)
        ev_V = p.mark("vector")

        for hi in range(4):
            h = 4 * hg + hi
            for qt in range(TT):
                qsl = slice(qt * 512, qt * 512 + 512)
                nkc = 4 * qt + 4
                ev_av = None
                LAG = 2
                pend = []

                def emit_av(kq, evq):
                    lhsv = V_tok[kq][:, 65 * hi:65 * hi + 65]
                    ev = p.op("tensor", lambda e, lhsv=lhsv, eb=E_sb[kq % 4],
                              kq=kq, nkc=nkc: e.matmul(
                                  pav.h[:, 0:512], lhsv, eb[:],
                                  start=(kq == 0), stop=(kq == nkc - 1)),
                              deps=[evq, ev_V] +
                              ([k.use(pav.name)] if kq == 0 else []),
                              label=f"AV{h}_{qt}_{kq}")
                    k.rel(f"E{kq % 4}", ev)
                    return ev

                for kc in range(nkc):
                    bank = (pt0, pt1, pt2)[kc % 3]
                    lhsp = k_hm[0:64, hi * T + kc * 128: hi * T + kc * 128 + 128]
                    rhsp = q_hm[0:64, hi * T:hi * T + T][:, qsl]
                    ev_pt = p.op("tensor", lambda e, lhsp=lhsp, rhsp=rhsp,
                                 bank=bank: e.matmul(
                                     bank.h[:, 0:512], lhsp, rhsp,
                                     start=True, stop=True),
                                 deps=[hev["k"], hev["q"], k.use(bank.name)],
                                 label=f"PT{h}_{qt}_{kc}")
                    eb = E_sb[kc % 4]
                    ev_exp = p.op("scalar", lambda e, eb=eb, bank=bank:
                                  e.activation(eb[:], bank.h[:, 0:512],
                                               AF.Exp, scale=0.125),
                                  deps=[ev_pt, k.use(f"E{kc % 4}")],
                                  label=f"ex{h}_{qt}_{kc}")
                    k.rel(bank.name, ev_exp)
                    if kc >= 4 * qt:
                        r_ = kc - 4 * qt
                        ev_exp = p.op("vector", lambda e, eb=eb, r_=r_:
                                      e.tensor_mul(eb[:], eb[:],
                                                   sdpm[:, 512 * r_:
                                                        512 * r_ + 512]),
                                      deps=[ev_exp, k.use("consts")],
                                      label=f"msk{h}_{qt}_{kc}")
                    pend.append((kc, ev_exp))
                    if len(pend) > LAG:
                        ev_av = emit_av(*pend.pop(0))
                for kq, evq in pend:
                    ev_av = emit_av(kq, evq)
                ev_oe = p.op("vector", lambda e, hi=hi, qsl=qsl: e.tensor_copy(
                    o_hm[0:64, hi * T:hi * T + T][:, qsl], pav.h[0:64, 0:512]),
                    deps=[ev_av, k.use("o_hm")], label=f"oev{h}_{qt}")
                ev_de = p.op("vector", lambda e: e.tensor_copy(
                    dtmp[64:65, 0:512], pav.h[64:65, 0:512]),
                    deps=[ev_av, k.use("dtmp")], label=f"dev{h}_{qt}")
                k.rel(pav.name, [ev_oe, ev_de])
                ev_dd = p.op("gpsimd", lambda e, hi=hi, qsl=qsl: e.dma_start(
                    out=rd16[hi:hi + 1, qsl], in_=dtmp[64:65, 0:512]),
                    deps=[ev_de, k.use("rd16")], dma="dq",
                    label=f"dmv{h}_{qt}")
                k.rel("dtmp", ev_dd)
                k.rel("rd_part", ev_dd)
        ev_rd = p.op("vector", lambda e: e.reciprocal(
            rd16[0:4, :], rd16[0:4, :]), deps=[k.use("rd_part")],
            label=f"rcp{hg}")

        ev_scan = p.mark("vector")
        ohm_e = o_hm.ap().rearrange("p (hp pb t) -> p hp pb t", pb=2, t=T)
        ev_oe1 = p.op("vector", lambda e: e.tensor_copy(
            oTall.ap()[0:64, :].rearrange("p (hp t) -> p hp t", t=T),
            ohm_e[:, :, 0, :]),
            deps=[ev_scan, k.use("oTall")], label=f"oTe{hg}")
        ev_oe2 = p.op("gpsimd", lambda e: e.dma_start(
            out=oTall.ap()[64:128, :].rearrange("p (hp t) -> p hp t", t=T),
            in_=ohm_e[:, :, 1, :]),
            deps=[ev_scan, k.use("oTall")], dma="oTq", label=f"oTo{hg}")
        ev_oT = [ev_oe1, ev_oe2]
        k.rel("o_hm", ev_oT)

        for hpl in range(2):
            c = 2 * hg + hpl
            last = None
            for qt in range(TT):
                qsl = slice(qt * 512, qt * 512 + 512)
                ev_bc = p.op("tensor", lambda e, hpl=hpl, qsl=qsl: e.matmul(
                    pbc.h[:, 0:512], sel8[0:4, 128 * hpl:128 * hpl + 128],
                    rd16[0:4, qsl], start=True, stop=True),
                    deps=[ev_rd, ev_c2, k.use(pbc.name)],
                    label=f"bc{hg}_{hpl}_{qt}")
                ev_t = p.op("vector", lambda e, hpl=hpl, qsl=qsl: e.tensor_mul(
                    oTall[:, hpl * T:hpl * T + T][:, qsl],
                    oTall[:, hpl * T:hpl * T + T][:, qsl],
                    pbc.h[:, 0:512]),
                    deps=[ev_bc, ev_oT], label=f"dv{hg}_{hpl}_{qt}")
                k.rel(pbc.name, ev_t)
                last = ev_t
            p.op("gpsimd", lambda e, c=c, hpl=hpl: e.dma_start(
                out=os_h[c], in_=oTall[:, hpl * T:hpl * T + T]),
                deps=[last], dma="st", label=f"sto{c}")
        ev_done = Ev("st", p.cnt["st"])
        k.rel("oTall", ev_done)
        k.rel("rd16", p.mark("tensor"))
        k.rel("rd_part", None)
        for nm in ("q", "k", "v"):
            k.rel(f"hm_{nm}", p.mark("tensor"))
        for tc in range(16):
            k.rel(f"Vt{tc}", p.mark("tensor"))

    if standalone:
        p.finalize(k.nc, [Ev("st", p.cnt["st"])])
        return k.nc



def build_l3c(k=None, x_h=None, xin_h=None, os_h=None, y_h=None):
    standalone = k is None
    if standalone:
        k = K()
    p = k.p
    if x_h is None:
        x_h = k.dram_in("x", [D, T]).rearrange("(c p) t -> c p t", p=128)
        xin_h = k.dram_in("xin", [D, T], BF16).rearrange(
            "(c p) t -> c p t", p=128)
        os_h = k.dram_in("o_scr", [8, 128, T], BF16)
        y_h = k.dram_out("y", [D, T]).rearrange("(c p) t -> c p t", p=128)
    wpK_h = k.dram_in("l3_pK", [8, 128, 1024])
    wpV_h = k.dram_in("l3_pV", [8, 128, 1024])
    wfK_h = k.dram_in("l3_fK", [32, 128, 1024])
    wfV_h = k.dram_in("l3_fV", [8, 128, 4096])
    k.load_consts(("c_ones1", "c_onesb"))

    o_tt = [k.sbuf(f"o{c}", [128, 512], BF16) for c in range(NCH)]
    xi_tt = [k.sbuf(f"xi{c}", [128, 512], BF16) for c in range(NCH)]
    s_tt = [k.sbuf(f"s{c}", [128, 512], F32) for c in range(NCH)]
    h2_tt = [k.sbuf(f"h2{c}", [128, 512], BF16) for c in range(NCH)]
    x_tile = k.sbuf("x_tile", [128, 512], F32)
    tmp_f = k.sbuf("tmp_f", [128, 512], F32)
    E_p = [k.sbuf(f"Ep{i}", [128, 512], BF16) for i in range(8)]
    E_f = [k.sbuf(f"Ef{i}", [128, 512], BF16) for i in range(32)]
    drow = k.sbuf("drow", [1, 1536], F32)
    rb_sb = k.sbuf("rb_sb", [128, 512], F32)

    pt0, pt1, psp = k.bk("A0"), k.bk("A1"), k.bk("A2")
    po0, po1 = k.bk("A2"), k.bk("A3")
    pden = k.bk("PD")
    pbc = k.bk("A0")

    wspK = WCols(k, "wpK", wpK_h, [nc for _ in range(TT) for nc in range(8)],
                 1024, "sync")
    wspV = WCols(k, "wpV", wpV_h, [oc for _ in range(TT) for oc in range(8)],
                 1024, "gpsimd")
    wsfK = WCols(k, "wfK", wfK_h, [nc for _ in range(TT) for nc in range(32)],
                 1024, "sync")
    wsfV = WCols(k, "wfV", wfV_h, [oc for _ in range(TT) for oc in range(8)],
                 4096, "gpsimd", stg_bufs=1, col_bufs=1)
    spK = spV = sfK = sfV = 0

    for tt in range(TT):
        tsl = slice(tt * 512, (tt + 1) * 512)
        o_ev = []
        xi_ev = []
        for c in range(NCH):
            o_ev.append(p.op("sync", lambda e, c=c, tsl=tsl: e.dma_start(
                out=o_tt[c][:], in_=os_h[c, :, tsl]),
                deps=[k.use(f"o{c}")], dma="old", label=f"ldo{c}_{tt}"))
            xi_ev.append(p.op("sync", lambda e, c=c, tsl=tsl: e.dma_start(
                out=xi_tt[c][:], in_=xin_h[c, :, tsl]),
                deps=[k.use(f"xi{c}")], dma="old", label=f"ldxi{c}_{tt}"))
        ev_ld = Ev("old", p.cnt["old"])

        def dst_p(oc, ps, rb, evs):
            ev_m = p.op("vector", lambda e, ps=ps: e.tensor_mul(
                tmp_f[:], ps, rb[:]), deps=evs + [k.use("tmp_f")],
                label="pm")
            ev_s = p.op("vector", lambda e, oc=oc: e.tensor_add(
                s_tt[oc][:], tmp_f[:], xi_tt[oc][:]),
                deps=[ev_m, ev_ld, k.use(f"s{oc}")], label="ps")
            k.rel("tmp_f", ev_s)
            k.rel(f"o{oc}", ev_s)
            k.rel(f"s{oc}", ev_s)
            return ev_s

        spK, spV = emit_patt_tt(
            k, f"pp{tt}", [t[:] for t in o_tt], ev_ld, wspK, spK, wspV, spV,
            8, E_p, (pt0, pt1), (po0, po1), pden, pbc, drow, rb_sb, dst_p,
            ekey="Ep")
        s_ev = [k.use(f"s{c}") for c in range(NCH)]

        h2_ev = emit_ln(k, [t[:] for t in s_tt], s_ev,
                        [t[:] for t in h2_tt], pt0, pt1, psp, f"ln2{tt}",
                        wkey="h2w")
        for c in range(NCH):
            k.rel(f"h2w{c}", h2_ev[c])

        def dst_f(oc, ps, rb, evs):
            ev_m = p.op("vector", lambda e, ps=ps: e.tensor_mul(
                tmp_f[:], ps, rb[:]), deps=evs + [k.use("tmp_f")],
                label="fm")
            ev_x = p.op("sync", lambda e, oc=oc, tsl=tsl: e.dma_start(
                out=x_tile[:], in_=x_h[oc, :, tsl]),
                deps=[k.use("x_tile")], dma="xr", label=f"ldx{oc}_{tt}")
            ev_a1 = p.op("vector", lambda e, oc=oc: e.tensor_add(
                x_tile[:], x_tile[:], s_tt[oc][:]),
                deps=[ev_x, k.use(f"s{oc}")], label="fa1")
            ev_a2 = p.op("vector", lambda e: e.tensor_add(
                x_tile[:], x_tile[:], tmp_f[:]), deps=[ev_a1, ev_m],
                label="fa2")
            k.rel("tmp_f", ev_a2)
            k.rel(f"s{oc}", ev_a2)
            p.op("gpsimd", lambda e, oc=oc, tsl=tsl: e.dma_start(
                out=y_h[oc, :, tsl], in_=x_tile[:]), deps=[ev_a2],
                dma="st", label=f"sty{oc}_{tt}")
            k.rel("x_tile", Ev("st", p.cnt["st"]))
            return ev_a2

        sfK, sfV = emit_patt_tt(
            k, f"pf{tt}", [t[:] for t in h2_tt],
            [h2_ev, k.use("h2all")], wsfK, sfK, wsfV, sfV,
            32, E_f, (pt0, pt1), (po0, po1), pden, pbc, drow, rb_sb, dst_f,
            ekey="Ef")
        ev_t = Ev("tensor", p.cnt["tensor"])
        for c in range(NCH):
            k.rel(f"xi{c}", ev_t)
            k.rel(f"h2w{c}", ev_t)
        k.rel("h2all", ev_t)

    if standalone:
        p.finalize(k.nc, [Ev("st", p.cnt["st"])])
        return k.nc



ALL_CONSTS = ("c_ones1", "c_onesb", "c_identb", "c_identf", "c_triu",
              "c_tril", "c_sdpmask", "c_palace", "c_sel8", "c_sel16")


def build_fused(upto=None):
    k = K(fused=True)
    p = k.p
    k.load_consts(ALL_CONSTS)
    x_ext = k.dram_in("x", [D, T])
    y_ext = k.dram_out("y", [D, T])
    xi = k.dram_tmp("x_int", [D, T])
    xin_s = k.dram_tmp("xin_s", [D, T], BF16)
    q_s = k.dram_tmp("q_s", [16, 64, T], BF16)
    k_s = k.dram_tmp("k_s", [16, 64, T], BF16)
    v_s = k.dram_tmp("v_s", [16, 64, T], BF16)
    o_s = k.dram_tmp("o_s", [8, 128, T], BF16)
    r = lambda ap: ap.rearrange("(c p) t -> c p t", p=128)

    seq = [
        ("L0A_", build_l0_attn, dict(x_h=r(x_ext), y_h=r(xi))),
        ("L0F_", build_l0_ffn, dict(x_h=r(xi), y_h=r(xi))),
        ("L1A_", build_l1_attn, dict(x_h=r(xi), y_h=r(xi))),
        ("L1F_", build_l1_ffn, dict(x_h=r(xi), y_h=r(xi))),
        ("L2A_", build_l2_attn, dict(x_h=r(xi), y_h=r(xi))),
        ("L2F_", build_l2_ffn, dict(x_h=r(xi), y_h=r(xi))),
        ("L3A_", build_l3a, dict(x_h=r(xi), xin_h=r(xin_s), qs_h=q_s,
                                 ks_h=k_s, vs_h=v_s)),
        ("L3B_", build_l3b, dict(qs_h=q_s, ks_h=k_s, vs_h=v_s, os_h=o_s)),
        ("L3C_", build_l3c, dict(x_h=r(xi), xin_h=r(xin_s), os_h=o_s,
                                 y_h=r(xi))),
        ("L3F_", build_l3_ffn, dict(x_h=r(xi), y_h=r(y_ext))),
    ]
    if upto is not None:
        seq = seq[:upto]
    for pfx, fn, io in seq:
        k.new_block(pfx)
        fn(k=k, **io)
    if upto is not None and seq[-1][2].get("y_h") is not r(y_ext):
        k.barrier()
        src = xi if seq[-1][0] != "L3A_" else xi
        p.op("gpsimd", lambda e: e.dma_start(out=y_ext, in_=src),
             dma="st", label="dbgcopy")
    p.finalize(k.nc, [Ev("st", p.cnt["st"])])
    return k.nc



def _host_prep(inputs):
    pr = {}
    f = np.float32
    pr["c_ones1"] = np.ones((1, 128), f)
    pr["c_onesb"] = np.ones((128, 1), f)
    pr["c_identb"] = np.eye(128, dtype=f)
    i_ = np.arange(128)[:, None]
    j_ = np.arange(128)[None, :]
    pr["c_tril"] = (j_ <= i_).astype(f)
    pr["c_triu"] = (j_ >= i_).astype(f)
    jj = np.arange(512)[None, :]
    sdp = np.zeros((128, 4, 512), f)
    for r_ in range(4):
        sdp[:, r_, :] = (jj >= (i_ + 128 * r_)).astype(f)
    pr["c_sdpmask"] = sdp.reshape(128, 2048)
    jj64 = np.arange(64)[None, :]
    pr["c_palace"] = (((i_ % 64) // 8) == (jj64 // 8)).astype(f)

    g = lambda n: np.asarray(inputs[n], f)
    pr["l1_wqkv"] = colmajor(g("L1_attn_Wqkv_w").T)
    pr["l1_bqkv"] = pp_bias(g("L1_attn_Wqkv_b"), 24)
    pr["l1_wout"] = colmajor(g("L1_attn_out_w").T)
    pr["l1_bout"] = pp_bias(g("L1_attn_out_b"), 8)

    idx = np.arange(D, dtype=np.int64)
    m_ = np.outer(idx, idx) % D
    ang = (2.0 * np.pi / D) * m_
    FR = np.cos(ang).astype(f)
    FI = (-np.sin(ang)).astype(f)
    pr["c_fr"] = rowmajor(FR)
    pr["c_fi"] = rowmajor(FI)
    pr["c_fin"] = rowmajor(-FI)
    pr["l1f_wr"] = colmajor(g("L1_ffn_w_freq_real"))
    pr["l1f_wi"] = colmajor(g("L1_ffn_w_freq_imag"))
    pr["l1f_bias"] = pp_bias(g("L1_ffn_bias"), 8)

    pr["l2_wqk"] = np.concatenate(
        [colmajor(g("L2_attn_Wq").T), colmajor(g("L2_attn_Wk").T)], axis=0)
    pr["l2_wv"] = rowmajor(g("L2_attn_Wv").T)
    pr["l2_ipw"] = np.asarray(inputs["L2_attn_inter_palace_weight"],
                              f).reshape(1, 1)
    pr["c_identf"] = np.eye(128, dtype=f)
    sel16 = np.zeros((16, 2048), ml_dtypes.bfloat16)
    for e in range(16):
        sel16[e, 128 * e:128 * e + 128] = 1.0
    pr["c_sel16"] = sel16
    sel8 = np.zeros((8, 512), f)
    for hpl in range(4):
        for m in range(128):
            sel8[2 * hpl + m // 64, 128 * hpl + m] = 1.0
    pr["c_sel8"] = sel8

    def moe_prep(pfx, wr, w1, b1, w2, b2, E, F):
        OC1 = F // 128
        pr[f"{pfx}_wr"] = np.ascontiguousarray(
            rowmajor(wr).transpose(1, 0, 2).reshape(128, 8 * E))
        pr[f"{pfx}_w1"] = np.concatenate(
            [colmajor(w1[e]) for e in range(E)], axis=0)
        pr[f"{pfx}_b1"] = np.concatenate(
            [pp_bias(b1[e], OC1) for e in range(E)], axis=1)
        NPARTS = F // 1024
        w2c = np.stack([colmajor(w2[e]) for e in range(E)])
        pr[f"{pfx}_w2"] = np.ascontiguousarray(
            w2c.reshape(E, 8, 128, NPARTS, 1024).transpose(0, 3, 1, 2, 4)
            .reshape(E * NPARTS * 8, 128, 1024))
        pr[f"{pfx}_b2"] = np.concatenate(
            [pp_bias(b2[e], 8) for e in range(E)], axis=1)

    moe_prep("l0f", g("L0_ffn_Wg").T,
             np.transpose(g("L0_ffn_W1"), (0, 2, 1)), g("L0_ffn_b1"),
             np.transpose(g("L0_ffn_W2"), (0, 2, 1)), g("L0_ffn_b2"),
             8, 4096)
    moe_prep("l2f", g("L2_ffn_Wr").T, g("L2_ffn_W1"), g("L2_ffn_b1"),
             g("L2_ffn_W2"), g("L2_ffn_b2"), 8, 4096)
    moe_prep("l3f", g("L3_ffn_router_w"), g("L3_ffn_w1"), g("L3_ffn_b1"),
             g("L3_ffn_w2"), g("L3_ffn_b2"), 14, 2048)

    pr["l0_wqkv"] = np.concatenate([colmajor(g("L0_attn_W_q")),
                                    colmajor(g("L0_attn_W_k")),
                                    colmajor(g("L0_attn_W_v"))], axis=0)
    pr["l0_wo"] = colmajor(g("L0_attn_W_o"))

    pr["l3_K3"] = np.concatenate([colmajor(g("L3_attn_qK").T),
                                  colmajor(g("L3_attn_kK").T),
                                  colmajor(g("L3_attn_vK").T)], axis=0)
    pr["l3_V3"] = np.concatenate([colmajor(g("L3_attn_qV")),
                                  colmajor(g("L3_attn_kV")),
                                  colmajor(g("L3_attn_vV"))], axis=0)
    pr["l3_pK"] = colmajor(g("L3_attn_pK").T)
    pr["l3_pV"] = colmajor(g("L3_attn_pV"))
    pr["l3_fK"] = colmajor(g("L3_attn_fK").T)
    pr["l3_fV"] = colmajor(g("L3_attn_fV"))
    return pr


_BUILD_CACHE = {}


LAUNCHES = {
    "l1_attn": (build_l1_attn,
                ["c_ones1", "c_onesb", "c_identb", "c_triu",
                 "l1_wqkv", "l1_bqkv", "l1_wout", "l1_bout"]),
    "l1_ffn": (build_l1_ffn,
               ["c_ones1", "c_onesb", "c_fr", "c_fi", "c_fin",
                "l1f_wr", "l1f_wi", "l1f_bias"]),
    "l0_ffn": (build_l0_ffn,
               ["c_ones1", "c_onesb", "c_identf", "c_sel16",
                "l0f_wr", "l0f_w1", "l0f_b1", "l0f_w2", "l0f_b2"]),
    "l2_ffn": (build_l2_ffn,
               ["c_ones1", "c_onesb", "c_identf", "c_sel16",
                "l2f_wr", "l2f_w1", "l2f_b1", "l2f_w2", "l2f_b2"]),
    "l3_ffn": (build_l3_ffn,
               ["c_ones1", "c_onesb", "c_identf", "c_sel16",
                "l3f_wr", "l3f_w1", "l3f_b1", "l3f_w2", "l3f_b2"]),
    "l0_attn": (build_l0_attn,
                ["c_ones1", "c_onesb", "c_identb", "c_triu", "c_tril",
                 "l0_wqkv", "l0_wo"]),
    "l3a": (build_l3a, ["c_ones1", "c_onesb", "l3_K3", "l3_V3"],
            [("x", "x")],
            [("xin", "xin"), ("q", "q_scr"), ("k", "k_scr"), ("v", "v_scr")]),
    "l3b": (build_l3b,
            ["c_ones1", "c_onesb", "c_identb", "c_sdpmask", "c_sel8"],
            [("q", "q_scr"), ("k", "k_scr"), ("v", "v_scr")],
            [("o", "o_scr")]),
    "l3c": (build_l3c,
            ["c_ones1", "c_onesb", "l3_pK", "l3_pV", "l3_fK", "l3_fV"],
            [("x", "x"), ("xin", "xin"), ("o", "o_scr")],
            [("x", "y")]),
    "l2_attn": (build_l2_attn,
                ["c_ones1", "c_onesb", "c_palace", "c_sel8",
                 "l2_wqk", "l2_wv", "l2_ipw"]),
}
LAUNCH_ORDER = ["l0_attn", "l0_ffn", "l1_attn", "l1_ffn", "l2_attn", "l2_ffn", "l3a", "l3b", "l3c", "l3_ffn"]


def run_launch(name, xc, pr):
    state = [{"x": xc[b]} for b in range(len(xc))]
    run_launch_state(name, state, pr)
    return [st["x"] for st in state]


def run_launch_state(name, state, pr):
    ent = LAUNCHES[name]
    builder, keys = ent[0], ent[1]
    sin = ent[2] if len(ent) > 2 else [("x", "x")]
    sout = ent[3] if len(ent) > 3 else [("x", "y")]
    if name not in _BUILD_CACHE:
        _BUILD_CACHE[name] = builder()
    nc = _BUILD_CACHE[name]
    ncore = len(state)
    in_maps = []
    for b in range(ncore):
        m = {}
        for skey, hname in sin:
            m[hname] = state[b][skey]
        for key in keys:
            m[key] = pr[key]
        in_maps.append(m)
    res = run_bass_kernel_spmd(nc, in_maps, core_ids=list(range(ncore)))
    for b in range(ncore):
        for skey, hname in sout:
            state[b][skey] = np.asarray(res.results[b][hname])


FUSED = True


def kernel(**inputs):
    pr = _host_prep(inputs)
    x = np.asarray(inputs["L0_attn_x"], np.float32)
    if not FUSED:
        state = [{"x": np.ascontiguousarray(x[b].T)} for b in range(NCORES)]
        for name in LAUNCH_ORDER:
            run_launch_state(name, state, pr)
        return np.stack([st["x"].T for st in state]).astype(np.float32)
    if "fused" not in _BUILD_CACHE:
        _BUILD_CACHE["fused"] = build_fused()
    nc = _BUILD_CACHE["fused"]
    wkeys = set()
    for ent in LAUNCHES.values():
        wkeys.update(ent[1])
    wkeys.update(ALL_CONSTS)
    in_maps = []
    for b in range(NCORES):
        m = {"x": np.ascontiguousarray(x[b].T)}
        for key in wkeys:
            m[key] = pr[key]
        in_maps.append(m)
    res = run_bass_kernel_spmd(nc, in_maps, core_ids=list(range(NCORES)))
    return np.stack([np.asarray(r["y"]).T
                     for r in res.results]).astype(np.float32)
